# version 14
# baseline (speedup 1.0000x reference)
import numpy as np
import ml_dtypes

"""Core emitter + program builder for the mk_macaron_d1 megakernel.

This module is developed in scratch/ and inlined into kernel.py when stable.
"""
import math
import numpy as np
import concourse.bass as bass
import concourse.mybir as mybir

F32 = mybir.dt.float32
BF16 = mybir.dt.bfloat16
I32 = mybir.dt.int32
AF = mybir.ActivationFunctionType
ALU = mybir.AluOpType
AX = mybir.AxisListType

PI = float(np.pi)
INV2PI = float(1.0 / (2.0 * np.pi))
N2PI = float(-2.0 * np.pi)
RSQRT_C = float(0x5F3759DF)


ENGINES = ("pe", "act", "dve", "pool", "sp")


def _ap_region(ap):
    t = ap.tensor
    name = t.name
    dims = [list(d) for d in ap.ap]
    try:
        space = str(ap.space)
    except Exception:
        space = "?"
    try:
        esz = np.dtype(mybir.dt.np(ap.dtype)).itemsize
    except Exception:
        esz = 4
    if "DRAM" in space.upper() or "DRam" in str(type(t)):
        off = ap.offset if isinstance(ap.offset, int) else 0
        ext = 1
        for st, ct in dims:
            ext += abs(st) * (ct - 1)
        return (name, 0, 128, off * esz, (off + ext) * esz)
    off = ap.offset if isinstance(ap.offset, int) else 0
    pitch = abs(dims[0][0]) if dims and dims[0][0] else 1
    p0 = off // pitch if pitch > 0 else 0
    p1 = p0 + ap.partition_size()
    coff = off - p0 * pitch
    ext = 1
    for st, ct in dims[1:]:
        ext += abs(st) * (ct - 1)
    b0, b1 = coff * esz, (coff + ext) * esz
    if "PSUM" in space.upper():
        b0 = (b0 // 2048) * 2048
        b1 = ((b1 + 2047) // 2048) * 2048
        p0, p1 = 0, 128
    return (name, p0, p1, b0, b1)


def _overlap(r1, r2):
    return (r1[0] == r2[0] and r1[1] < r2[2] and r2[1] < r1[2]
            and r1[3] < r2[4] and r2[3] < r1[4])


class Op:
    __slots__ = ("engine", "build", "reads", "writes", "idx", "waits",
                 "carrier", "dma_sem", "tag", "accum")

    def __init__(self, engine, build, reads, writes, tag="", accum=False):
        self.engine = engine
        self.build = build
        self.reads = [_ap_region(a) for a in reads if a is not None]
        self.writes = [_ap_region(a) for a in writes if a is not None]
        self.idx = -1
        self.waits = []
        self.carrier = False
        self.tag = tag
        self.accum = accum


NDMASEM = 12


class Emitter:
    def __init__(self, nc):
        self.nc = nc
        self.ops = []
        self.counts = {e: 0 for e in ENGINES}
        self._writers = {}
        self._readers = {}
        self.dma_sem_names = [f"dq{i}" for i in range(NDMASEM)] + ["dqcc"]
        self.dma_counts = {n: 0 for n in self.dma_sem_names}
        self._dma_rr = 0
        self._dma_rr_sw = 4

    def _record(self, op):
        op.idx = len(self.ops)
        self.ops.append(op)
        eng = op.engine
        pre_dma_wait = None
        if eng.startswith("dma:"):
            q = eng.split(":")[1]
            if q.endswith("!cc"):
                sem = "dqcc"
                if self.dma_counts[sem] > 0:
                    pre_dma_wait = ("dma", sem, self.dma_counts[sem])
                self.dma_counts[sem] += 1
                seq = ("dma", sem, self.dma_counts[sem])
                op.dma_sem = sem
            elif q == "pool":
                sem = self.dma_sem_names[self._dma_rr_sw]
                self._dma_rr_sw = 4 + (self._dma_rr_sw + 1 - 4) % 4
            else:
                sem = self.dma_sem_names[self._dma_rr]
                self._dma_rr = (self._dma_rr + 1) % 4
            if not q.endswith("!cc"):
                if self.dma_counts[sem] > 0:
                    pre_dma_wait = ("dma", sem, self.dma_counts[sem])
                self.dma_counts[sem] += 16
                seq = ("dma", sem, self.dma_counts[sem])
                op.dma_sem = sem
        else:
            self.counts[eng] += 1
            seq = ("eng", eng, self.counts[eng])
            op.dma_sem = None
        waits = set()
        for r in op.reads:
            for (reg, src) in self._writers.get(r[0], []):
                if _overlap(reg, r):
                    waits.add(src)
        for w in op.writes:
            for (reg, src) in self._writers.get(w[0], []):
                if _overlap(reg, w):
                    if op.accum and src[0] == "eng" and src[1] == "pe" \
                            and (eng == "pe"):
                        continue
                    waits.add(src)
            for (reg, src) in self._readers.get(w[0], []):
                if _overlap(reg, w):
                    waits.add(src)
        final = []
        for (kind, key, val) in waits:
            if kind == "eng" and key == "pe" and not eng.startswith("dma:") \
                    and eng == "pe":
                continue
            final.append((kind, key, val))
        if pre_dma_wait is not None:
            final.append(pre_dma_wait)
        op.waits = final
        for w in op.writes:
            lst = self._writers.setdefault(w[0], [])
            lst[:] = [(reg, src) for (reg, src) in lst if not _overlap(reg, w)]
            lst.append((w, seq))
            rl = self._readers.get(w[0])
            if rl:
                rl[:] = [(reg, src) for (reg, src) in rl if not _overlap(reg, w)]
        for r in op.reads:
            self._readers.setdefault(r[0], []).append((r, seq))
        return op

    def op(self, engine, build, reads, writes, tag="", accum=False):
        return self._record(Op(engine, build, reads, writes, tag, accum))

    def barrier(self):
        snap_eng = dict(self.counts)
        snap_dma = dict(self.dma_counts)
        for e in ENGINES:
            waits = [("eng", o, snap_eng[o]) for o in ENGINES
                     if snap_eng[o] > 0]
            waits += [("dma", n, v) for n, v in snap_dma.items() if v > 0]
            op = Op(e, lambda h: h.nop(), [], [], tag="barrier")
            self.counts[e] += 1
            op.dma_sem = None
            op.waits = waits
            op.idx = len(self.ops)
            self.ops.append(op)

    def dma(self, queue, out, in_, tag="", slow=False):
        def build(h):
            if slow:
                return h.dma_start(out=out, in_=in_,
                                   allow_slow_non_contiguous=True)
            return h.dma_start(out=out, in_=in_)
        return self._record(Op("dma:" + queue, build, [in_], [out], tag))

    def emit(self, block, sems):
        eng_wait_targets = {e: set() for e in ENGINES}
        for op in self.ops:
            for (kind, key, val) in op.waits:
                if kind == "eng":
                    eng_wait_targets[key].add(val)
        carrier_val = {}
        for e in ENGINES:
            targets = eng_wait_targets[e]
            cnt = 0
            vals = {}
            for n in range(1, self.counts[e] + 1):
                if n in targets:
                    cnt += 1
                    vals[n] = cnt
                else:
                    vals[n] = None
            carrier_val[e] = vals
        carrier_cum = {}
        for e in ENGINES:
            cum = {}
            c = 0
            for n in range(1, self.counts[e] + 1):
                if carrier_val[e][n] is not None:
                    c += 1
                cum[n] = c
            carrier_cum[e] = cum

        streams = {e: [] for e in ENGINES}
        eng_op_no = {e: 0 for e in ENGINES}
        dma_no = {n: 0 for n in self.dma_sem_names}
        for op in self.ops:
            if op.engine.startswith("dma:"):
                q = op.engine.split(":")[1].replace("!cc", "")
                sem = op.dma_sem
                dma_no[sem] += 1 if sem == "dqcc" else 16
                streams[q].append((op, ("dma", sem, dma_no[sem])))
            else:
                eng_op_no[op.engine] += 1
                streams[op.engine].append((op, ("eng", op.engine,
                                                eng_op_no[op.engine])))

        handles = {
            "pe": self.nc.tensor, "act": self.nc.scalar, "dve": self.nc.vector,
            "pool": self.nc.gpsimd, "sp": self.nc.sync,
        }
        emitter = self

        def run_stream(eng_name, eng):
            floors = {}

            def need_wait(kind, key, val):
                if kind == "eng":
                    v = carrier_cum[key][val]
                    if v == 0:
                        return None
                    sem = sems[key]
                    if floors.get(key, 0) >= v:
                        return None
                    floors[key] = v
                    return (sem, v)
                else:
                    if floors.get(key, 0) >= val:
                        return None
                    floors[key] = val
                    return (sems[key], val)

            for (op, seq) in streams[eng_name]:
                for (kind, key, val) in sorted(op.waits,
                                               key=lambda t: (t[1], t[2])):
                    w = need_wait(kind, key, val)
                    if w is not None:
                        eng.wait_ge(w[0], w[1])
                inst = op.build(eng)
                if seq[0] == "eng":
                    v = carrier_val[eng_name].get(seq[2])
                    if v is not None:
                        inst.then_inc(sems[eng_name], 1)
                else:
                    inst.then_inc(sems[seq[1]], 1 if seq[1] == "dqcc" else 16)
            if eng_name == "sp":
                for n in emitter.dma_sem_names:
                    if emitter.dma_counts[n] > 0:
                        eng.wait_ge(sems[n], emitter.dma_counts[n])

        for name in ("sp", "pe", "act", "dve", "pool"):
            dec = {"sp": block.sync, "pe": block.tensor, "act": block.scalar,
                   "dve": block.vector, "pool": block.gpsimd}[name]

            def make(fn_name, engine_obj):
                def _inner(eng):
                    run_stream(fn_name, eng)
                return _inner
            dec(make(name, handles[name]))

    def mm(self, out, lhsT, rhs, start=True, stop=True, tag=""):
        return self.op("pe",
                       lambda e: e.matmul(out, lhsT, rhs, start=start, stop=stop),
                       [lhsT, rhs], [out], tag, accum=not start)

    def transpose(self, out, in_, ident, tag=""):
        return self.op("pe", lambda e: e.transpose(out, in_, ident),
                       [in_, ident], [out], tag)

    def act(self, out, in_, func, bias=0.0, scale=1.0, accum_out=None, tag=""):
        reads = [in_]
        if isinstance(bias, bass.AP):
            reads.append(bias)
        if isinstance(scale, bass.AP):
            reads.append(scale)
        writes = [out] + ([accum_out] if accum_out is not None else [])
        return self.op("act",
                       lambda e: e.activation(out, in_, func, bias=bias,
                                              scale=scale, accum_out=accum_out),
                       reads, writes, tag)

    def _vec(self, engine, meth, out, ins, scalars, tag, kwargs=None):
        reads = list(ins) + [s for s in scalars if isinstance(s, bass.AP)]
        kw = kwargs or {}

        def build(e):
            return getattr(e, meth)(*([out] + list(ins) + list(scalars)), **kw)
        return self.op(engine, build, reads, [out], tag)

    def tt(self, out, a, b, alu, engine="dve", tag=""):
        return self.op(engine, lambda e: e.tensor_tensor(out, a, b, alu),
                       [a, b], [out], tag)

    def ts(self, out, in_, s1, s2, op0, op1=ALU.bypass, engine="dve", tag="",
           accum_out=None):
        reads = [in_] + [s for s in (s1, s2) if isinstance(s, bass.AP)]
        writes = [out] + ([accum_out] if accum_out is not None else [])
        return self.op(engine,
                       lambda e: e.tensor_scalar(out, in_, s1, s2, op0, op1,
                                                 accum_out=accum_out),
                       reads, writes, tag)

    def stt(self, out, in0, scalar, in1, op0, op1, engine="dve", tag=""):
        reads = [in0, in1] + ([scalar] if isinstance(scalar, bass.AP) else [])
        return self.op(engine,
                       lambda e: e.scalar_tensor_tensor(out, in0, scalar, in1,
                                                        op0, op1),
                       reads, [out], tag)

    def copy(self, out, in_, engine="dve", tag=""):
        return self.op(engine, lambda e: e.tensor_copy(out, in_),
                       [in_], [out], tag)

    def memset(self, ap, val, engine="dve", tag=""):
        return self.op(engine, lambda e: e.memset(ap, val), [], [ap], tag)

    def reduce(self, out, in_, alu, engine="dve", tag=""):
        return self.op(engine,
                       lambda e: e.tensor_reduce(out, in_, AX.X, alu),
                       [in_], [out], tag)

    def scan(self, out, d0, d1, init, op0, op1, tag=""):
        reads = [d0, d1] + ([init] if isinstance(init, bass.AP) else [])
        return self.op("dve",
                       lambda e: e.tensor_tensor_scan(out, d0, d1, init, op0, op1),
                       reads, [out], tag)

    def collective(self, kind, alu, groups, ins, outs, tag=""):
        def build(e):
            return e.collective_compute(kind, alu, replica_groups=groups,
                                        ins=ins, outs=outs)
        op = Op("dma:pool!cc", build, ins, outs, tag)
        return self._record(op)

    def wrap1(self, out, a, g1, g2, engine="dve"):
        self.ts(g1, a, -PI, None, ALU.is_lt, engine=engine)
        self.ts(g2, a, PI, None, ALU.is_gt, engine=engine)
        self.tt(g1, g1, g2, ALU.subtract, engine=engine)
        self.stt(out, g1, 2 * PI, a, ALU.mult, ALU.add, engine=engine)
        return out

    def rsqrt(self, out, v, scr1, scr2=None, iters=3):
        self.copy(scr1, v.bitcast(I32))
        self.ts(scr1, scr1, -0.5, RSQRT_C, ALU.mult, ALU.add)
        self.copy(out.bitcast(I32), scr1)
        for _ in range(iters):
            self.tt(scr1, out, out, ALU.mult)
            self.tt(scr1, scr1, v, ALU.mult)
            self.ts(scr1, scr1, -0.5, 1.5, ALU.mult, ALU.add)
            self.tt(out, out, scr1, ALU.mult)
        return out


"""Program builder: phases of the macaron megakernel. TILE=256 tokens."""
import contextlib

D = 1024
DI = 2048
H = 32
PD = 64
NS = 128
CONVDIM = 2560
DFF = 4096
CK = 128
T = 4096
NCH = 32
USL = 4352
EPS = 1e-5
TILE = 256
NG = TILE // 128

DBG = False


def _v(h, offset, dims):
    ap = h if isinstance(h, bass.AP) else h[:, :]
    return bass.AP(ap.tensor, offset, dims)


def build_program(phases=("P1", "P2")):
    nc = bass.Bass("TRN2", target_bir_lowering=False)
    em = Emitter(nc)

    din = {}
    for name, shape in [
        ("u", [USL, D]), ("w1", [D, DFF]), ("w2", [DFF, D]),
        ("b1", [DFF]), ("b2", [D]),
        ("phases", [DFF]), ("wreal", [DFF]), ("wimag", [DFF]),
        ("ipwT", [D, 4672]), ("opwT", [DI, D]), ("fcdT", [DI, H]),
        ("convw", [CONVDIM, 4]), ("convb", [CONVDIM]),
        ("dtbias2", [2 * H]), ("alog", [H]), ("dskip", [H]),
        ("normw", [DI]),
        ("identb", [128, 128]), ("identf", [128, 128]),
        ("triu", [128, 128]), ("tril", [128, 128]),
        ("maskF", [128, 128]), ("maskB", [128, 128]),
        ("indh", [H, DFF]), ("indhp", [H, DI]),
        ("lnmask", [4352]), ("drep", [128, DI]),
        ("gown", [8]), ("gprev", [8]),
        ("selown", [4]), ("selprev", [4]), ("selnxt", [4]),
        ("selobc", [128, 4]), ("selpbc", [128, 4]), ("selnbc", [128, 4]),
        ("zer32", [32, 128]), ("neg3", [3, 128]),
        ("mask2", [128, 2]), ("d32bc", [128, 32]),
    ]:
        dt = BF16 if name in ("identb", "maskF", "maskB", "indh",
                              "indhp", "zer32", "neg3") else F32
        din[name] = nc.dram_tensor(name, shape, dt, kind="ExternalInput").ap()

    x1buf = nc.dram_tensor("x1buf", [USL, D], F32).ap()
    xln2fm = nc.dram_tensor("xln2fm", [D, USL], BF16).ap()
    xogbuf = nc.dram_tensor("xogbuf", [T + 2, DI], BF16).ap()
    xogfm = nc.dram_tensor("xogfm", [CONVDIM, T], BF16).ap()
    ylf = nc.dram_tensor("ylf", [T + 2, DI], BF16).ap()
    ylb = nc.dram_tensor("ylb", [T + 2, DI], BF16).ap()
    flscr = nc.dram_tensor("flscr", [10, DFF], BF16).ap()
    ccin = nc.dram_tensor("ccin", [264, DI], F32).ap()
    ccout = nc.dram_tensor("ccout", [4 * 264, DI], F32).ap()
    zbuf = nc.dram_tensor("zbuf", [T, DI], BF16).ap()
    adtbuf = nc.dram_tensor("adtbuf", [64, T], F32).ap()
    x2buf = nc.dram_tensor("x2buf", [T, D], F32).ap()
    dtfmbuf = nc.dram_tensor("dtfmbuf", [64, T], F32).ap()
    dtlnF = nc.dram_tensor("dtlnF", [64, T], F32).ap()

    outs = {}
    if DBG:
        for nm, shape, dt in [
            ("dbg_x1", [USL, D], F32), ("dbg_xln2fm", [D, USL], BF16),
            ("dbg_dt", [64, T], F32), ("dbg_xog", [T + 2, DI], BF16),
            ("dbg_ylf", [T, DI], BF16), ("dbg_ylb", [T, DI], BF16),
            ("dbg_sf", [NS, DI], F32), ("dbg_sb", [NS, DI], F32),
            ("dbg_fd", [H, T], BF16), ("dbg_eact", [128, NCH * 64], BF16),
        ]:
            outs[nm] = nc.dram_tensor(nm, shape, dt,
                                      kind="ExternalOutput").ap()
    out = nc.dram_tensor("out", [T, D], F32, kind="ExternalOutput").ap()

    topstack = contextlib.ExitStack()

    def SB(stk, name, shape, dt=F32):
        return stk.enter_context(nc.sbuf_tensor(name, shape, dt))

    def PS(stk, name, shape, dt=F32):
        return stk.enter_context(nc.psum_tensor(name, shape, dt))

    identb = SB(topstack, "identb_s", [128, 128], BF16)
    identf = SB(topstack, "identf_s", [128, 128])
    lnmask = SB(topstack, "lnmask_s", [128, 34])
    em.dma("sp", identb[:, :], din["identb"])
    em.dma("sp", identf[:, :], din["identf"])
    em.dma("sp", lnmask[:, :], din["lnmask"].rearrange("(c p) -> p c", p=128),
           slow=True)

    def p1(stk):
        w1b = SB(stk, "w1b", [128, 8 * DFF], BF16)
        w2b = SB(stk, "w2b", [128, 32 * D], BF16)
        b1c = SB(stk, "b1c", [128, 32])
        b2c = SB(stk, "b2c", [128, 8])
        thetac = SB(stk, "thetac", [128, 32])
        thetab = SB(stk, "thetab", [128, 32])
        coefc = SB(stk, "coefc", [128, 32])
        ut = SB(stk, "ut", [128, NG * D])
        sumc = SB(stk, "sumc", [128, 8])
        sqc = SB(stk, "sqc", [128, 8])
        st1 = SB(stk, "st1", [128, 8])
        st2 = SB(stk, "st2", [128, 8])
        st3 = SB(stk, "st3", [128, 8])
        rstd = SB(stk, "rstd", [128, 8])
        nmr = SB(stk, "nmr", [128, 8])
        sq_scr = SB(stk, "sq_scr", [128, D])
        xlnt = SB(stk, "xlnt", [128, NG * D], BF16)
        xfm = SB(stk, "xfm", [128, 8 * TILE], BF16)
        aa = SB(stk, "aa", [128, TILE], BF16)
        kfl = SB(stk, "kfl", [128, TILE], BF16)
        sargt = SB(stk, "sargt", [128, TILE], BF16)
        stile = SB(stk, "stile", [128, TILE])
        qt = SB(stk, "qt", [128, TILE], BF16)
        hact = SB(stk, "hact", [128, 32 * TILE], BF16)
        o2t = [SB(stk, f"o2t{i}", [128, TILE], BF16) for i in range(2)]
        x1t = SB(stk, "x1t", [128, NG * D])
        xfm2 = SB(stk, "xfm2", [128, 8 * TILE], BF16)
        phs = SB(stk, "phs", [128, 32])
        wrs = SB(stk, "wrs", [128, 32])
        wis = SB(stk, "wis", [128, 32])
        t32a = SB(stk, "t32a", [128, 32])
        t32b = SB(stk, "t32b", [128, 32])
        ptT = PS(stk, "ptT", [128, 2048], BF16)
        pfb = [PS(stk, f"pf{i}", [128, 512]) for i in range(3)]
        pob = [PS(stk, f"po{i}", [128, 512]) for i in range(2)]

        for k in range(8):
            em.dma("pool", w1b[:, k * DFF:(k + 1) * DFF],
                   din["w1"][k * 128:(k + 1) * 128, :], tag="w1")
        for k in range(8):
            em.dma("pool",
                   w2b[:, k * 4 * D:(k + 1) * 4 * D].rearrange(
                       "p (c n) -> p c n", c=4),
                   din["w2"].rearrange("(c p) n -> p c n",
                                       p=128)[:, 4 * k:4 * k + 4, :], tag="w2")
        em.dma("sp", b1c[:, :], din["b1"].rearrange("(c p) -> p c", p=128),
               slow=True)
        em.dma("sp", b2c[:, :], din["b2"].rearrange("(c p) -> p c", p=128),
               slow=True)
        em.dma("sp", phs[:, :], din["phases"].rearrange("(c p) -> p c", p=128),
               slow=True)
        em.dma("sp", wrs[:, :], din["wreal"].rearrange("(c p) -> p c", p=128),
               slow=True)
        em.dma("sp", wis[:, :], din["wimag"].rearrange("(c p) -> p c", p=128),
               slow=True)
        em.copy(thetac[:, :], phs[:, :])
        for _ in range(6):
            em.wrap1(thetac[:, :], thetac[:, :], t32a[:, :], t32b[:, :])
        em.act(t32a[:, :], thetac[:, :], AF.Sin)
        em.act(t32b[:, :], thetac[:, :], AF.Sin, scale=0.5)
        em.tt(t32b[:, :], t32b[:, :], t32b[:, :], ALU.mult)
        em.ts(t32b[:, :], t32b[:, :], -2.0, 1.0, ALU.mult, ALU.add)
        em.tt(t32a[:, :], t32a[:, :], wis[:, :], ALU.mult)
        em.tt(t32b[:, :], t32b[:, :], wrs[:, :], ALU.mult)
        em.tt(t32a[:, :], t32a[:, :], t32b[:, :], ALU.add)
        em.tt(t32a[:, :], t32a[:, :], wis[:, :], ALU.mult)
        em.ts(coefc[:, :], t32a[:, :], 0.1, None, ALU.mult)
        em.stt(thetab[:, :], b1c[:, :], 0.1, thetac[:, :], ALU.mult, ALU.add)

        def ln_tokmajor(src, xout, mask_col=None, soff=0):
            for g in range(NG):
                gs = src[:, g * D:(g + 1) * D]
                em.act(sq_scr[:, :], gs, AF.Square,
                       accum_out=sqc[:, soff + g:soff + g + 1])
                em.reduce(sumc[:, soff + g:soff + g + 1], gs, ALU.add)
            sl = slice(soff, soff + NG)
            em.ts(st1[:, sl], sumc[:, sl], 1.0 / D, None, ALU.mult)
            em.ts(st2[:, sl], sqc[:, sl], 1.0 / D, None, ALU.mult)
            em.tt(st3[:, sl], st1[:, sl], st1[:, sl], ALU.mult)
            em.tt(st2[:, sl], st2[:, sl], st3[:, sl], ALU.subtract)
            em.ts(st2[:, sl], st2[:, sl], EPS, None, ALU.add)
            em.rsqrt(rstd[:, sl], st2[:, sl], st3[:, sl])
            em.tt(nmr[:, sl], st1[:, sl], rstd[:, sl], ALU.mult)
            em.ts(nmr[:, sl], nmr[:, sl], -1.0, None, ALU.mult)
            for g in range(NG):
                c = soff + g
                em.ts(xout[:, g * D:(g + 1) * D], src[:, g * D:(g + 1) * D],
                      rstd[:, c:c + 1], nmr[:, c:c + 1], ALU.mult, ALU.add)
                if mask_col is not None:
                    em.ts(xout[:, g * D:(g + 1) * D],
                          xout[:, g * D:(g + 1) * D],
                          mask_col[:, g:g + 1], None, ALU.mult)

        def tm_to_fm(src_bf, dst, tag):
            for k in range(8):
                for g in range(NG):
                    em.transpose(
                        ptT[:, (k * NG + g) * 128:(k * NG + g + 1) * 128],
                        src_bf[:, g * D + k * 128: g * D + (k + 1) * 128],
                        identb[:, :], tag=tag)
            dap = dst[:, :]
            sap = ptT[:, :]
            dv = bass.AP(dap.tensor, 0, [list(dap.ap[0]), [TILE, 8],
                                         [128, NG], [1, 128]])
            sv = bass.AP(sap.tensor, 0, [list(sap.ap[0]), [NG * 128, 8],
                                         [128, NG], [1, 128]])
            em.copy(dv, sv, tag=tag)

        ntile = USL // TILE
        for t in range(ntile):
            R0 = t * TILE
            em.dma("sp", ut[:, :].rearrange("p (g d) -> p g d", g=NG),
                   din["u"][R0:R0 + TILE, :].rearrange("(g p) d -> p g d",
                                                       p=128), tag=f"p1u{t}")
            ln_tokmajor(ut, xlnt)
            tm_to_fm(xlnt, xfm, f"p1t{t}")
            for f in range(32):
                bank = pfb[f % 3]
                for k in range(8):
                    em.mm(bank[:, 0:TILE],
                          w1b[:, k * DFF + f * 128: k * DFF + (f + 1) * 128],
                          xfm[:, k * TILE:(k + 1) * TILE],
                          start=(k == 0), stop=(k == 7))
                em.ts(aa[:, :], bank[:, 0:TILE], 0.1, thetab[:, f:f + 1],
                      ALU.mult, ALU.add)
                em.wrap1(sargt[:, :], aa[:, :], kfl[:, :], qt[:, :])
                em.act(stile[:, :], sargt[:, :], AF.Sin)
                em.stt(qt[:, :], stile[:, :], coefc[:, f:f + 1],
                       bank[:, 0:TILE], ALU.mult, ALU.add)
                em.act(hact[:, f * TILE:(f + 1) * TILE], qt[:, :], AF.Relu,
                       bias=b1c[:, f:f + 1])
            for do in range(8):
                bank = pob[do % 2]
                for kf in range(32):
                    em.mm(bank[:, 0:TILE],
                          w2b[:, kf * D + do * 128: kf * D + (do + 1) * 128],
                          hact[:, kf * TILE:(kf + 1) * TILE],
                          start=(kf == 0), stop=(kf == 31))
                em.ts(o2t[do % 2][:, :], bank[:, 0:TILE], b2c[:, do:do + 1],
                      None, ALU.add)
                for g in range(NG):
                    em.transpose(ptT[:, (do % 2) * 1024 + g * 128:
                                     (do % 2) * 1024 + (g + 1) * 128],
                                 o2t[do % 2][:, g * 128:(g + 1) * 128],
                                 identb[:, :])
                for g in range(NG):
                    em.stt(x1t[:, g * D + do * 128: g * D + (do + 1) * 128],
                           ptT[:, (do % 2) * 1024 + g * 128:
                               (do % 2) * 1024 + g * 128 + 128],
                           0.5,
                           ut[:, g * D + do * 128: g * D + (do + 1) * 128],
                           ALU.mult, ALU.add)
            mask_col = (lnmask[:, R0 // 128: R0 // 128 + NG]
                        if (t == 0 or t == ntile - 1) else None)
            ln_tokmajor(x1t, xlnt, mask_col=mask_col, soff=4)
            tm_to_fm(xlnt, xfm2, f"p1s{t}")
            em.dma("act",
                   x1buf[R0:R0 + TILE, :].rearrange("(g p) d -> p g d", p=128),
                   x1t[:, :].rearrange("p (g d) -> p g d", g=NG),
                   tag=f"x1st{t}")
            em.dma("act",
                   xln2fm[:, R0:R0 + TILE].rearrange("(k p) c -> p k c",
                                                     p=128),
                   xfm2[:, :].rearrange("p (k c) -> p k c", k=8),
                   tag=f"xlst{t}")

    def p2(stk, fd_fm, eact):
        wdtb = SB(stk, "wdtb", [128, 8 * 64], BF16)
        wxbcb = SB(stk, "wxbcb", [128, 8 * CONVDIM], BF16)
        fcdb = SB(stk, "fcdb", [128, 16 * H], BF16)
        convws = SB(stk, "convws", [128, 20 * 4])
        convbs = SB(stk, "convbs", [128, 20])
        convbh = SB(stk, "convbh", [128, 20])
        dtb2c = SB(stk, "dtb2c", [64, 1])
        a64c = SB(stk, "a64c", [64, 1])
        onec = SB(stk, "onec", [64, 1])
        adtc2 = [SB(stk, f"adtc{i}", [32, 128]) for i in range(2)]
        tri_fs = SB(stk, "tri_fs", [128, 128])
        tri_bs = SB(stk, "tri_bs", [128, 128])
        maskf_s = SB(stk, "maskf_s", [128, 128], BF16)
        maskb_s = SB(stk, "maskb_s", [128, 128], BF16)
        xz2 = [SB(stk, f"xz{i}", [128, 8 * 517], BF16)
               for i in range(2)]
        cog = [SB(stk, "cog0", [128, 20 * 514], BF16)] * 2
        convacc = SB(stk, "convacc", [128, 1028])
        convv = SB(stk, "convv", [128, 1028])
        offc = SB(stk, "offc", [32, 2])
        aclast = SB(stk, "aclast", [32, 8])
        cdc = SB(stk, "cdc", [32, 8])
        acs_c2 = [SB(stk, f"acs_c{i}", [32, 128]) for i in range(2)]
        acsn2 = [SB(stk, f"acsn{i}", [32, 128]) for i in range(2)]
        acsH2 = [SB(stk, f"acsH{i}", [32, 128], BF16) for i in range(2)]
        acsM2 = [SB(stk, f"acsM{i}", [32, 128], BF16) for i in range(2)]
        acsL2 = [SB(stk, f"acsL{i}", [32, 128], BF16) for i in range(2)]
        acsr2 = [SB(stk, f"acsr{i}", [32, 128]) for i in range(2)]
        lndt322 = [SB(stk, f"lndt32{i}", [32, 128]) for i in range(2)]
        xsp2 = [SB(stk, f"xsp{i}", [32, 128]) for i in range(2)]
        dtH2 = [SB(stk, f"dtH{i}", [32, 128], BF16) for i in range(2)]
        dtM2 = [SB(stk, f"dtM{i}", [32, 128], BF16) for i in range(2)]
        dtL2 = [SB(stk, f"dtL{i}", [32, 128], BF16) for i in range(2)]
        e_loc2 = [SB(stk, f"e_loc{i}", [32, 128]) for i in range(2)]
        dts322 = [SB(stk, f"dts32{i}", [32, 128]) for i in range(2)]
        ddb322 = [SB(stk, f"ddb32{i}", [32, 128], BF16) for i in range(2)]
        cdmat = SB(stk, "cdmat", [32, 128], BF16)
        ones32 = SB(stk, "ones32", [32, 128], BF16)
        din_indhp = SB(stk, "din_indhp", [32, DI], BF16)
        dd_fm2 = [SB(stk, f"dd_fm{i}", [32, 128]) for i in range(2)]
        lhsD2 = [SB(stk, f"lhsD{i}", [128, 128], BF16) for i in range(2)]
        rhsD = SB(stk, "rhsD", [128, DFF], BF16)
        Dm_sb2 = [SB(stk, f"Dm_sb{i}", [128, DFF], BF16) for i in range(2)]
        U_sb2 = [SB(stk, f"U_sb{i}", [128, 256], BF16) for i in range(2)]
        xh_sb2 = [SB(stk, f"xh_sb{i}", [128, DI], BF16) for i in range(2)]
        xt2_sb2 = [SB(stk, f"xt2_sb{i}", [128, DI], BF16) for i in range(2)]
        btm_sb2 = [SB(stk, f"btm_sb{i}", [128, 128], BF16) for i in range(2)]
        dtT_c2 = [SB(stk, f"dtT_c{i}", [128, 32]) for i in range(2)]
        elocT2 = [SB(stk, f"elocT{i}", [128, 32]) for i in range(2)]
        yoff_sb2 = [SB(stk, f"yoff_sb{i}", [128, DI], BF16) for i in range(2)]
        y_sb2 = [SB(stk, f"y_sb{i}", [128, DI], BF16) for i in range(2)]
        S_f = SB(stk, "S_f", [128, DI])
        S_b = SB(stk, "S_b", [128, DI])
        sbf = SB(stk, "sbf", [128, DI], BF16)
        xgf2 = [SB(stk, f"xgf{i}", [128, 16 * 128], BF16) for i in range(2)]
        bfm2 = [SB(stk, f"bfm{i}", [128, 128], BF16) for i in range(2)]
        cfm2 = [SB(stk, f"cfm{i}", [128, 128], BF16) for i in range(2)]
        a32 = SB(stk, "a32", [32, 1])
        pA = PS(stk, "pA", [128, 2048])
        pB = PS(stk, "pB", [128, 1024])
        pT = PS(stk, "pT", [128, 2048], BF16)

        for k in range(8):
            em.dma("pool", wdtb[:, k * 64:(k + 1) * 64],
                   din["ipwT"][k * 128:(k + 1) * 128, 4608:4672], tag="wdt")
        for k in range(8):
            em.dma("pool", wxbcb[:, k * CONVDIM:(k + 1) * CONVDIM],
                   din["ipwT"][k * 128:(k + 1) * 128, DI:DI + CONVDIM],
                   tag="wxbc")
        for k in range(16):
            em.dma("pool", fcdb[:, k * H:(k + 1) * H],
                   din["fcdT"][k * 128:(k + 1) * 128, :], tag="fcd")
        em.dma("sp", convws[:, :].rearrange("p (c j) -> p c j", c=20),
               din["convw"].rearrange("(c p) j -> p c j", p=128), slow=True)
        em.dma("sp", convbs[:, :],
               din["convb"].rearrange("(c p) -> p c", p=128), slow=True)
        em.dma("sp", dtb2c[:, :], din["dtbias2"].unsqueeze(1),
               slow=True)
        em.dma("sp", a32[:, :], din["alog"].unsqueeze(1), slow=True)
        em.dma("sp", tri_fs[:, :], din["triu"])
        em.dma("sp", tri_bs[:, :], din["tril"])
        em.dma("sp", maskf_s[:, :], din["maskF"])
        em.dma("sp", maskb_s[:, :], din["maskB"])
        for r in range(3):
            em.dma("sp", rhsD[32 * r:32 * (r + 1), :], din["indh"], tag="ind")
        em.memset(ones32[:, :], 1.0)
        em.dma("sp", din_indhp[:, :], din["indhp"])
        for rz in range(96, 106):
            em.dma("sp", rhsD[rz:rz + 1, :],
                   din["zer32"].rearrange("a b -> (a b)").unsqueeze(0),
                   tag=f"rz{rz}")
        em.dma("sp", lhsD2[0][96:128, :], din["zer32"], tag="lz0")
        em.dma("sp", lhsD2[1][96:128, :], din["zer32"], tag="lz1")
        em.dma("sp", lhsD2[0][96:99, :], din["neg3"], tag="ln0")
        em.dma("sp", lhsD2[1][99:102, :], din["neg3"], tag="ln1")
        em.ts(convbh[:, :], convbs[:, :], 0.5, None, ALU.mult)
        em.memset(onec[:, :], 1.0)
        em.act(a32[:, :], a32[:, :], AF.Exp)
        em.ts(a32[:, :], a32[:, :], -1.0, None, ALU.mult)
        em.copy(a64c[0:32, :], a32[:, :])
        em.copy(a64c[32:64, :], a32[:, :])
        dtcol = convacc[0:64, 0:512]
        dtscr = convv[0:64, 0:512]
        for t8 in range(8):
            c0 = 4 + 512 * t8
            xz = xz2[t8 % 2]
            em.dma("sp",
                   bass.AP(xz[:, :].tensor, 0,
                           [list(xz[:, :].ap[0]), [517, 8], [1, 512]]),
                   xln2fm[:, c0:c0 + 512].rearrange("(k p) c -> p k c", p=128),
                   tag=f"dtx{t8}")
            for k in range(8):
                em.mm(pB[0:64, 0:512], wdtb[:, k * 64:(k + 1) * 64],
                      xz[:, k * 517:k * 517 + 512], start=(k == 0),
                      stop=(k == 7))
            em.ts(dtcol, pB[0:64, 0:512], dtb2c[:, :], None, ALU.add)
            em.act(dtscr, dtcol, AF.Abs)
            em.act(dtscr, dtscr, AF.Exp, scale=-1.0)
            em.act(dtscr, dtscr, AF.Ln, bias=onec[:, :])
            em.ts(dtcol, dtcol, 0.0, None, ALU.max)
            em.tt(dtcol, dtscr, dtcol, ALU.add)
            adts = convv[64:128, 512:1024]
            em.ts(adts, dtcol, a64c[:, :], None, ALU.mult)
            em.dma("act", adtbuf[:, 512 * t8:512 * (t8 + 1)], adts,
                   tag=f"adtst{t8}")
            em.dma("act", dtfmbuf[:, 512 * t8:512 * (t8 + 1)],
                   dtcol, tag=f"dtfm{t8}")
            em.act(dtscr, dtcol, AF.Ln)
            em.ts(dtscr, dtscr, -88.0, None, ALU.max)
            em.dma("act", dtlnF[:, 512 * t8:512 * (t8 + 1)], dtscr,
                   tag=f"lnf{t8}")
        em.memset(offc[:, :], 0.0)
        em.memset(S_f[:, :], 0.0)
        em.memset(S_b[:, :], 0.0)

        def conv_group(g, buf):
            c0 = 4 + 512 * g
            xz = xz2[buf]
            em.dma("sp", xz[:, :].rearrange("p (k c) -> p k c", k=8),
                   xln2fm[:, c0 - 4:c0 + 513].rearrange("(k p) c -> p k c",
                                                        p=128), tag=f"cvx{g}")
            co = cog[buf]
            for fc in range(20):
                pc = pA[:, (fc % 2) * 1024:(fc % 2) * 1024 + 517]
                ca = convacc[:, (fc % 2) * 514:(fc % 2) * 514 + 514]
                cv = convv[:, (fc % 2) * 514:(fc % 2) * 514 + 514]
                for k in range(8):
                    em.mm(pc[:, 0:512],
                          wxbcb[:, k * CONVDIM + fc * 128:
                                k * CONVDIM + (fc + 1) * 128],
                          xz[:, k * 517:k * 517 + 512], start=(k == 0),
                          stop=(k == 7))
                for k in range(8):
                    em.mm(pc[:, 512:517],
                          wxbcb[:, k * CONVDIM + fc * 128:
                                k * CONVDIM + (fc + 1) * 128],
                          xz[:, k * 517 + 512:k * 517 + 517], start=(k == 0),
                          stop=(k == 7))
                em.ts(ca, pc[:, 0:514],
                      convws[:, 4 * fc:4 * fc + 1], None, ALU.mult)
                for j in range(1, 4):
                    em.stt(ca, pc[:, j:514 + j],
                           convws[:, 4 * fc + j:4 * fc + j + 1],
                           ca, ALU.mult, ALU.add)
                em.ts(cv, ca, convbs[:, fc:fc + 1], None, ALU.add)
                em.act(ca, ca, AF.Tanh, scale=0.5, bias=convbh[:, fc:fc + 1])
                em.ts(ca, ca, 0.5, 0.5, ALU.mult, ALU.add)
                em.tt(co[:, fc * 514:(fc + 1) * 514], cv, ca, ALU.mult)
            if g == 0:
                em.dma("act", xogbuf[0:1, :].rearrange("t (c p) -> p c t",
                                                       p=128),
                       co[:, :].rearrange("p (c t) -> p c t",
                                          c=20)[:, 0:16, 0:1],
                       tag="xogpad0", slow=True)
            if g == 7:
                em.dma("act", xogbuf[T + 1:T + 2, :].rearrange(
                           "t (c p) -> p c t", p=128),
                       co[:, :].rearrange("p (c t) -> p c t",
                                          c=20)[:, 0:16, 513:514],
                       tag="xogpad1", slow=True)
            em.dma("act",
                   xogfm[:, 512 * g:512 * (g + 1)].rearrange(
                       "(c p) t -> p c t", p=128),
                   bass.AP(co[:, :].tensor, 1,
                           [list(co[:, :].ap[0]), [514, 20], [1, 512]]),
                   tag=f"xfmst{g}")
            for k in range(16):
                em.mm(pB[0:32, 512:1024], fcdb[:, k * H:(k + 1) * H],
                      co[:, k * 514 + 1:k * 514 + 513], start=(k == 0),
                      stop=(k == 15))
            em.copy(fd_fm[:, 512 * g:512 * (g + 1)], pB[0:32, 512:1024])

        qctr = {0: 0, 1: 0}

        def ssd_stages(c, fwd):
            r0, r1 = (0, 32) if fwd else (32, 64)
            tri = tri_fs if fwd else tri_bs
            mask = maskf_s if fwd else maskb_s
            fcB, fcC = (16, 17) if fwd else (18, 19)
            S = S_f if fwd else S_b
            oc = 0 if fwd else 1
            d = 0 if fwd else 1
            dircol = 0 if fwd else 32
            xgf, bfm, cfm = xgf2[d], bfm2[d], cfm2[d]
            elocT = elocT2[d]
            lhsD = lhsD2[d]
            dts32, ddb32 = dts322[d], ddb322[d]
            acs_c, acsn, acsH = acs_c2[d], acsn2[d], acsH2[d]
            acsM, acsL, acsr = acsM2[d], acsL2[d], acsr2[d]
            lndt32, xsp = lndt322[d], xsp2[d]
            dtH, dtM, dtL = dtH2[d], dtM2[d], dtL2[d]
            e_loc, dd_fm, dtT_c = e_loc2[d], dd_fm2[d], dtT_c2[d]
            Dm_sb, U_sb, xh_sb = Dm_sb2[d], U_sb2[d], xh_sb2[d]
            xt2_sb, btm_sb = xt2_sb2[d], btm_sb2[d]
            yoff_sb, y_sb = yoff_sb2[d], y_sb2[d]
            pAd = lambda lo, hi: pA[:, d * 1024 + lo:d * 1024 + hi]

            def nextq():
                i = qctr[d]
                qctr[d] += 1
                return d * 1024 + (i % 2) * 512
            pBd = lambda lo, hi: pB[:, d * 512 + lo:d * 512 + hi]
            pTd = lambda lo, hi: pT[:, d * 1024 + lo:d * 1024 + hi]
            rr0 = 96 if fwd else 99

            def H1():
                em.dma("sp", xgf[:, :].rearrange("p (c t) -> p c t", c=16),
                       xogfm[:, c * 128:(c + 1) * 128].rearrange(
                           "(c p) t -> p c t", p=128)[:, 0:16, :],
                       tag=f"xgl{c}{d}")
                em.dma("sp", bfm[:, :], xogfm[fcB * 128:(fcB + 1) * 128,
                                              c * 128:(c + 1) * 128],
                       tag=f"bl{c}{d}")
                em.dma("sp", cfm[:, :], xogfm[fcC * 128:(fcC + 1) * 128,
                                              c * 128:(c + 1) * 128],
                       tag=f"cl{c}{d}")
                em.dma("sp", adtc2[d][:, :],
                       adtbuf[r0:r1, c * 128:(c + 1) * 128], tag=f"adl{c}{d}")
                em.transpose(pBd(0, 32), adtc2[d][:, :], identf[0:32, 0:32])
                em.copy(dtT_c[:, :], pBd(0, 32))
                em.mm(pBd(32, 160)[0:32, :], dtT_c[:, :], tri[:, :],
                      start=True, stop=True)
                em.copy(acs_c[:, :], pBd(32, 160)[0:32, :])
                last = 127 if fwd else 0
                em.copy(aclast[:, oc:oc + 1], acs_c[:, last:last + 1])
                em.act(e_loc[:, :], acs_c[:, :], AF.Exp)
                em.act(dd_fm[:, :], acs_c[:, :], AF.Exp, scale=-1.0,
                       bias=aclast[:, oc:oc + 1])
                em.act(cdc[:, oc:oc + 1], aclast[:, oc:oc + 1], AF.Exp)
                em.act(acsn[:, :], acs_c[:, :], AF.Exp, bias=offc[:, oc:oc + 1])
                em.transpose(pBd(160, 192), acsn[:, :], identf[0:32, 0:32])
                em.copy(eact[:, c * 64 + dircol: c * 64 + dircol + 32],
                        pBd(160, 192))
                em.tt(offc[:, oc:oc + 1], offc[:, oc:oc + 1],
                      aclast[:, oc:oc + 1], ALU.add)
                em.transpose(pBd(192, 224), e_loc[:, :], identf[0:32, 0:32])
                em.copy(elocT[:, :], pBd(192, 224))
                em.dma("sp", dts32[:, :],
                       dtfmbuf[r0:r1, c * 128:(c + 1) * 128],
                       tag=f"dtl{c}{d}")
                em.dma("sp", lndt32[:, :],
                       dtlnF[r0:r1, c * 128:(c + 1) * 128],
                       tag=f"lnl{c}{d}")
                em.tt(dd_fm[:, :], dd_fm[:, :], dts32[:, :], ALU.mult)
                em.copy(ddb32[:, :], dd_fm[:, :])
                em.ts(acsn[:, :], acs_c[:, :], -1.0, None, ALU.mult)
                em.copy(acsH[:, :], acsn[:, :])
                em.tt(acsr[:, :], acsn[:, :], acsH[:, :], ALU.subtract)
                em.copy(acsM[:, :], acsr[:, :])
                em.tt(acsr[:, :], acsr[:, :], acsM[:, :], ALU.subtract)
                em.copy(acsL[:, :], acsr[:, :])
                em.tt(xsp[:, :], acsn[:, :], lndt32[:, :], ALU.add)
                em.copy(dtH[:, :], xsp[:, :])
                em.tt(xsp[:, :], xsp[:, :], dtH[:, :], ALU.subtract)
                em.copy(dtM[:, :], xsp[:, :])
                em.tt(xsp[:, :], xsp[:, :], dtM[:, :], ALU.subtract)
                em.copy(dtL[:, :], xsp[:, :])
                em.copy(lhsD[0:32, :], dtH[:, :])
                em.copy(lhsD[32:64, :], dtM[:, :])
                em.copy(lhsD[64:96, :], dtL[:, :])
                em.dma("pool", rhsD[rr0:rr0 + 1, :],
                       flscr[3 * d:3 * d + 1, :], tag="dum") if False else None
                for ri, asrc in ((0, acsH), (1, acsM), (2, acsL)):
                    em.dma("pool", flscr[3 * d + ri:3 * d + ri + 1, :]
                           .rearrange("a (b c) -> (a b) c", b=32)
                           if False else
                           flscr[3 * d + ri:3 * d + ri + 1, :].rearrange(
                               "a (b c) -> (a b) c", b=32), asrc[:, :],
                           tag=f"flA{d}{ri}")
                    em.dma("pool", rhsD[rr0 + ri:rr0 + ri + 1, :],
                           flscr[3 * d + ri:3 * d + ri + 1, :],
                           tag=f"flB{d}{ri}")

            def H2():
                for q in range(8):
                    b = nextq()
                    em.mm(pA[:, b:b + 512], lhsD[0:102, :],
                          rhsD[0:102, q * 512:(q + 1) * 512],
                          start=True, stop=True)
                    em.ts(Dm_sb[:, q * 512:(q + 1) * 512],
                          pA[:, b:b + 512], 30.0, None, ALU.min)
                em.act(Dm_sb[:, :], Dm_sb[:, :], AF.Exp)
                em.mm(pBd(224, 352), bfm[:, :], cfm[:, :], start=True,
                      stop=True)
                em.tt(U_sb[:, 0:128], pBd(224, 352), mask[:, :], ALU.mult)
                em.copy(U_sb[:, 128:256], U_sb[:, 0:128])

            def M1():
                for w in range(2):
                    for k in range(8):
                        em.transpose(pTd(k * 128, (k + 1) * 128),
                                     xgf[:, (w * 8 + k) * 128:
                                         (w * 8 + k + 1) * 128], identb[:, :])
                    em.copy(xh_sb[:, w * 1024:(w + 1) * 1024], pTd(0, 1024))
                if fwd:
                    em.dma("act", xogbuf[c * 128 + 1:c * 128 + 129, :],
                           xh_sb[:, :], tag=f"xogst{c}")
                for q in range(4):
                    b = nextq()
                    em.mm(pA[:, b:b + 512], ddb32[:, :],
                          din_indhp[0:32, q * 512:(q + 1) * 512],
                          start=True, stop=True)
                    em.tt(xt2_sb[:, q * 512:(q + 1) * 512],
                          xh_sb[:, q * 512:(q + 1) * 512], pA[:, b:b + 512],
                          ALU.mult)
                em.transpose(pTd(0, 128), bfm[:, :], identb[:, :])
                em.copy(btm_sb[:, :], pTd(0, 128))
                for h2 in range(16):
                    em.tt(Dm_sb[:, h2 * 256:(h2 + 1) * 256],
                          Dm_sb[:, h2 * 256:(h2 + 1) * 256], U_sb[:, :],
                          ALU.mult)

            def T1():
                em.copy(sbf[:, :], S[:, :])
                ybuf = ylf if fwd else ylb
                for q in range(4):
                    b = nextq()
                    em.mm(pA[:, b:b + 512], cfm[:, :],
                          sbf[:, q * 512:(q + 1) * 512],
                          start=True, stop=True)
                    for h in range(8):
                        hh = q * 8 + h
                        em.act(yoff_sb[:, hh * 64:(hh + 1) * 64],
                               pA[:, b + h * 64:b + (h + 1) * 64], AF.Copy,
                               scale=elocT[:, hh:hh + 1])
                    b2 = nextq()
                    for h in range(8):
                        hh = q * 8 + h
                        em.mm(pA[:, b2 + h * 64:b2 + (h + 1) * 64],
                              Dm_sb[:, hh * 128:(hh + 1) * 128],
                              xh_sb[:, hh * 64:(hh + 1) * 64], start=True,
                              stop=True)
                    em.tt(y_sb[:, q * 512:(q + 1) * 512],
                          pA[:, b2:b2 + 512],
                          yoff_sb[:, q * 512:(q + 1) * 512], ALU.add)
                em.dma("act", ybuf[c * 128 + 1:c * 128 + 129, :], y_sb[:, :],
                       tag=f"yst{c}{d}")

            def T2():
                em.ts(cdmat[:, :], ones32[:, :], cdc[:, oc:oc + 1], None,
                      ALU.mult)
                for q in range(4):
                    b = nextq()
                    em.mm(pA[:, b:b + 512], cdmat[:, :],
                          din_indhp[0:32, q * 512:(q + 1) * 512],
                          start=True, stop=True)
                    em.tt(S[:, q * 512:(q + 1) * 512],
                          S[:, q * 512:(q + 1) * 512], pA[:, b:b + 512],
                          ALU.mult)
                    b2 = nextq()
                    em.mm(pA[:, b2:b2 + 512], btm_sb[:, :],
                          xt2_sb[:, q * 512:(q + 1) * 512],
                          start=True, stop=True)
                    em.tt(S[:, q * 512:(q + 1) * 512], pA[:, b2:b2 + 512],
                          S[:, q * 512:(q + 1) * 512], ALU.add)

            return [H1, H2, M1, T1, T2]

        conv_order = [0, 7, 1, 6, 2, 5, 3, 4]
        conv_group(conv_order[0], 0)
        conv_group(conv_order[1], 1)
        convs_done = 2
        units = []
        for ci in range(NCH):
            units.append((ci, True))
            units.append((NCH - 1 - ci, False))
        stv = [ssd_stages(c, f) for (c, f) in units]
        stv[0][0]()
        prev = None
        for i in range(len(stv)):
            if i in (4, 12, 20) and convs_done < 8:
                conv_group(conv_order[convs_done], convs_done % 2)
                conv_group(conv_order[convs_done + 1],
                           (convs_done + 1) % 2)
                convs_done += 2
            if prev is not None:
                stv[prev][3]()
                stv[prev][4]()
            stv[i][1]()
            stv[i][2]()
            if i + 1 < len(stv):
                stv[i + 1][0]()
            prev = i
        stv[prev][3]()
        stv[prev][4]()

        cdt2 = SB(stk, "cdt2", [32, 2])
        em.memset(convacc[:, :], 0.0)
        em.dma("sp", ccin[259:264, :].rearrange("r (a c) -> (r a) c", a=4),
               convacc[0:20, 0:512], tag="ccz1")
        em.dma("sp", ccin[256:257, 384:2048].rearrange(
                   "r (a c) -> (r a) c", a=4), convacc[20:24, 0:416],
               tag="ccz2")
        em.act(cdt2[:, :], offc[:, :], AF.Exp)
        em.dma("sp", ccin[0:128, :], S_f[:, :], tag="ccsf")
        em.dma("sp", ccin[128:256, :], S_b[:, :], tag="ccsb")
        em.dma("sp", ccin[256:257, 0:32].rearrange("a (b c) -> (a b) c", b=32),
               cdt2[:, 0:1], tag="cccdf", slow=True)
        em.dma("sp", ccin[256:257, 32:64].rearrange("a (b c) -> (a b) c", b=32),
               cdt2[:, 1:2], tag="cccdb", slow=True)
        em.dma("pool", ccin[256:257, 64:96], eact[127:128, 31 * 64:31 * 64 + 32],
               tag="cccf")
        em.dma("pool", ccin[256:257, 96:128], eact[0:1, 32:64], tag="cccb")
        em.dma("pool", ccin[256:257, 128:256].rearrange(
                   "a (b c) -> (a b) c", b=128),
               xogfm[17 * 128:18 * 128, 4095:4096], tag="ccCf", slow=True)
        em.dma("pool", ccin[256:257, 256:384].rearrange(
                   "a (b c) -> (a b) c", b=128),
               xogfm[19 * 128:20 * 128, 0:1], tag="ccCb", slow=True)
        em.dma("pool", ccin[257:258, :], ylf[T:T + 1, :], tag="ccyf")
        em.dma("pool", ccin[258:259, :], ylb[1:2, :], tag="ccyb")
        if DBG:
            em.dma("sp", outs["dbg_dt"], dtfmbuf[:, :], tag="dbgdt")
            em.dma("sp", outs["dbg_sf"], S_f[:, :], tag="dbgsf")
            em.dma("sp", outs["dbg_sb"], S_b[:, :], tag="dbgsb")
            em.dma("sp", outs["dbg_fd"], fd_fm[:, :], tag="dbgfd")
            em.dma("sp", outs["dbg_eact"], eact[:, :], tag="dbgea")


    def p3x(stk, eact):
        onesf = SB(stk, "onesf", [1, 128])
        sgt = SB(stk, "sgt", [128, DI])
        kf = [SB(stk, f"kf{j}", [128, DI]) for j in (1, 2, 3)]
        kb = [SB(stk, f"kb{j}", [128, DI]) for j in (0, 1, 2)]
        carryb16 = {n: SB(stk, "cb16_" + n, [128, DI], BF16)
                    for n in ("f", "fp", "b", "bn")}
        cdflat2 = SB(stk, "cdflat2", [1, 32])
        cdbc2 = SB(stk, "cdbc2", [128, 32])
        selo = SB(stk, "selo", [128, 4])
        selp = SB(stk, "selp", [128, 4])
        seln = SB(stk, "seln", [128, 4])
        ytile = SB(stk, "ytile", [1, 4 * DI], BF16)
        yacc = SB(stk, "yacc", [1, DI])
        cmat = SB(stk, "cmat", [128, 4])
        cw = SB(stk, "cw", [128, 4])
        csel = SB(stk, "csel", [128, 1], BF16)
        cself = SB(stk, "cself", [128, 1])
        cfl = SB(stk, "cfl", [1, 4 * 32])
        cflacc = SB(stk, "cflacc", [1, 32])
        padrow = SB(stk, "padrow", [1, DI])
        ylt4 = [[SB(stk, f"ylt{x}{d}", [128, DI], BF16) for d in range(2)]
                for x in range(2)]
        yout4 = [[SB(stk, f"yout{x}{d}", [128, DI], BF16) for d in range(2)]
                 for x in range(2)]
        cchk2 = [SB(stk, f"cchk{d}", [128, 128], BF16) for d in range(2)]
        gsca2 = [SB(stk, f"gsca{d}", [128, 1024], BF16) for d in range(2)]
        coefT2 = [SB(stk, f"coefT{d}", [128, 32]) for d in range(2)]
        wzr = SB(stk, "wzr", [128, 8 * DI], BF16)
        zsl = [SB(stk, f"zsl{x}", [128, DI], BF16) for x in range(2)]
        xzt2 = [SB(stk, f"xzt{x}", [128, 8 * 128], BF16) for x in range(2)]
        pA = PS(stk, "pA3", [128, 2048])
        pB = PS(stk, "pB3", [128, 512])
        pAz = PS(stk, "pAz", [128, 1024])
        em.memset(onesf[:, :], 1.0)
        em.dma("sp", selo[:, :], din["selobc"])
        em.dma("sp", selp[:, :], din["selpbc"])
        em.dma("sp", seln[:, :], din["selnbc"])
        em.collective("AllGather", ALU.bypass, [[0, 1, 2, 3], [4, 5, 6, 7]],
                      [ccin], [ccout], tag="ag")
        for k in range(8):
            em.dma("pool", wzr[:, k * DI:(k + 1) * DI],
                   din["ipwT"][k * 128:(k + 1) * 128, 0:DI], tag="wz")

        def z_sub(i):
            p = i % 2
            em.dma("sp", xzt2[p][:, :].rearrange("p (k c) -> p k c", k=8),
                   xln2fm[:, 4 + i * 128:4 + (i + 1) * 128].rearrange(
                       "(k p) c -> p k c", p=128), tag=f"zx{i}")
            for q in range(4):
                ph = pAz[:, (q % 2) * 512:(q % 2) * 512 + 512]
                for k in range(8):
                    em.mm(ph, xzt2[p][:, k * 128:(k + 1) * 128],
                          wzr[:, k * DI + q * 512:k * DI + (q + 1) * 512],
                          start=(k == 0), stop=(k == 7))
                em.act(zsl[p][:, q * 512:(q + 1) * 512], ph, AF.Silu)
            em.dma("act", zbuf[i * 128:(i + 1) * 128, :], zsl[p][:, :],
                   tag=f"zst{i}")

        for zi in range(12):
            z_sub(zi)

        def cdrep_load(j, col0):
            em.dma("sp", cdflat2[:, :], ccout[j * 264 + 256:j * 264 + 257,
                                              col0:col0 + 32], tag="cdl")
            em.mm(pB[:, 0:32], onesf[:, :], cdflat2[:, :], start=True,
                  stop=True)
            em.copy(cdbc2[:, :], pB[:, 0:32])

        def chain_step(dst, src_k, sg_row, j, col0):
            em.dma("sp", sgt[:, :], ccout[sg_row:sg_row + 128, :], tag="sgl")
            cdrep_load(j, col0)
            for h in range(32):
                em.ts(dst[:, h * 64:(h + 1) * 64],
                      src_k[:, h * 64:(h + 1) * 64],
                      cdbc2[:, h:h + 1], None, ALU.mult)
            em.tt(dst[:, :], dst[:, :], sgt[:, :], ALU.add)

        em.dma("sp", kf[0][:, :], ccout[0 * 264:0 * 264 + 128, :], tag="sg0")
        chain_step(kf[1], kf[0], 1 * 264, 1, 0)
        chain_step(kf[2], kf[1], 2 * 264, 2, 0)
        em.dma("sp", kb[2][:, :], ccout[3 * 264 + 128:3 * 264 + 256, :],
               tag="sg3")
        chain_step(kb[1], kb[2], 2 * 264 + 128, 2, 32)
        chain_step(kb[0], kb[1], 1 * 264 + 128, 1, 32)

        def select3(dst, ks, selcol, idxs):
            em.ts(dst[:, :], ks[0][:, :], selcol[:, idxs[0]:idxs[0] + 1],
                  None, ALU.mult)
            for k, j in zip(ks[1:], idxs[1:]):
                em.stt(dst[:, :], k[:, :], selcol[:, j:j + 1], dst[:, :],
                       ALU.mult, ALU.add)

        select3(carryb16["f"], kf, selo, [1, 2, 3])
        select3(carryb16["fp"], kf, selp, [1, 2, 3])
        select3(carryb16["b"], kb, selo, [0, 1, 2])
        select3(carryb16["bn"], kb, seln, [0, 1, 2])

        def pad_assemble(ybufrow, yrow_base, cf_col0, cc_col0, selt, cname,
                         tag):
            for j in range(4):
                em.dma("pool", ytile[:, j * DI:(j + 1) * DI],
                       ccout[j * 264 + yrow_base:j * 264 + yrow_base + 1, :],
                       tag=tag + "y")
                em.dma("sp", cfl[:, j * 32:(j + 1) * 32],
                       ccout[j * 264 + 256:j * 264 + 257,
                             cf_col0:cf_col0 + 32], tag=tag + "c")
                em.dma("sp", cmat[:, j:j + 1],
                       ccout[j * 264 + 256:j * 264 + 257,
                             cc_col0:cc_col0 + 128].rearrange(
                                 "a (b c) -> (a b) c", b=128), tag=tag + "C",
                       slow=True)
            em.ts(yacc[:, :], ytile[:, 0:DI], selt[0:1, 0:1], None, ALU.mult)
            for j in range(1, 4):
                em.stt(yacc[:, :], ytile[:, j * DI:(j + 1) * DI],
                       selt[0:1, j:j + 1], yacc[:, :], ALU.mult, ALU.add)
            em.ts(cflacc[:, :], cfl[:, 0:32], selt[0:1, 0:1], None, ALU.mult)
            for j in range(1, 4):
                em.stt(cflacc[:, :], cfl[:, j * 32:(j + 1) * 32],
                       selt[0:1, j:j + 1], cflacc[:, :], ALU.mult, ALU.add)
            em.tt(cw[:, :], cmat[:, :], selt[:, :], ALU.mult)
            em.reduce(cself[:, :], cw[:, :], ALU.add)
            em.copy(csel[:, :], cself[:, :])
            for j in range(4):
                em.mm(pA[0:1, j * 512:(j + 1) * 512], csel[:, :],
                      carryb16[cname][:, j * 512:(j + 1) * 512],
                      start=True, stop=True)
            for h in range(32):
                em.stt(padrow[:, h * 64:(h + 1) * 64],
                       pA[0:1, h * 64:(h + 1) * 64],
                       cflacc[0:1, h:h + 1], yacc[:, h * 64:(h + 1) * 64],
                       ALU.mult, ALU.add)
            em.dma("pool", ybufrow, padrow[:, :], tag=tag + "st")

        pad_assemble(ylf[0:1, :], 257, 64, 128, selp, "fp", "padf")
        pad_assemble(ylb[T + 1:T + 2, :], 258, 96, 256, seln, "bn", "padb")

        def p4a_stage(i, fwd, st):
            d = 0 if fwd else 1
            ybuf = ylf if fwd else ylb
            fcC = 17 if fwd else 19
            cn = "f" if fwd else "b"
            dircol = 0 if fwd else 32
            pAd = pA[:, d * 1024:(d + 1) * 1024]
            if st == 0:
                em.dma("sp", cchk2[d][:, :],
                       xogfm[fcC * 128:(fcC + 1) * 128,
                             i * 128:(i + 1) * 128], tag=f"c4{i}{d}")
                em.dma("sp", ylt4[i % 2][d][:, :],
                       ybuf[i * 128 + 1:i * 128 + 129, :], tag=f"yl4{i}{d}")
                em.copy(coefT2[d][:, :], eact[:, i * 64 + dircol:
                                              i * 64 + dircol + 32])
            else:
                for w in range(2):
                    for j in range(2):
                        em.mm(pAd[:, j * 512:(j + 1) * 512], cchk2[d][:, :],
                              carryb16[cn][:, w * 1024 + j * 512:
                                           w * 1024 + (j + 1) * 512],
                              start=True, stop=True)
                    if w == 0:
                        for h in range(16):
                            em.stt(yout4[i % 2][d][:, h * 64:(h + 1) * 64],
                                   pAd[:, h * 64:(h + 1) * 64],
                                   coefT2[d][:, h:h + 1],
                                   ylt4[i % 2][d][:, h * 64:(h + 1) * 64],
                                   ALU.mult, ALU.add)
                    else:
                        for h in range(8):
                            hh = 16 + h
                            em.act(gsca2[d][:, h * 64:(h + 1) * 64],
                                   pAd[:, h * 64:(h + 1) * 64], AF.Copy,
                                   scale=coefT2[d][:, hh:hh + 1])
                        em.tt(yout4[i % 2][d][:, 1024:1536],
                              ylt4[i % 2][d][:, 1024:1536],
                              gsca2[d][:, 0:512], ALU.add)
                        for h in range(8, 16):
                            hh = 16 + h
                            em.stt(yout4[i % 2][d][:, hh * 64:(hh + 1) * 64],
                                   pAd[:, h * 64:(h + 1) * 64],
                                   coefT2[d][:, hh:hh + 1],
                                   ylt4[i % 2][d][:, hh * 64:(hh + 1) * 64],
                                   ALU.mult, ALU.add)
                em.dma("act", ybuf[i * 128 + 1:i * 128 + 129, :],
                       yout4[i % 2][d][:, :], tag=f"yw4{i}{d}")
        p4a_stage(0, True, 0)
        p4a_stage(0, False, 0)
        for i in range(NCH):
            p4a_stage(i, True, 1)
            if i + 1 < NCH:
                p4a_stage(i + 1, True, 0)
            p4a_stage(i, False, 1)
            if i + 1 < NCH:
                p4a_stage(i + 1, False, 0)
            if 12 + i < NCH:
                z_sub(12 + i)

    def p4m(stk, fd_fm):
        opwb = SB(stk, "opwb", [128, 16 * D], BF16)
        nwc = SB(stk, "nwc", [128, 16])
        drept = SB(stk, "drept", [128, DI], BF16)
        d32 = SB(stk, "d32", [128, 32])
        m2 = SB(stk, "m2", [128, 2])
        x1t = SB(stk, "x1t4", [128, NG * D])
        yft = SB(stk, "yft", [128, NG * DI], BF16)
        ybt = SB(stk, "ybt", [128, NG * DI], BF16)
        xm1 = SB(stk, "xm1", [128, NG * DI], BF16)
        xct = SB(stk, "xct", [128, NG * DI], BF16)
        xp1 = SB(stk, "xp1", [128, NG * DI], BF16)
        zst = SB(stk, "zst", [128, NG * DI], BF16)
        indhp4 = SB(stk, "indhp4", [32, DI], BF16)
        drow = SB(stk, "drow", [1, DI], BF16)
        ones1 = SB(stk, "ones1", [1, 128], BF16)
        fdd2 = [SB(stk, f"fdd{g}", [128, DI], BF16)
                for g in range(NG)]
        yga = SB(stk, "yga", [128, NG * DI], BF16)
        ygb2 = [SB(stk, f"ygb{g}", [128, DI], BF16)
                for g in range(NG)]
        ygbf = SB(stk, "ygbf", [128, NG * DI], BF16)
        sqc = SB(stk, "sqc4", [128, 8])
        st2 = SB(stk, "st24", [128, 8])
        st3 = SB(stk, "st34", [128, 8])
        rstd = SB(stk, "rstd4", [128, 8])
        sq_scr = SB(stk, "sq_scr4", [128, DI])
        ynfm = SB(stk, "ynfm", [128, 16 * 256], BF16)
        mixt = SB(stk, "mixt", [128, NG * D])
        x2t = SB(stk, "x2t", [128, NG * D])
        ptT = PS(stk, "ptT4", [128, 2048], BF16)
        pob = [PS(stk, f"po{i}_4", [128, 512]) for i in range(2)]
        pm4 = PS(stk, "pm4", [128, 2048])

        for k in range(16):
            em.dma("pool", opwb[:, k * D:(k + 1) * D],
                   din["opwT"][k * 128:(k + 1) * 128, :], tag="opw")
        em.dma("sp", nwc[:, :], din["normw"].rearrange("(c p) -> p c", p=128),
               slow=True)
        em.dma("pool", drept[:, :], din["drep"])
        em.dma("pool", drow[:, :], din["drep"][0:1, :])
        em.memset(ones1[:, :], 1.0)
        em.dma("sp", indhp4[:, :], din["indhp"])
        em.dma("sp", m2[:, :], din["mask2"])
        for k in range(16):
            em.ts(opwb[:, k * D:(k + 1) * D], opwb[:, k * D:(k + 1) * D],
                  nwc[:, k:k + 1], None, ALU.mult)

        def rms_pair(src, xout):
            for g in range(NG):
                gs = src[:, g * DI:(g + 1) * DI]
                em.act(sq_scr[:, :], gs, AF.Square, accum_out=sqc[:, g:g + 1])
            sl = slice(0, NG)
            em.ts(st2[:, sl], sqc[:, sl], 1.0 / DI, EPS, ALU.mult, ALU.add)
            em.rsqrt(rstd[:, sl], st2[:, sl], st3[:, sl])
            for g in range(NG):
                em.ts(xout[:, g * DI:(g + 1) * DI],
                      src[:, g * DI:(g + 1) * DI], rstd[:, g:g + 1], None,
                      ALU.mult)

        for pr in range(16):
            i0 = pr * 2
            em.dma("sp", x1t[:, :].rearrange("p (g d) -> p g d", g=NG),
                   x1buf[4 + i0 * 128:4 + (i0 + 2) * 128, :].rearrange(
                       "(g p) d -> p g d", p=128), tag=f"x1l{pr}")
            em.dma("sp", yft[:, :].rearrange("p (g d) -> p g d", g=NG),
                   ylf[i0 * 128:(i0 + 2) * 128, :].rearrange(
                       "(g p) d -> p g d", p=128), tag=f"yfl{pr}")
            em.dma("sp", ybt[:, :].rearrange("p (g d) -> p g d", g=NG),
                   ylb[i0 * 128 + 2:(i0 + 2) * 128 + 2, :].rearrange(
                       "(g p) d -> p g d", p=128), tag=f"ybl{pr}")
            em.dma("sp", xm1[:, :].rearrange("p (g d) -> p g d", g=NG),
                   xogbuf[i0 * 128:(i0 + 2) * 128, :].rearrange(
                       "(g p) d -> p g d", p=128), tag=f"xml{pr}")
            em.dma("sp", xct[:, :].rearrange("p (g d) -> p g d", g=NG),
                   xogbuf[i0 * 128 + 1:(i0 + 2) * 128 + 1, :].rearrange(
                       "(g p) d -> p g d", p=128), tag=f"xcl{pr}")
            em.dma("sp", xp1[:, :].rearrange("p (g d) -> p g d", g=NG),
                   xogbuf[i0 * 128 + 2:(i0 + 2) * 128 + 2, :].rearrange(
                       "(g p) d -> p g d", p=128), tag=f"xpl{pr}")
            em.dma("sp", zst[:, :].rearrange("p (g d) -> p g d", g=NG),
                   zbuf[i0 * 128:(i0 + 2) * 128, :].rearrange(
                       "(g p) d -> p g d", p=128), tag=f"zl{pr}")
            if pr == 0:
                em.ts(xm1[:, 0:DI], xm1[:, 0:DI], m2[:, 0:1], None, ALU.mult)
            if pr == 15:
                em.ts(xp1[:, DI:2 * DI], xp1[:, DI:2 * DI], m2[:, 1:2], None,
                      ALU.mult)
            for g in range(NG):
                i = i0 + g
                ys = slice(g * DI, (g + 1) * DI)
                ygb = ygb2[g]
                fdd = fdd2[g]
                for w in range(2):
                    ph = pm4[:, (g % 2) * 1024:(g % 2) * 1024 + 1024]
                    for j in range(2):
                        jj = w * 2 + j
                        em.mm(ph[:, j * 512:(j + 1) * 512],
                              fd_fm[:, i * 128:(i + 1) * 128],
                              indhp4[0:32, jj * 512:(jj + 1) * 512],
                              start=True, stop=False)
                        em.mm(ph[:, j * 512:(j + 1) * 512], ones1[:, :],
                              drow[:, jj * 512:(jj + 1) * 512], start=False,
                              stop=True)
                    em.act(fdd[:, w * 1024:(w + 1) * 1024], ph[:, :],
                           AF.Copy)
                em.tt(yga[:, ys], yft[:, ys], ybt[:, ys], ALU.add)
                em.tt(ygb[:, :], xm1[:, ys], xp1[:, ys], ALU.add)
                em.tt(ygb[:, :], ygb[:, :], drept[:, :], ALU.mult)
                em.tt(yga[:, ys], yga[:, ys], ygb[:, :], ALU.add)
                em.tt(ygb[:, :], fdd[:, :], xct[:, ys], ALU.mult)
                em.tt(yga[:, ys], yga[:, ys], ygb[:, :], ALU.add)
                em.tt(yga[:, ys], yga[:, ys], zst[:, ys], ALU.mult)
            rms_pair(yga, ygbf)
            for g in range(NG):
                for k in range(16):
                    em.transpose(ptT[:, k * 128:(k + 1) * 128],
                                 ygbf[:, g * DI + k * 128:
                                      g * DI + (k + 1) * 128], identb[:, :])
                em.copy(ynfm[:, :].rearrange("p (k r) -> p k r", k=16)
                        [:, :, g * 128:(g + 1) * 128],
                        ptT[:, :].rearrange("p (k r) -> p k r", k=16))
            for do in range(8):
                bank = pob[do % 2]
                for k in range(16):
                    em.mm(bank[:, 0:256],
                          opwb[:, k * D + do * 128:k * D + (do + 1) * 128],
                          ynfm[:, k * 256:(k + 1) * 256],
                          start=(k == 0), stop=(k == 15))
                em.copy(mixt[:, :].rearrange("p (g e) -> p g e", g=NG)
                        [:, :, do * 128:(do + 1) * 128],
                        bank[:, 0:256].rearrange("p (g c) -> p g c", g=NG))
            for g in range(NG):
                for w in range(2):
                    for dd in range(4):
                        em.transpose(
                            pob[w % 2][:, dd * 128:(dd + 1) * 128],
                            mixt[:, g * D + (w * 4 + dd) * 128:
                                 g * D + (w * 4 + dd + 1) * 128],
                            identf[:, :])
                    for dd in range(4):
                        em.copy(x2t[:, g * D + (w * 4 + dd) * 128:
                                    g * D + (w * 4 + dd + 1) * 128],
                                pob[w % 2][:, dd * 128:(dd + 1) * 128])
            for g in range(NG):
                em.tt(x2t[:, g * D:(g + 1) * D], x2t[:, g * D:(g + 1) * D],
                      x1t[:, g * D:(g + 1) * D], ALU.add)
            em.dma("act",
                   x2buf[i0 * 128:(i0 + 2) * 128, :].rearrange(
                       "(g p) d -> p g d", p=128),
                   x2t[:, :].rearrange("p (g d) -> p g d", g=NG),
                   tag=f"x2st{pr}")

    def p4f(stk):
        w1b = SB(stk, "w1b4", [128, 8 * DFF], BF16)
        w2b = SB(stk, "w2b4", [128, 32 * D], BF16)
        b1c = SB(stk, "b1c4", [128, 32])
        b2c = SB(stk, "b2c4", [128, 8])
        thetac = SB(stk, "thetac4", [128, 32])
        thetab = SB(stk, "thetab4", [128, 32])
        coefc = SB(stk, "coefc4", [128, 32])
        ut = SB(stk, "ut4", [128, NG * D])
        sumc = SB(stk, "sumc4", [128, 8])
        sqc = SB(stk, "sqc4f", [128, 8])
        st1 = SB(stk, "st14", [128, 8])
        st2 = SB(stk, "st24f", [128, 8])
        st3 = SB(stk, "st34f", [128, 8])
        rstd = SB(stk, "rstd4f", [128, 8])
        nmr = SB(stk, "nmr4", [128, 8])
        sq_scr = SB(stk, "sq_scr4f", [128, D])
        xlnt = SB(stk, "xlnt4", [128, NG * D], BF16)
        xfm = SB(stk, "xfm4", [128, 8 * TILE], BF16)
        aa = SB(stk, "aa4", [128, TILE], BF16)
        kfl = SB(stk, "kfl4", [128, TILE], BF16)
        sargt = SB(stk, "sargt4", [128, TILE], BF16)
        stile = SB(stk, "stile4", [128, TILE])
        qt = SB(stk, "qt4", [128, TILE], BF16)
        hact = SB(stk, "hact4", [128, 32 * TILE], BF16)
        o2t = [SB(stk, f"o2t{i}_4", [128, TILE], BF16) for i in range(2)]
        outt = SB(stk, "outt", [128, NG * D])
        phs = SB(stk, "phs4", [128, 32])
        wrs = SB(stk, "wrs4", [128, 32])
        wis = SB(stk, "wis4", [128, 32])
        t32a = SB(stk, "t32a4", [128, 32])
        t32b = SB(stk, "t32b4", [128, 32])
        ptT = PS(stk, "ptT4f", [128, 2048], BF16)
        pfb = [PS(stk, f"pf{i}_4", [128, 512]) for i in range(3)]
        pob = [PS(stk, f"po{i}_4f", [128, 512]) for i in range(2)]

        for k in range(8):
            em.dma("pool", w1b[:, k * DFF:(k + 1) * DFF],
                   din["w1"][k * 128:(k + 1) * 128, :], tag="w1r")
        for k in range(8):
            em.dma("pool",
                   w2b[:, k * 4 * D:(k + 1) * 4 * D].rearrange(
                       "p (c n) -> p c n", c=4),
                   din["w2"].rearrange("(c p) n -> p c n",
                                       p=128)[:, 4 * k:4 * k + 4, :],
                   tag="w2r")
        em.dma("sp", b1c[:, :], din["b1"].rearrange("(c p) -> p c", p=128),
               slow=True)
        em.dma("sp", b2c[:, :], din["b2"].rearrange("(c p) -> p c", p=128),
               slow=True)
        em.dma("sp", phs[:, :], din["phases"].rearrange("(c p) -> p c", p=128),
               slow=True)
        em.dma("sp", wrs[:, :], din["wreal"].rearrange("(c p) -> p c", p=128),
               slow=True)
        em.dma("sp", wis[:, :], din["wimag"].rearrange("(c p) -> p c", p=128),
               slow=True)
        em.copy(thetac[:, :], phs[:, :])
        for _ in range(6):
            em.wrap1(thetac[:, :], thetac[:, :], t32a[:, :], t32b[:, :])
        em.act(t32a[:, :], thetac[:, :], AF.Sin)
        em.act(t32b[:, :], thetac[:, :], AF.Sin, scale=0.5)
        em.tt(t32b[:, :], t32b[:, :], t32b[:, :], ALU.mult)
        em.ts(t32b[:, :], t32b[:, :], -2.0, 1.0, ALU.mult, ALU.add)
        em.tt(t32a[:, :], t32a[:, :], wis[:, :], ALU.mult)
        em.tt(t32b[:, :], t32b[:, :], wrs[:, :], ALU.mult)
        em.tt(t32a[:, :], t32a[:, :], t32b[:, :], ALU.add)
        em.tt(t32a[:, :], t32a[:, :], wis[:, :], ALU.mult)
        em.ts(coefc[:, :], t32a[:, :], 0.1, None, ALU.mult)
        em.stt(thetab[:, :], b1c[:, :], 0.1, thetac[:, :], ALU.mult, ALU.add)

        for t in range(16):
            R0 = t * TILE
            em.dma("sp", ut[:, :].rearrange("p (g d) -> p g d", g=NG),
                   x2buf[R0:R0 + TILE, :].rearrange("(g p) d -> p g d",
                                                    p=128), tag=f"p4u{t}")
            for g in range(NG):
                gs = ut[:, g * D:(g + 1) * D]
                em.act(sq_scr[:, :], gs, AF.Square,
                       accum_out=sqc[:, g:g + 1])
                em.reduce(sumc[:, g:g + 1], gs, ALU.add)
            sl = slice(0, NG)
            em.ts(st1[:, sl], sumc[:, sl], 1.0 / D, None, ALU.mult)
            em.ts(st2[:, sl], sqc[:, sl], 1.0 / D, None, ALU.mult)
            em.tt(st3[:, sl], st1[:, sl], st1[:, sl], ALU.mult)
            em.tt(st2[:, sl], st2[:, sl], st3[:, sl], ALU.subtract)
            em.ts(st2[:, sl], st2[:, sl], EPS, None, ALU.add)
            em.rsqrt(rstd[:, sl], st2[:, sl], st3[:, sl])
            em.tt(nmr[:, sl], st1[:, sl], rstd[:, sl], ALU.mult)
            em.ts(nmr[:, sl], nmr[:, sl], -1.0, None, ALU.mult)
            for g in range(NG):
                em.ts(xlnt[:, g * D:(g + 1) * D], ut[:, g * D:(g + 1) * D],
                      rstd[:, g:g + 1], nmr[:, g:g + 1], ALU.mult, ALU.add)
            for k in range(8):
                for g in range(NG):
                    em.transpose(
                        ptT[:, (k * NG + g) * 128:(k * NG + g + 1) * 128],
                        xlnt[:, g * D + k * 128: g * D + (k + 1) * 128],
                        identb[:, :])
            dap = xfm[:, :]
            sap = ptT[:, :]
            em.copy(bass.AP(dap.tensor, 0, [list(dap.ap[0]), [TILE, 8],
                                            [128, NG], [1, 128]]),
                    bass.AP(sap.tensor, 0, [list(sap.ap[0]), [NG * 128, 8],
                                            [128, NG], [1, 128]]))
            for f in range(32):
                bank = pfb[f % 3]
                for k in range(8):
                    em.mm(bank[:, 0:TILE],
                          w1b[:, k * DFF + f * 128: k * DFF + (f + 1) * 128],
                          xfm[:, k * TILE:(k + 1) * TILE],
                          start=(k == 0), stop=(k == 7))
                em.ts(aa[:, :], bank[:, 0:TILE], 0.1, thetab[:, f:f + 1],
                      ALU.mult, ALU.add)
                em.wrap1(sargt[:, :], aa[:, :], kfl[:, :], qt[:, :])
                em.act(stile[:, :], sargt[:, :], AF.Sin)
                em.stt(qt[:, :], stile[:, :], coefc[:, f:f + 1],
                       bank[:, 0:TILE], ALU.mult, ALU.add)
                em.act(hact[:, f * TILE:(f + 1) * TILE], qt[:, :], AF.Relu,
                       bias=b1c[:, f:f + 1])
            for do in range(8):
                bank = pob[do % 2]
                for kf2 in range(32):
                    em.mm(bank[:, 0:TILE],
                          w2b[:, kf2 * D + do * 128: kf2 * D + (do + 1) * 128],
                          hact[:, kf2 * TILE:(kf2 + 1) * TILE],
                          start=(kf2 == 0), stop=(kf2 == 31))
                em.ts(o2t[do % 2][:, :], bank[:, 0:TILE], b2c[:, do:do + 1],
                      None, ALU.add)
                for g in range(NG):
                    em.transpose(ptT[:, (do % 2) * 1024 + g * 128:
                                     (do % 2) * 1024 + (g + 1) * 128],
                                 o2t[do % 2][:, g * 128:(g + 1) * 128],
                                 identb[:, :])
                for g in range(NG):
                    em.stt(outt[:, g * D + do * 128: g * D + (do + 1) * 128],
                           ptT[:, (do % 2) * 1024 + g * 128:
                               (do % 2) * 1024 + g * 128 + 128],
                           0.5,
                           ut[:, g * D + do * 128: g * D + (do + 1) * 128],
                           ALU.mult, ALU.add)
            em.dma("act",
                   out[R0:R0 + TILE, :].rearrange("(g p) d -> p g d", p=128),
                   outt[:, :].rearrange("p (g d) -> p g d", g=NG),
                   tag=f"outst{t}")

    midstack = contextlib.ExitStack()
    fd_fm = SB(midstack, "fd_fm", [32, T], BF16)
    eact = SB(midstack, "eact", [128, NCH * 64], BF16)
    if "P1" in phases:
        with contextlib.ExitStack() as stk:
            p1(stk)
            em.barrier()
    if "P2" in phases:
        with contextlib.ExitStack() as stk:
            p2(stk, fd_fm, eact)
            em.barrier()
    if "P4" in phases:
        with contextlib.ExitStack() as stk:
            p3x(stk, eact)
            em.barrier()
        with contextlib.ExitStack() as stk:
            p4m(stk, fd_fm)
            em.barrier()
        with contextlib.ExitStack() as stk:
            p4f(stk)
            em.barrier()
    midstack.close()

    if DBG:
        em.dma("sp", outs["dbg_x1"], x1buf, tag="dbgx1")
        em.dma("sp", outs["dbg_xln2fm"], xln2fm, tag="dbgxln")
        if "P2" in phases:
            em.dma("sp", outs["dbg_xog"], xogbuf, tag="dbgxog")
            em.dma("sp", outs["dbg_ylf"], ylf[1:T + 1, :], tag="dbgylf")
            em.dma("sp", outs["dbg_ylb"], ylb[1:T + 1, :], tag="dbgylb")
    if "P4" not in phases:
        em.dma("sp", out, x1buf[4:4 + T, :], tag="outplaceholder")

    with contextlib.ExitStack() as semstack:
        sems = {}
        for n in ("pe", "act", "dve", "pool", "sp"):
            sems[n] = semstack.enter_context(nc.semaphore("s_" + n))
        for n in em.dma_sem_names:
            sems[n] = semstack.enter_context(nc.semaphore("s_" + n))
        block = semstack.enter_context(nc.Block())
        em.emit(block, sems)
    topstack.close()
    return nc



BF = ml_dtypes.bfloat16
L = 16384


def aux_constants():
    c = {}
    c["identb"] = np.eye(128, dtype=BF)
    c["identf"] = np.eye(128, dtype=np.float32)
    tri = np.zeros((128, 128), np.float32)
    for s in range(128):
        tri[s, s:] = 1.0
    c["triu"] = tri
    c["tril"] = np.ascontiguousarray(tri.T)
    c["maskF"] = tri.astype(BF)
    c["maskB"] = np.ascontiguousarray(tri.T).astype(BF)
    indh = np.zeros((H, DFF), np.float32)
    for h in range(H):
        indh[h, h * 128:(h + 1) * 128] = 1.0
    c["indh"] = indh.astype(BF)
    indhp = np.zeros((H, DI), np.float32)
    for h in range(H):
        indhp[h, h * 64:(h + 1) * 64] = 1.0
    c["indhp"] = indhp.astype(BF)
    c["zer32"] = np.zeros((32, 128), BF)
    c["neg3"] = np.full((3, 128), -1.0, BF)
    return c


def prep_inputs(inp):
    u = np.asarray(inp["L0_attn_u"])
    ipw = np.asarray(inp["L0_attn_in_proj_w"])
    aux = aux_constants()
    ipwT = np.ascontiguousarray(ipw.T)
    opwT = np.ascontiguousarray(np.asarray(inp["L0_attn_out_proj_w"]).T)
    fcdT = np.ascontiguousarray(np.asarray(inp["L0_attn_fc_D_w"]).T)
    convw = np.ascontiguousarray(
        np.asarray(inp["L0_attn_conv_w"]).reshape(CONVDIM, 4))
    dtb = np.asarray(inp["L0_attn_dt_bias"])
    dtbias2 = np.ascontiguousarray(np.concatenate([dtb, dtb]))
    drep = np.ascontiguousarray(np.broadcast_to(
        np.repeat(np.asarray(inp["L0_attn_D"]), 64)[None, :], (128, DI))
    ).astype(np.float32)

    d32bc = np.ascontiguousarray(np.broadcast_to(
        np.asarray(inp["L0_attn_D"], np.float32)[None, :], (128, H)))
    shared = dict(
        d32bc=d32bc,
        w1=np.asarray(inp["L0_ffn_W1"]), w2=np.asarray(inp["L0_ffn_W2"]),
        b1=np.asarray(inp["L0_ffn_b1"]), b2=np.asarray(inp["L0_ffn_b2"]),
        phases=np.asarray(inp["L0_ffn_quantum_phases"]),
        wreal=np.asarray(inp["L0_ffn_quantum_weights_real"]),
        wimag=np.asarray(inp["L0_ffn_quantum_weights_imag"]),
        ipwT=ipwT, opwT=opwT, fcdT=fcdT, convw=convw,
        convb=np.asarray(inp["L0_attn_conv_b"]),
        dtbias2=dtbias2, alog=np.asarray(inp["L0_attn_A_log"]),
        dskip=np.asarray(inp["L0_attn_D"]),
        normw=np.asarray(inp["L0_attn_norm_w"]),
        drep=drep,
        **aux,
    )
    in_maps = []
    for c in range(8):
        b, s = c // 4, c % 4
        a = s * T
        usl = np.zeros((USL, D), np.float32)
        lo = a - 4
        lo2, hi2 = max(lo, 0), min(lo + USL, L)
        usl[lo2 - lo: hi2 - lo] = u[b, lo2:hi2]
        lnmask = np.ones(4352, np.float32)
        if s == 0:
            lnmask[0:4] = 0.0
        if s == 3:
            lnmask[4100:] = 0.0
        lnmask[4101:] = 0.0
        gown = np.zeros(8, np.float32)
        gprev = np.zeros(8, np.float32)
        for j in range(4):
            gown[2 * j] = 1.0 if j < s else 0.0
            gown[2 * j + 1] = 1.0 - gown[2 * j]
            gprev[2 * j] = 1.0 if j < s - 1 else 0.0
            gprev[2 * j + 1] = 1.0 - gprev[2 * j]
        selown = np.zeros(4, np.float32)
        selown[s] = 1.0
        selprev = np.zeros(4, np.float32)
        if s > 0:
            selprev[s - 1] = 1.0
        selnxt = np.zeros(4, np.float32)
        if s < 3:
            selnxt[s + 1] = 1.0
        mask2 = np.ones((128, 2), np.float32)
        if s == 0:
            mask2[0, 0] = 0.0
        if s == 3:
            mask2[127, 1] = 0.0
        m = dict(shared)
        m.update(u=usl, lnmask=lnmask, gown=gown, gprev=gprev,
                 selown=selown, selprev=selprev, selnxt=selnxt,
                 selobc=np.ascontiguousarray(
                     np.broadcast_to(selown, (128, 4))),
                 selpbc=np.ascontiguousarray(
                     np.broadcast_to(selprev, (128, 4))),
                 selnbc=np.ascontiguousarray(
                     np.broadcast_to(selnxt, (128, 4))),
                 mask2=mask2)
        in_maps.append(m)
    return in_maps


_CACHE = {}


def kernel(**inputs):
    if "nc" not in _CACHE:
        _CACHE["nc"] = build_program(phases=("P1", "P2", "P4"))
    nc = _CACHE["nc"]
    in_maps = prep_inputs(inputs)
    from concourse.bass_utils import run_bass_kernel_spmd
    res = run_bass_kernel_spmd(nc, in_maps, core_ids=list(range(8)))
    full = np.zeros((2, L, D), np.float32)
    for c in range(8):
        b, s = c // 4, c % 4
        full[b, s * T:(s + 1) * T] = res.results[c]["out"]
    return full


# version 15
# speedup vs baseline: 1.1837x; 1.1837x over previous
import numpy as np
import ml_dtypes

"""Core emitter + program builder for the mk_macaron_d1 megakernel.

This module is developed in scratch/ and inlined into kernel.py when stable.
"""
import math
import numpy as np
import concourse.bass as bass
import concourse.mybir as mybir

F32 = mybir.dt.float32
BF16 = mybir.dt.bfloat16
I32 = mybir.dt.int32
AF = mybir.ActivationFunctionType
ALU = mybir.AluOpType
AX = mybir.AxisListType

PI = float(np.pi)
INV2PI = float(1.0 / (2.0 * np.pi))
N2PI = float(-2.0 * np.pi)
RSQRT_C = float(0x5F3759DF)


ENGINES = ("pe", "act", "dve", "pool", "sp")


def _ap_region(ap):
    t = ap.tensor
    name = t.name
    dims = [list(d) for d in ap.ap]
    try:
        space = str(ap.space)
    except Exception:
        space = "?"
    try:
        esz = np.dtype(mybir.dt.np(ap.dtype)).itemsize
    except Exception:
        esz = 4
    if "DRAM" in space.upper() or "DRam" in str(type(t)):
        off = ap.offset if isinstance(ap.offset, int) else 0
        ext = 1
        for st, ct in dims:
            ext += abs(st) * (ct - 1)
        return (name, 0, 128, off * esz, (off + ext) * esz)
    off = ap.offset if isinstance(ap.offset, int) else 0
    pitch = abs(dims[0][0]) if dims and dims[0][0] else 1
    p0 = off // pitch if pitch > 0 else 0
    p1 = p0 + ap.partition_size()
    coff = off - p0 * pitch
    ext = 1
    for st, ct in dims[1:]:
        ext += abs(st) * (ct - 1)
    b0, b1 = coff * esz, (coff + ext) * esz
    if "PSUM" in space.upper():
        b0 = (b0 // 2048) * 2048
        b1 = ((b1 + 2047) // 2048) * 2048
        p0, p1 = 0, 128
    return (name, p0, p1, b0, b1)


def _overlap(r1, r2):
    return (r1[0] == r2[0] and r1[1] < r2[2] and r2[1] < r1[2]
            and r1[3] < r2[4] and r2[3] < r1[4])


class Op:
    __slots__ = ("engine", "build", "reads", "writes", "idx", "waits",
                 "carrier", "dma_sem", "tag", "accum")

    def __init__(self, engine, build, reads, writes, tag="", accum=False):
        self.engine = engine
        self.build = build
        self.reads = [_ap_region(a) for a in reads if a is not None]
        self.writes = [_ap_region(a) for a in writes if a is not None]
        self.idx = -1
        self.waits = []
        self.carrier = False
        self.tag = tag
        self.accum = accum


NDMASEM = 12


class Emitter:
    def __init__(self, nc):
        self.nc = nc
        self.ops = []
        self.counts = {e: 0 for e in ENGINES}
        self._writers = {}
        self._readers = {}
        self.dma_sem_names = [f"dq{i}" for i in range(NDMASEM)] + ["dqcc"]
        self.dma_counts = {n: 0 for n in self.dma_sem_names}
        self._dma_rr = 0
        self._dma_rr_sw = 4

    def _record(self, op):
        op.idx = len(self.ops)
        self.ops.append(op)
        eng = op.engine
        pre_dma_wait = None
        if eng.startswith("dma:"):
            q = eng.split(":")[1]
            if q.endswith("!cc"):
                sem = "dqcc"
                if self.dma_counts[sem] > 0:
                    pre_dma_wait = ("dma", sem, self.dma_counts[sem])
                self.dma_counts[sem] += 1
                seq = ("dma", sem, self.dma_counts[sem])
                op.dma_sem = sem
            elif q == "pool":
                sem = self.dma_sem_names[self._dma_rr_sw]
                self._dma_rr_sw = 4 + (self._dma_rr_sw + 1 - 4) % 4
            else:
                sem = self.dma_sem_names[self._dma_rr]
                self._dma_rr = (self._dma_rr + 1) % 4
            if not q.endswith("!cc"):
                if self.dma_counts[sem] > 0:
                    pre_dma_wait = ("dma", sem, self.dma_counts[sem])
                self.dma_counts[sem] += 16
                seq = ("dma", sem, self.dma_counts[sem])
                op.dma_sem = sem
        else:
            self.counts[eng] += 1
            seq = ("eng", eng, self.counts[eng])
            op.dma_sem = None
        waits = set()
        for r in op.reads:
            for (reg, src) in self._writers.get(r[0], []):
                if _overlap(reg, r):
                    waits.add(src)
        for w in op.writes:
            for (reg, src) in self._writers.get(w[0], []):
                if _overlap(reg, w):
                    if op.accum and src[0] == "eng" and src[1] == "pe" \
                            and (eng == "pe"):
                        continue
                    waits.add(src)
            for (reg, src) in self._readers.get(w[0], []):
                if _overlap(reg, w):
                    waits.add(src)
        final = []
        for (kind, key, val) in waits:
            if kind == "eng" and key == "pe" and not eng.startswith("dma:") \
                    and eng == "pe":
                continue
            final.append((kind, key, val))
        if pre_dma_wait is not None:
            final.append(pre_dma_wait)
        op.waits = final
        for w in op.writes:
            lst = self._writers.setdefault(w[0], [])
            lst[:] = [(reg, src) for (reg, src) in lst if not _overlap(reg, w)]
            lst.append((w, seq))
            rl = self._readers.get(w[0])
            if rl:
                rl[:] = [(reg, src) for (reg, src) in rl if not _overlap(reg, w)]
        for r in op.reads:
            self._readers.setdefault(r[0], []).append((r, seq))
        return op

    def op(self, engine, build, reads, writes, tag="", accum=False):
        return self._record(Op(engine, build, reads, writes, tag, accum))

    def barrier(self):
        snap_eng = dict(self.counts)
        snap_dma = dict(self.dma_counts)
        for e in ENGINES:
            waits = [("eng", o, snap_eng[o]) for o in ENGINES
                     if snap_eng[o] > 0]
            waits += [("dma", n, v) for n, v in snap_dma.items() if v > 0]
            op = Op(e, lambda h: h.nop(), [], [], tag="barrier")
            self.counts[e] += 1
            op.dma_sem = None
            op.waits = waits
            op.idx = len(self.ops)
            self.ops.append(op)

    def dma(self, queue, out, in_, tag="", slow=False):
        def build(h):
            if slow:
                return h.dma_start(out=out, in_=in_,
                                   allow_slow_non_contiguous=True)
            return h.dma_start(out=out, in_=in_)
        return self._record(Op("dma:" + queue, build, [in_], [out], tag))

    def emit(self, block, sems):
        eng_wait_targets = {e: set() for e in ENGINES}
        for op in self.ops:
            for (kind, key, val) in op.waits:
                if kind == "eng":
                    eng_wait_targets[key].add(val)
        carrier_val = {}
        for e in ENGINES:
            targets = eng_wait_targets[e]
            cnt = 0
            vals = {}
            for n in range(1, self.counts[e] + 1):
                if n in targets:
                    cnt += 1
                    vals[n] = cnt
                else:
                    vals[n] = None
            carrier_val[e] = vals
        carrier_cum = {}
        for e in ENGINES:
            cum = {}
            c = 0
            for n in range(1, self.counts[e] + 1):
                if carrier_val[e][n] is not None:
                    c += 1
                cum[n] = c
            carrier_cum[e] = cum

        streams = {e: [] for e in ENGINES}
        eng_op_no = {e: 0 for e in ENGINES}
        dma_no = {n: 0 for n in self.dma_sem_names}
        for op in self.ops:
            if op.engine.startswith("dma:"):
                q = op.engine.split(":")[1].replace("!cc", "")
                sem = op.dma_sem
                dma_no[sem] += 1 if sem == "dqcc" else 16
                streams[q].append((op, ("dma", sem, dma_no[sem])))
            else:
                eng_op_no[op.engine] += 1
                streams[op.engine].append((op, ("eng", op.engine,
                                                eng_op_no[op.engine])))

        handles = {
            "pe": self.nc.tensor, "act": self.nc.scalar, "dve": self.nc.vector,
            "pool": self.nc.gpsimd, "sp": self.nc.sync,
        }
        emitter = self

        def run_stream(eng_name, eng):
            floors = {}

            def need_wait(kind, key, val):
                if kind == "eng":
                    v = carrier_cum[key][val]
                    if v == 0:
                        return None
                    sem = sems[key]
                    if floors.get(key, 0) >= v:
                        return None
                    floors[key] = v
                    return (sem, v)
                else:
                    if floors.get(key, 0) >= val:
                        return None
                    floors[key] = val
                    return (sems[key], val)

            for (op, seq) in streams[eng_name]:
                for (kind, key, val) in sorted(op.waits,
                                               key=lambda t: (t[1], t[2])):
                    w = need_wait(kind, key, val)
                    if w is not None:
                        eng.wait_ge(w[0], w[1])
                inst = op.build(eng)
                if seq[0] == "eng":
                    v = carrier_val[eng_name].get(seq[2])
                    if v is not None:
                        inst.then_inc(sems[eng_name], 1)
                else:
                    inst.then_inc(sems[seq[1]], 1 if seq[1] == "dqcc" else 16)
            if eng_name == "sp":
                for n in emitter.dma_sem_names:
                    if emitter.dma_counts[n] > 0:
                        eng.wait_ge(sems[n], emitter.dma_counts[n])

        for name in ("sp", "pe", "act", "dve", "pool"):
            dec = {"sp": block.sync, "pe": block.tensor, "act": block.scalar,
                   "dve": block.vector, "pool": block.gpsimd}[name]

            def make(fn_name, engine_obj):
                def _inner(eng):
                    run_stream(fn_name, eng)
                return _inner
            dec(make(name, handles[name]))

    def mm(self, out, lhsT, rhs, start=True, stop=True, tag=""):
        return self.op("pe",
                       lambda e: e.matmul(out, lhsT, rhs, start=start, stop=stop),
                       [lhsT, rhs], [out], tag, accum=not start)

    def transpose(self, out, in_, ident, tag=""):
        return self.op("pe", lambda e: e.transpose(out, in_, ident),
                       [in_, ident], [out], tag)

    def act(self, out, in_, func, bias=0.0, scale=1.0, accum_out=None, tag=""):
        reads = [in_]
        if isinstance(bias, bass.AP):
            reads.append(bias)
        if isinstance(scale, bass.AP):
            reads.append(scale)
        writes = [out] + ([accum_out] if accum_out is not None else [])
        return self.op("act",
                       lambda e: e.activation(out, in_, func, bias=bias,
                                              scale=scale, accum_out=accum_out),
                       reads, writes, tag)

    def _vec(self, engine, meth, out, ins, scalars, tag, kwargs=None):
        reads = list(ins) + [s for s in scalars if isinstance(s, bass.AP)]
        kw = kwargs or {}

        def build(e):
            return getattr(e, meth)(*([out] + list(ins) + list(scalars)), **kw)
        return self.op(engine, build, reads, [out], tag)

    def tt(self, out, a, b, alu, engine="dve", tag=""):
        return self.op(engine, lambda e: e.tensor_tensor(out, a, b, alu),
                       [a, b], [out], tag)

    def ts(self, out, in_, s1, s2, op0, op1=ALU.bypass, engine="dve", tag="",
           accum_out=None):
        reads = [in_] + [s for s in (s1, s2) if isinstance(s, bass.AP)]
        writes = [out] + ([accum_out] if accum_out is not None else [])
        return self.op(engine,
                       lambda e: e.tensor_scalar(out, in_, s1, s2, op0, op1,
                                                 accum_out=accum_out),
                       reads, writes, tag)

    def stt(self, out, in0, scalar, in1, op0, op1, engine="dve", tag=""):
        reads = [in0, in1] + ([scalar] if isinstance(scalar, bass.AP) else [])
        return self.op(engine,
                       lambda e: e.scalar_tensor_tensor(out, in0, scalar, in1,
                                                        op0, op1),
                       reads, [out], tag)

    def copy(self, out, in_, engine="dve", tag=""):
        return self.op(engine, lambda e: e.tensor_copy(out, in_),
                       [in_], [out], tag)

    def memset(self, ap, val, engine="dve", tag=""):
        return self.op(engine, lambda e: e.memset(ap, val), [], [ap], tag)

    def reduce(self, out, in_, alu, engine="dve", tag=""):
        return self.op(engine,
                       lambda e: e.tensor_reduce(out, in_, AX.X, alu),
                       [in_], [out], tag)

    def scan(self, out, d0, d1, init, op0, op1, tag=""):
        reads = [d0, d1] + ([init] if isinstance(init, bass.AP) else [])
        return self.op("dve",
                       lambda e: e.tensor_tensor_scan(out, d0, d1, init, op0, op1),
                       reads, [out], tag)

    def collective(self, kind, alu, groups, ins, outs, tag=""):
        def build(e):
            return e.collective_compute(kind, alu, replica_groups=groups,
                                        ins=ins, outs=outs)
        op = Op("dma:pool!cc", build, ins, outs, tag)
        return self._record(op)

    def wrap1(self, out, a, g1, g2, engine="dve"):
        self.ts(g1, a, -PI, None, ALU.is_lt, engine=engine)
        self.ts(g2, a, PI, None, ALU.is_gt, engine=engine)
        self.tt(g1, g1, g2, ALU.subtract, engine=engine)
        self.stt(out, g1, 2 * PI, a, ALU.mult, ALU.add, engine=engine)
        return out

    def rsqrt(self, out, v, scr1, scr2=None, iters=3):
        self.copy(scr1, v.bitcast(I32))
        self.ts(scr1, scr1, -0.5, RSQRT_C, ALU.mult, ALU.add)
        self.copy(out.bitcast(I32), scr1)
        for _ in range(iters):
            self.tt(scr1, out, out, ALU.mult)
            self.tt(scr1, scr1, v, ALU.mult)
            self.ts(scr1, scr1, -0.5, 1.5, ALU.mult, ALU.add)
            self.tt(out, out, scr1, ALU.mult)
        return out


"""Program builder: phases of the macaron megakernel. TILE=256 tokens."""
import contextlib

D = 1024
DI = 2048
H = 32
PD = 64
NS = 128
CONVDIM = 2560
DFF = 4096
CK = 128
T = 4096
NCH = 32
USL = 4352
EPS = 1e-5
TILE = 256
NG = TILE // 128

DBG = False


def _v(h, offset, dims):
    ap = h if isinstance(h, bass.AP) else h[:, :]
    return bass.AP(ap.tensor, offset, dims)


def build_program(phases=("P1", "P2")):
    nc = bass.Bass("TRN2", target_bir_lowering=False)
    em = Emitter(nc)

    din = {}
    for name, shape in [
        ("u", [USL, D]), ("w1", [D, DFF]), ("w2", [DFF, D]),
        ("b1", [DFF]), ("b2", [D]),
        ("phases", [DFF]), ("wreal", [DFF]), ("wimag", [DFF]),
        ("ipwT", [D, 4672]), ("opwT", [DI, D]), ("fcdT", [DI, H]),
        ("convw", [CONVDIM, 4]), ("convb", [CONVDIM]),
        ("dtbias2", [2 * H]), ("alog", [H]), ("dskip", [H]),
        ("normw", [DI]),
        ("identb", [128, 128]), ("identf", [128, 128]),
        ("triu", [128, 128]), ("tril", [128, 128]),
        ("maskF", [128, 128]), ("maskB", [128, 128]),
        ("indh", [H, DFF]), ("indhp", [H, DI]),
        ("lnmask", [4352]), ("drep", [128, DI]),
        ("gown", [8]), ("gprev", [8]),
        ("selown", [4]), ("selprev", [4]), ("selnxt", [4]),
        ("selobc", [128, 4]), ("selpbc", [128, 4]), ("selnbc", [128, 4]),
        ("zer32", [32, 128]), ("neg3", [3, 128]),
        ("mask2", [128, 2]), ("d32bc", [128, 32]),
    ]:
        dt = BF16 if name in ("identb", "maskF", "maskB", "indh",
                              "indhp", "zer32", "neg3") else F32
        din[name] = nc.dram_tensor(name, shape, dt, kind="ExternalInput").ap()

    x1buf = nc.dram_tensor("x1buf", [USL, D], F32).ap()
    xln2fm = nc.dram_tensor("xln2fm", [D, USL], BF16).ap()
    xogbuf = nc.dram_tensor("xogbuf", [T + 2, DI], BF16).ap()
    xogfm = nc.dram_tensor("xogfm", [CONVDIM, T], BF16).ap()
    ylf = nc.dram_tensor("ylf", [T + 2, DI], BF16).ap()
    ylb = nc.dram_tensor("ylb", [T + 2, DI], BF16).ap()
    flscr = nc.dram_tensor("flscr", [10, DFF], BF16).ap()
    ccin = nc.dram_tensor("ccin", [264, DI], F32).ap()
    ccout = nc.dram_tensor("ccout", [4 * 264, DI], F32).ap()
    zbuf = nc.dram_tensor("zbuf", [T, DI], BF16).ap()
    adtbuf = nc.dram_tensor("adtbuf", [64, T], F32).ap()
    x2buf = nc.dram_tensor("x2buf", [T, D], F32).ap()
    dtfmbuf = nc.dram_tensor("dtfmbuf", [64, T], F32).ap()
    dtlnF = nc.dram_tensor("dtlnF", [64, T], F32).ap()

    outs = {}
    if DBG:
        for nm, shape, dt in [
            ("dbg_x1", [USL, D], F32), ("dbg_xln2fm", [D, USL], BF16),
            ("dbg_dt", [64, T], F32), ("dbg_xog", [T + 2, DI], BF16),
            ("dbg_ylf", [T, DI], BF16), ("dbg_ylb", [T, DI], BF16),
            ("dbg_sf", [NS, DI], F32), ("dbg_sb", [NS, DI], F32),
            ("dbg_fd", [H, T], BF16), ("dbg_eact", [128, NCH * 64], BF16),
        ]:
            outs[nm] = nc.dram_tensor(nm, shape, dt,
                                      kind="ExternalOutput").ap()
    out = nc.dram_tensor("out", [T, D], F32, kind="ExternalOutput").ap()

    topstack = contextlib.ExitStack()

    def SB(stk, name, shape, dt=F32):
        return stk.enter_context(nc.sbuf_tensor(name, shape, dt))

    def PS(stk, name, shape, dt=F32):
        return stk.enter_context(nc.psum_tensor(name, shape, dt))

    identb = SB(topstack, "identb_s", [128, 128], BF16)
    identf = SB(topstack, "identf_s", [128, 128])
    lnmask = SB(topstack, "lnmask_s", [128, 34])
    em.dma("sp", identb[:, :], din["identb"])
    em.dma("sp", identf[:, :], din["identf"])
    em.dma("sp", lnmask[:, :], din["lnmask"].rearrange("(c p) -> p c", p=128),
           slow=True)

    def p1(stk):
        w1b = SB(stk, "w1b", [128, 8 * DFF], BF16)
        w2b = SB(stk, "w2b", [128, 32 * D], BF16)
        b1c = SB(stk, "b1c", [128, 32])
        b2c = SB(stk, "b2c", [128, 8])
        thetac = SB(stk, "thetac", [128, 32])
        thetab = SB(stk, "thetab", [128, 32])
        coefc = SB(stk, "coefc", [128, 32])
        ut = SB(stk, "ut", [128, NG * D])
        sumc = SB(stk, "sumc", [128, 8])
        sqc = SB(stk, "sqc", [128, 8])
        st1 = SB(stk, "st1", [128, 8])
        st2 = SB(stk, "st2", [128, 8])
        st3 = SB(stk, "st3", [128, 8])
        rstd = SB(stk, "rstd", [128, 8])
        nmr = SB(stk, "nmr", [128, 8])
        sq_scr = SB(stk, "sq_scr", [128, D])
        xlnt = SB(stk, "xlnt", [128, NG * D], BF16)
        xfm = SB(stk, "xfm", [128, 8 * TILE], BF16)
        aa = SB(stk, "aa", [128, TILE], BF16)
        kfl = SB(stk, "kfl", [128, TILE], BF16)
        sargt = SB(stk, "sargt", [128, TILE], BF16)
        stile = SB(stk, "stile", [128, TILE])
        qt = SB(stk, "qt", [128, TILE], BF16)
        hact = SB(stk, "hact", [128, 32 * TILE], BF16)
        o2t = [SB(stk, f"o2t{i}", [128, TILE], BF16) for i in range(2)]
        x1t = SB(stk, "x1t", [128, NG * D])
        xfm2 = SB(stk, "xfm2", [128, 8 * TILE], BF16)
        phs = SB(stk, "phs", [128, 32])
        wrs = SB(stk, "wrs", [128, 32])
        wis = SB(stk, "wis", [128, 32])
        t32a = SB(stk, "t32a", [128, 32])
        t32b = SB(stk, "t32b", [128, 32])
        ptT = PS(stk, "ptT", [128, 2048], BF16)
        pfb = [PS(stk, f"pf{i}", [128, 512]) for i in range(3)]
        pob = [PS(stk, f"po{i}", [128, 512]) for i in range(2)]

        for k in range(8):
            em.dma("pool", w1b[:, k * DFF:(k + 1) * DFF],
                   din["w1"][k * 128:(k + 1) * 128, :], tag="w1")
        for k in range(8):
            em.dma("pool",
                   w2b[:, k * 4 * D:(k + 1) * 4 * D].rearrange(
                       "p (c n) -> p c n", c=4),
                   din["w2"].rearrange("(c p) n -> p c n",
                                       p=128)[:, 4 * k:4 * k + 4, :], tag="w2")
        em.dma("sp", b1c[:, :], din["b1"].rearrange("(c p) -> p c", p=128),
               slow=True)
        em.dma("sp", b2c[:, :], din["b2"].rearrange("(c p) -> p c", p=128),
               slow=True)
        em.dma("sp", phs[:, :], din["phases"].rearrange("(c p) -> p c", p=128),
               slow=True)
        em.dma("sp", wrs[:, :], din["wreal"].rearrange("(c p) -> p c", p=128),
               slow=True)
        em.dma("sp", wis[:, :], din["wimag"].rearrange("(c p) -> p c", p=128),
               slow=True)
        em.copy(thetac[:, :], phs[:, :])
        for _ in range(6):
            em.wrap1(thetac[:, :], thetac[:, :], t32a[:, :], t32b[:, :])
        em.act(t32a[:, :], thetac[:, :], AF.Sin)
        em.act(t32b[:, :], thetac[:, :], AF.Sin, scale=0.5)
        em.tt(t32b[:, :], t32b[:, :], t32b[:, :], ALU.mult)
        em.ts(t32b[:, :], t32b[:, :], -2.0, 1.0, ALU.mult, ALU.add)
        em.tt(t32a[:, :], t32a[:, :], wis[:, :], ALU.mult)
        em.tt(t32b[:, :], t32b[:, :], wrs[:, :], ALU.mult)
        em.tt(t32a[:, :], t32a[:, :], t32b[:, :], ALU.add)
        em.tt(t32a[:, :], t32a[:, :], wis[:, :], ALU.mult)
        em.ts(coefc[:, :], t32a[:, :], 0.1, None, ALU.mult)
        em.stt(thetab[:, :], b1c[:, :], 0.1, thetac[:, :], ALU.mult, ALU.add)

        def ln_tokmajor(src, xout, mask_col=None, soff=0):
            for g in range(NG):
                gs = src[:, g * D:(g + 1) * D]
                em.act(sq_scr[:, :], gs, AF.Square,
                       accum_out=sqc[:, soff + g:soff + g + 1])
                em.reduce(sumc[:, soff + g:soff + g + 1], gs, ALU.add)
            sl = slice(soff, soff + NG)
            em.ts(st1[:, sl], sumc[:, sl], 1.0 / D, None, ALU.mult)
            em.ts(st2[:, sl], sqc[:, sl], 1.0 / D, None, ALU.mult)
            em.tt(st3[:, sl], st1[:, sl], st1[:, sl], ALU.mult)
            em.tt(st2[:, sl], st2[:, sl], st3[:, sl], ALU.subtract)
            em.ts(st2[:, sl], st2[:, sl], EPS, None, ALU.add)
            em.rsqrt(rstd[:, sl], st2[:, sl], st3[:, sl])
            em.tt(nmr[:, sl], st1[:, sl], rstd[:, sl], ALU.mult)
            em.ts(nmr[:, sl], nmr[:, sl], -1.0, None, ALU.mult)
            for g in range(NG):
                c = soff + g
                em.ts(xout[:, g * D:(g + 1) * D], src[:, g * D:(g + 1) * D],
                      rstd[:, c:c + 1], nmr[:, c:c + 1], ALU.mult, ALU.add)
                if mask_col is not None:
                    em.ts(xout[:, g * D:(g + 1) * D],
                          xout[:, g * D:(g + 1) * D],
                          mask_col[:, g:g + 1], None, ALU.mult)

        def tm_to_fm(src_bf, dst, tag):
            for k in range(8):
                for g in range(NG):
                    em.transpose(
                        ptT[:, (k * NG + g) * 128:(k * NG + g + 1) * 128],
                        src_bf[:, g * D + k * 128: g * D + (k + 1) * 128],
                        identb[:, :], tag=tag)
            dap = dst[:, :]
            sap = ptT[:, :]
            dv = bass.AP(dap.tensor, 0, [list(dap.ap[0]), [TILE, 8],
                                         [128, NG], [1, 128]])
            sv = bass.AP(sap.tensor, 0, [list(sap.ap[0]), [NG * 128, 8],
                                         [128, NG], [1, 128]])
            em.copy(dv, sv, tag=tag)

        ntile = USL // TILE
        for t in range(ntile):
            R0 = t * TILE
            em.dma("sp", ut[:, :].rearrange("p (g d) -> p g d", g=NG),
                   din["u"][R0:R0 + TILE, :].rearrange("(g p) d -> p g d",
                                                       p=128), tag=f"p1u{t}")
            ln_tokmajor(ut, xlnt)
            tm_to_fm(xlnt, xfm, f"p1t{t}")
            for f in range(32):
                bank = pfb[f % 3]
                for k in range(8):
                    em.mm(bank[:, 0:TILE],
                          w1b[:, k * DFF + f * 128: k * DFF + (f + 1) * 128],
                          xfm[:, k * TILE:(k + 1) * TILE],
                          start=(k == 0), stop=(k == 7))
                em.ts(aa[:, :], bank[:, 0:TILE], 0.1, thetab[:, f:f + 1],
                      ALU.mult, ALU.add)
                em.wrap1(sargt[:, :], aa[:, :], kfl[:, :], qt[:, :])
                em.act(stile[:, :], sargt[:, :], AF.Sin)
                em.stt(qt[:, :], stile[:, :], coefc[:, f:f + 1],
                       bank[:, 0:TILE], ALU.mult, ALU.add)
                em.act(hact[:, f * TILE:(f + 1) * TILE], qt[:, :], AF.Relu,
                       bias=b1c[:, f:f + 1])
            for do in range(8):
                bank = pob[do % 2]
                for kf in range(32):
                    em.mm(bank[:, 0:TILE],
                          w2b[:, kf * D + do * 128: kf * D + (do + 1) * 128],
                          hact[:, kf * TILE:(kf + 1) * TILE],
                          start=(kf == 0), stop=(kf == 31))
                em.ts(o2t[do % 2][:, :], bank[:, 0:TILE], b2c[:, do:do + 1],
                      None, ALU.add)
                for g in range(NG):
                    em.transpose(ptT[:, (do % 2) * 1024 + g * 128:
                                     (do % 2) * 1024 + (g + 1) * 128],
                                 o2t[do % 2][:, g * 128:(g + 1) * 128],
                                 identb[:, :])
                for g in range(NG):
                    em.stt(x1t[:, g * D + do * 128: g * D + (do + 1) * 128],
                           ptT[:, (do % 2) * 1024 + g * 128:
                               (do % 2) * 1024 + g * 128 + 128],
                           0.5,
                           ut[:, g * D + do * 128: g * D + (do + 1) * 128],
                           ALU.mult, ALU.add)
            mask_col = (lnmask[:, R0 // 128: R0 // 128 + NG]
                        if (t == 0 or t == ntile - 1) else None)
            ln_tokmajor(x1t, xlnt, mask_col=mask_col, soff=4)
            tm_to_fm(xlnt, xfm2, f"p1s{t}")
            em.dma("act",
                   x1buf[R0:R0 + TILE, :].rearrange("(g p) d -> p g d", p=128),
                   x1t[:, :].rearrange("p (g d) -> p g d", g=NG),
                   tag=f"x1st{t}")
            em.dma("act",
                   xln2fm[:, R0:R0 + TILE].rearrange("(k p) c -> p k c",
                                                     p=128),
                   xfm2[:, :].rearrange("p (k c) -> p k c", k=8),
                   tag=f"xlst{t}")

    def p2(stk, fd_fm, eact):
        wdtb = SB(stk, "wdtb", [128, 8 * 64], BF16)
        wxbcb = SB(stk, "wxbcb", [128, 8 * CONVDIM], BF16)
        fcdb = SB(stk, "fcdb", [128, 16 * H], BF16)
        convws = SB(stk, "convws", [128, 20 * 4])
        convbs = SB(stk, "convbs", [128, 20])
        convbh = SB(stk, "convbh", [128, 20])
        dtb2c = SB(stk, "dtb2c", [64, 1])
        a64c = SB(stk, "a64c", [64, 1])
        onec = SB(stk, "onec", [64, 1])
        adtc2 = [SB(stk, f"adtc{i}", [32, 128]) for i in range(2)]
        tri_fs = SB(stk, "tri_fs", [128, 128])
        tri_bs = SB(stk, "tri_bs", [128, 128])
        maskf_s = SB(stk, "maskf_s", [128, 128], BF16)
        maskb_s = SB(stk, "maskb_s", [128, 128], BF16)
        xz2 = [SB(stk, f"xz{i}", [128, 8 * 517], BF16)
               for i in range(2)]
        cog = [SB(stk, "cog0", [128, 20 * 514], BF16)] * 2
        convacc = SB(stk, "convacc", [128, 1028])
        convv = SB(stk, "convv", [128, 1028])
        offc = SB(stk, "offc", [32, 2])
        aclast = SB(stk, "aclast", [32, 8])
        cdc = SB(stk, "cdc", [32, 8])
        acs_c2 = [SB(stk, f"acs_c{i}", [32, 128]) for i in range(2)]
        acsn2 = [SB(stk, f"acsn{i}", [32, 128]) for i in range(2)]
        acsH2 = [SB(stk, f"acsH{i}", [32, 128], BF16) for i in range(2)]
        acsM2 = [SB(stk, f"acsM{i}", [32, 128], BF16) for i in range(2)]
        acsL2 = [SB(stk, f"acsL{i}", [32, 128], BF16) for i in range(2)]
        acsr2 = [SB(stk, f"acsr{i}", [32, 128]) for i in range(2)]
        lndt322 = [SB(stk, f"lndt32{i}", [32, 128]) for i in range(2)]
        xsp2 = [SB(stk, f"xsp{i}", [32, 128]) for i in range(2)]
        dtH2 = [SB(stk, f"dtH{i}", [32, 128], BF16) for i in range(2)]
        dtM2 = [SB(stk, f"dtM{i}", [32, 128], BF16) for i in range(2)]
        dtL2 = [SB(stk, f"dtL{i}", [32, 128], BF16) for i in range(2)]
        e_loc2 = [SB(stk, f"e_loc{i}", [32, 128]) for i in range(2)]
        dts322 = [SB(stk, f"dts32{i}", [32, 128]) for i in range(2)]
        ddb322 = [SB(stk, f"ddb32{i}", [32, 128], BF16) for i in range(2)]
        cdmat = SB(stk, "cdmat", [32, 128], BF16)
        ones32 = SB(stk, "ones32", [32, 128], BF16)
        din_indhp = SB(stk, "din_indhp", [32, DI], BF16)
        dd_fm2 = [SB(stk, f"dd_fm{i}", [32, 128]) for i in range(2)]
        lhsD2 = [SB(stk, f"lhsD{i}", [128, 128], BF16) for i in range(2)]
        rhsD = SB(stk, "rhsD", [128, DFF], BF16)
        Dm_sb2 = [SB(stk, f"Dm_sb{i}", [128, DFF], BF16) for i in range(2)]
        U_sb2 = [SB(stk, f"U_sb{i}", [128, 256], BF16) for i in range(2)]
        xh_sb2 = [SB(stk, f"xh_sb{i}", [128, DI], BF16) for i in range(2)]
        xt2_sb2 = [SB(stk, f"xt2_sb{i}", [128, DI], BF16) for i in range(2)]
        btm_sb2 = [SB(stk, f"btm_sb{i}", [128, 128], BF16) for i in range(2)]
        dtT_c2 = [SB(stk, f"dtT_c{i}", [128, 32]) for i in range(2)]
        elocT2 = [SB(stk, f"elocT{i}", [128, 32]) for i in range(2)]
        yoff_sb2 = [SB(stk, f"yoff_sb{i}", [128, DI], BF16) for i in range(2)]
        y_sb2 = [SB(stk, f"y_sb{i}", [128, DI], BF16) for i in range(2)]
        S_f = SB(stk, "S_f", [128, DI])
        S_b = SB(stk, "S_b", [128, DI])
        sbf = SB(stk, "sbf", [128, DI], BF16)
        xgf2 = [SB(stk, f"xgf{i}", [128, 16 * 128], BF16) for i in range(2)]
        bfm2 = [SB(stk, f"bfm{i}", [128, 128], BF16) for i in range(2)]
        cfm2 = [SB(stk, f"cfm{i}", [128, 128], BF16) for i in range(2)]
        a32 = SB(stk, "a32", [32, 1])
        pA = PS(stk, "pA", [128, 2048])
        pB = PS(stk, "pB", [128, 1024])
        pT = PS(stk, "pT", [128, 2048], BF16)

        for k in range(8):
            em.dma("pool", wdtb[:, k * 64:(k + 1) * 64],
                   din["ipwT"][k * 128:(k + 1) * 128, 4608:4672], tag="wdt")
        for k in range(8):
            em.dma("pool", wxbcb[:, k * CONVDIM:(k + 1) * CONVDIM],
                   din["ipwT"][k * 128:(k + 1) * 128, DI:DI + CONVDIM],
                   tag="wxbc")
        for k in range(16):
            em.dma("pool", fcdb[:, k * H:(k + 1) * H],
                   din["fcdT"][k * 128:(k + 1) * 128, :], tag="fcd")
        em.dma("sp", convws[:, :].rearrange("p (c j) -> p c j", c=20),
               din["convw"].rearrange("(c p) j -> p c j", p=128), slow=True)
        em.dma("sp", convbs[:, :],
               din["convb"].rearrange("(c p) -> p c", p=128), slow=True)
        em.dma("sp", dtb2c[:, :], din["dtbias2"].unsqueeze(1),
               slow=True)
        em.dma("sp", a32[:, :], din["alog"].unsqueeze(1), slow=True)
        em.dma("sp", tri_fs[:, :], din["triu"])
        em.dma("sp", tri_bs[:, :], din["tril"])
        em.dma("sp", maskf_s[:, :], din["maskF"])
        em.dma("sp", maskb_s[:, :], din["maskB"])
        for r in range(3):
            em.dma("sp", rhsD[32 * r:32 * (r + 1), :], din["indh"], tag="ind")
        em.memset(ones32[:, :], 1.0)
        em.dma("sp", din_indhp[:, :], din["indhp"])
        for rz in range(96, 106):
            em.dma("sp", rhsD[rz:rz + 1, :],
                   din["zer32"].rearrange("a b -> (a b)").unsqueeze(0),
                   tag=f"rz{rz}")
        em.dma("sp", lhsD2[0][96:128, :], din["zer32"], tag="lz0")
        em.dma("sp", lhsD2[1][96:128, :], din["zer32"], tag="lz1")
        em.dma("sp", lhsD2[0][96:99, :], din["neg3"], tag="ln0")
        em.dma("sp", lhsD2[1][99:102, :], din["neg3"], tag="ln1")
        em.ts(convbh[:, :], convbs[:, :], 0.5, None, ALU.mult)
        em.memset(onec[:, :], 1.0)
        em.act(a32[:, :], a32[:, :], AF.Exp)
        em.ts(a32[:, :], a32[:, :], -1.0, None, ALU.mult)
        em.copy(a64c[0:32, :], a32[:, :])
        em.copy(a64c[32:64, :], a32[:, :])
        dtcol = convacc[0:64, 0:512]
        dtscr = convv[0:64, 0:512]
        for t8 in range(8):
            c0 = 4 + 512 * t8
            xz = xz2[t8 % 2]
            em.dma("sp",
                   bass.AP(xz[:, :].tensor, 0,
                           [list(xz[:, :].ap[0]), [517, 8], [1, 512]]),
                   xln2fm[:, c0:c0 + 512].rearrange("(k p) c -> p k c", p=128),
                   tag=f"dtx{t8}")
            for k in range(8):
                em.mm(pB[0:64, 0:512], wdtb[:, k * 64:(k + 1) * 64],
                      xz[:, k * 517:k * 517 + 512], start=(k == 0),
                      stop=(k == 7))
            em.ts(dtcol, pB[0:64, 0:512], dtb2c[:, :], None, ALU.add)
            em.act(dtscr, dtcol, AF.Abs)
            em.act(dtscr, dtscr, AF.Exp, scale=-1.0)
            em.act(dtscr, dtscr, AF.Ln, bias=onec[:, :])
            em.ts(dtcol, dtcol, 0.0, None, ALU.max)
            em.tt(dtcol, dtscr, dtcol, ALU.add)
            adts = convv[64:128, 512:1024]
            em.ts(adts, dtcol, a64c[:, :], None, ALU.mult)
            em.dma("act", adtbuf[:, 512 * t8:512 * (t8 + 1)], adts,
                   tag=f"adtst{t8}")
            em.dma("act", dtfmbuf[:, 512 * t8:512 * (t8 + 1)],
                   dtcol, tag=f"dtfm{t8}")
            em.act(dtscr, dtcol, AF.Ln)
            em.ts(dtscr, dtscr, -88.0, None, ALU.max)
            em.dma("act", dtlnF[:, 512 * t8:512 * (t8 + 1)], dtscr,
                   tag=f"lnf{t8}")
        em.memset(offc[:, :], 0.0)
        em.memset(S_f[:, :], 0.0)
        em.memset(S_b[:, :], 0.0)

        def conv_group(g, buf):
            c0 = 4 + 512 * g
            xz = xz2[buf]
            em.dma("sp", xz[:, :].rearrange("p (k c) -> p k c", k=8),
                   xln2fm[:, c0 - 4:c0 + 513].rearrange("(k p) c -> p k c",
                                                        p=128), tag=f"cvx{g}")
            co = cog[buf]
            for fc in range(20):
                pc = pA[:, (fc % 2) * 1024:(fc % 2) * 1024 + 517]
                ca = convacc[:, (fc % 2) * 514:(fc % 2) * 514 + 514]
                cv = convv[:, (fc % 2) * 514:(fc % 2) * 514 + 514]
                for k in range(8):
                    em.mm(pc[:, 0:512],
                          wxbcb[:, k * CONVDIM + fc * 128:
                                k * CONVDIM + (fc + 1) * 128],
                          xz[:, k * 517:k * 517 + 512], start=(k == 0),
                          stop=(k == 7))
                for k in range(8):
                    em.mm(pc[:, 512:517],
                          wxbcb[:, k * CONVDIM + fc * 128:
                                k * CONVDIM + (fc + 1) * 128],
                          xz[:, k * 517 + 512:k * 517 + 517], start=(k == 0),
                          stop=(k == 7))
                em.ts(ca, pc[:, 0:514],
                      convws[:, 4 * fc:4 * fc + 1], None, ALU.mult)
                for j in range(1, 4):
                    em.stt(ca, pc[:, j:514 + j],
                           convws[:, 4 * fc + j:4 * fc + j + 1],
                           ca, ALU.mult, ALU.add)
                em.ts(cv, ca, convbs[:, fc:fc + 1], None, ALU.add)
                em.act(ca, ca, AF.Tanh, scale=0.5, bias=convbh[:, fc:fc + 1])
                em.ts(ca, ca, 0.5, 0.5, ALU.mult, ALU.add)
                em.tt(co[:, fc * 514:(fc + 1) * 514], cv, ca, ALU.mult)
            if g == 0:
                em.dma("act", xogbuf[0:1, :].rearrange("t (c p) -> p c t",
                                                       p=128),
                       co[:, :].rearrange("p (c t) -> p c t",
                                          c=20)[:, 0:16, 0:1],
                       tag="xogpad0", slow=True)
            if g == 7:
                em.dma("act", xogbuf[T + 1:T + 2, :].rearrange(
                           "t (c p) -> p c t", p=128),
                       co[:, :].rearrange("p (c t) -> p c t",
                                          c=20)[:, 0:16, 513:514],
                       tag="xogpad1", slow=True)
            em.dma("act",
                   xogfm[:, 512 * g:512 * (g + 1)].rearrange(
                       "(c p) t -> p c t", p=128),
                   bass.AP(co[:, :].tensor, 1,
                           [list(co[:, :].ap[0]), [514, 20], [1, 512]]),
                   tag=f"xfmst{g}")
            for k in range(16):
                em.mm(pA[0:32, 0:512], fcdb[:, k * H:(k + 1) * H],
                      co[:, k * 514 + 1:k * 514 + 513], start=(k == 0),
                      stop=(k == 15))
            em.copy(fd_fm[:, 512 * g:512 * (g + 1)], pA[0:32, 0:512])

        qctr = {0: 0, 1: 0}

        def ssd_stages(c, fwd):
            r0, r1 = (0, 32) if fwd else (32, 64)
            tri = tri_fs if fwd else tri_bs
            mask = maskf_s if fwd else maskb_s
            fcB, fcC = (16, 17) if fwd else (18, 19)
            S = S_f if fwd else S_b
            oc = 0 if fwd else 1
            d = 0 if fwd else 1
            dircol = 0 if fwd else 32
            xgf, bfm, cfm = xgf2[d], bfm2[d], cfm2[d]
            elocT = elocT2[d]
            lhsD = lhsD2[d]
            dts32, ddb32 = dts322[d], ddb322[d]
            acs_c, acsn, acsH = acs_c2[d], acsn2[d], acsH2[d]
            acsM, acsL, acsr = acsM2[d], acsL2[d], acsr2[d]
            lndt32, xsp = lndt322[d], xsp2[d]
            dtH, dtM, dtL = dtH2[d], dtM2[d], dtL2[d]
            e_loc, dd_fm, dtT_c = e_loc2[d], dd_fm2[d], dtT_c2[d]
            Dm_sb, U_sb, xh_sb = Dm_sb2[d], U_sb2[d], xh_sb2[d]
            xt2_sb, btm_sb = xt2_sb2[d], btm_sb2[d]
            yoff_sb, y_sb = yoff_sb2[d], y_sb2[d]
            pAd = lambda lo, hi: pA[:, d * 1024 + lo:d * 1024 + hi]

            def nextq():
                i = qctr[d]
                qctr[d] += 1
                return d * 1024 + (i % 2) * 512
            pBd = lambda lo, hi: pB[:, d * 512 + lo:d * 512 + hi]
            pTd = lambda lo, hi: pT[:, d * 1024 + lo:d * 1024 + hi]
            rr0 = 96 if fwd else 99

            def H1():
                em.dma("sp", xgf[:, :].rearrange("p (c t) -> p c t", c=16),
                       xogfm[:, c * 128:(c + 1) * 128].rearrange(
                           "(c p) t -> p c t", p=128)[:, 0:16, :],
                       tag=f"xgl{c}{d}")
                em.dma("sp", bfm[:, :], xogfm[fcB * 128:(fcB + 1) * 128,
                                              c * 128:(c + 1) * 128],
                       tag=f"bl{c}{d}")
                em.dma("sp", cfm[:, :], xogfm[fcC * 128:(fcC + 1) * 128,
                                              c * 128:(c + 1) * 128],
                       tag=f"cl{c}{d}")
                em.dma("sp", adtc2[d][:, :],
                       adtbuf[r0:r1, c * 128:(c + 1) * 128], tag=f"adl{c}{d}")
                em.transpose(pBd(0, 32), adtc2[d][:, :], identf[0:32, 0:32])
                em.copy(dtT_c[:, :], pBd(0, 32))
                em.mm(pBd(32, 160)[0:32, :], dtT_c[:, :], tri[:, :],
                      start=True, stop=True)
                em.copy(acs_c[:, :], pBd(32, 160)[0:32, :])
                last = 127 if fwd else 0
                em.copy(aclast[:, oc:oc + 1], acs_c[:, last:last + 1])
                em.act(e_loc[:, :], acs_c[:, :], AF.Exp)
                em.act(dd_fm[:, :], acs_c[:, :], AF.Exp, scale=-1.0,
                       bias=aclast[:, oc:oc + 1])
                em.act(cdc[:, oc:oc + 1], aclast[:, oc:oc + 1], AF.Exp)
                em.act(acsn[:, :], acs_c[:, :], AF.Exp, bias=offc[:, oc:oc + 1])
                em.transpose(pBd(160, 192), acsn[:, :], identf[0:32, 0:32])
                em.copy(eact[:, c * 64 + dircol: c * 64 + dircol + 32],
                        pBd(160, 192))
                em.tt(offc[:, oc:oc + 1], offc[:, oc:oc + 1],
                      aclast[:, oc:oc + 1], ALU.add)
                em.transpose(pBd(192, 224), e_loc[:, :], identf[0:32, 0:32])
                em.copy(elocT[:, :], pBd(192, 224))
                em.dma("sp", dts32[:, :],
                       dtfmbuf[r0:r1, c * 128:(c + 1) * 128],
                       tag=f"dtl{c}{d}")
                em.dma("sp", lndt32[:, :],
                       dtlnF[r0:r1, c * 128:(c + 1) * 128],
                       tag=f"lnl{c}{d}")
                em.tt(dd_fm[:, :], dd_fm[:, :], dts32[:, :], ALU.mult)
                em.copy(ddb32[:, :], dd_fm[:, :])
                em.ts(acsn[:, :], acs_c[:, :], -1.0, None, ALU.mult)
                em.copy(acsH[:, :], acsn[:, :])
                em.tt(acsr[:, :], acsn[:, :], acsH[:, :], ALU.subtract)
                em.copy(acsM[:, :], acsr[:, :])
                em.tt(acsr[:, :], acsr[:, :], acsM[:, :], ALU.subtract)
                em.copy(acsL[:, :], acsr[:, :])
                em.tt(xsp[:, :], acsn[:, :], lndt32[:, :], ALU.add)
                em.copy(dtH[:, :], xsp[:, :])
                em.tt(xsp[:, :], xsp[:, :], dtH[:, :], ALU.subtract)
                em.copy(dtM[:, :], xsp[:, :])
                em.tt(xsp[:, :], xsp[:, :], dtM[:, :], ALU.subtract)
                em.copy(dtL[:, :], xsp[:, :])
                em.copy(lhsD[0:32, :], dtH[:, :])
                em.copy(lhsD[32:64, :], dtM[:, :])
                em.copy(lhsD[64:96, :], dtL[:, :])
                em.dma("pool", rhsD[rr0:rr0 + 1, :],
                       flscr[3 * d:3 * d + 1, :], tag="dum") if False else None
                for ri, asrc in ((0, acsH), (1, acsM), (2, acsL)):
                    em.dma("pool", flscr[3 * d + ri:3 * d + ri + 1, :]
                           .rearrange("a (b c) -> (a b) c", b=32)
                           if False else
                           flscr[3 * d + ri:3 * d + ri + 1, :].rearrange(
                               "a (b c) -> (a b) c", b=32), asrc[:, :],
                           tag=f"flA{d}{ri}")
                    em.dma("pool", rhsD[rr0 + ri:rr0 + ri + 1, :],
                           flscr[3 * d + ri:3 * d + ri + 1, :],
                           tag=f"flB{d}{ri}")

            def H2():
                for q in range(8):
                    b = nextq()
                    em.mm(pA[:, b:b + 512], lhsD[0:102, :],
                          rhsD[0:102, q * 512:(q + 1) * 512],
                          start=True, stop=True)
                    em.ts(Dm_sb[:, q * 512:(q + 1) * 512],
                          pA[:, b:b + 512], 30.0, None, ALU.min)
                em.act(Dm_sb[:, :], Dm_sb[:, :], AF.Exp)
                em.mm(pBd(224, 352), bfm[:, :], cfm[:, :], start=True,
                      stop=True)
                em.tt(U_sb[:, 0:128], pBd(224, 352), mask[:, :], ALU.mult)
                em.copy(U_sb[:, 128:256], U_sb[:, 0:128])

            def M1():
                for w in range(2):
                    for k in range(8):
                        em.transpose(pTd(k * 128, (k + 1) * 128),
                                     xgf[:, (w * 8 + k) * 128:
                                         (w * 8 + k + 1) * 128], identb[:, :])
                    em.copy(xh_sb[:, w * 1024:(w + 1) * 1024], pTd(0, 1024))
                if fwd:
                    em.dma("act", xogbuf[c * 128 + 1:c * 128 + 129, :],
                           xh_sb[:, :], tag=f"xogst{c}")
                for q in range(4):
                    b = nextq()
                    em.mm(pA[:, b:b + 512], ddb32[:, :],
                          din_indhp[0:32, q * 512:(q + 1) * 512],
                          start=True, stop=True)
                    em.tt(xt2_sb[:, q * 512:(q + 1) * 512],
                          xh_sb[:, q * 512:(q + 1) * 512], pA[:, b:b + 512],
                          ALU.mult)
                em.transpose(pTd(0, 128), bfm[:, :], identb[:, :])
                em.copy(btm_sb[:, :], pTd(0, 128))
                for h2 in range(16):
                    em.tt(Dm_sb[:, h2 * 256:(h2 + 1) * 256],
                          Dm_sb[:, h2 * 256:(h2 + 1) * 256], U_sb[:, :],
                          ALU.mult)

            def T1():
                em.copy(sbf[:, :], S[:, :])
                ybuf = ylf if fwd else ylb
                for q in range(4):
                    b = nextq()
                    em.mm(pA[:, b:b + 512], cfm[:, :],
                          sbf[:, q * 512:(q + 1) * 512],
                          start=True, stop=True)
                    for h in range(8):
                        hh = q * 8 + h
                        em.act(yoff_sb[:, hh * 64:(hh + 1) * 64],
                               pA[:, b + h * 64:b + (h + 1) * 64], AF.Copy,
                               scale=elocT[:, hh:hh + 1])
                    b2 = nextq()
                    for h in range(8):
                        hh = q * 8 + h
                        em.mm(pA[:, b2 + h * 64:b2 + (h + 1) * 64],
                              Dm_sb[:, hh * 128:(hh + 1) * 128],
                              xh_sb[:, hh * 64:(hh + 1) * 64], start=True,
                              stop=True)
                    em.tt(y_sb[:, q * 512:(q + 1) * 512],
                          pA[:, b2:b2 + 512],
                          yoff_sb[:, q * 512:(q + 1) * 512], ALU.add)
                em.dma("act", ybuf[c * 128 + 1:c * 128 + 129, :], y_sb[:, :],
                       tag=f"yst{c}{d}")

            def T2():
                em.ts(cdmat[:, :], ones32[:, :], cdc[:, oc:oc + 1], None,
                      ALU.mult)
                for q in range(4):
                    b = nextq()
                    em.mm(pA[:, b:b + 512], cdmat[:, :],
                          din_indhp[0:32, q * 512:(q + 1) * 512],
                          start=True, stop=True)
                    em.tt(S[:, q * 512:(q + 1) * 512],
                          S[:, q * 512:(q + 1) * 512], pA[:, b:b + 512],
                          ALU.mult)
                    b2 = nextq()
                    em.mm(pA[:, b2:b2 + 512], btm_sb[:, :],
                          xt2_sb[:, q * 512:(q + 1) * 512],
                          start=True, stop=True)
                    em.tt(S[:, q * 512:(q + 1) * 512], pA[:, b2:b2 + 512],
                          S[:, q * 512:(q + 1) * 512], ALU.add)

            return [H1, H2, M1, T1, T2]

        conv_order = [0, 7, 1, 6, 2, 5, 3, 4]
        conv_group(conv_order[0], 0)
        conv_group(conv_order[1], 1)
        convs_done = 2
        units = []
        for ci in range(NCH):
            units.append((ci, True))
            units.append((NCH - 1 - ci, False))
        stv = [ssd_stages(c, f) for (c, f) in units]
        stv[0][0]()
        prev = None
        for i in range(len(stv)):
            if i in (4, 12, 20) and convs_done < 8:
                conv_group(conv_order[convs_done], convs_done % 2)
                conv_group(conv_order[convs_done + 1],
                           (convs_done + 1) % 2)
                convs_done += 2
            if prev is not None:
                stv[prev][3]()
                stv[prev][4]()
            stv[i][1]()
            stv[i][2]()
            if i + 1 < len(stv):
                stv[i + 1][0]()
            prev = i
        stv[prev][3]()
        stv[prev][4]()

        cdt2 = SB(stk, "cdt2", [32, 2])
        em.memset(convacc[:, :], 0.0)
        em.dma("sp", ccin[259:264, :].rearrange("r (a c) -> (r a) c", a=4),
               convacc[0:20, 0:512], tag="ccz1")
        em.dma("sp", ccin[256:257, 384:2048].rearrange(
                   "r (a c) -> (r a) c", a=4), convacc[20:24, 0:416],
               tag="ccz2")
        em.act(cdt2[:, :], offc[:, :], AF.Exp)
        em.dma("sp", ccin[0:128, :], S_f[:, :], tag="ccsf")
        em.dma("sp", ccin[128:256, :], S_b[:, :], tag="ccsb")
        em.dma("sp", ccin[256:257, 0:32].rearrange("a (b c) -> (a b) c", b=32),
               cdt2[:, 0:1], tag="cccdf", slow=True)
        em.dma("sp", ccin[256:257, 32:64].rearrange("a (b c) -> (a b) c", b=32),
               cdt2[:, 1:2], tag="cccdb", slow=True)
        em.dma("pool", ccin[256:257, 64:96], eact[127:128, 31 * 64:31 * 64 + 32],
               tag="cccf")
        em.dma("pool", ccin[256:257, 96:128], eact[0:1, 32:64], tag="cccb")
        em.dma("pool", ccin[256:257, 128:256].rearrange(
                   "a (b c) -> (a b) c", b=128),
               xogfm[17 * 128:18 * 128, 4095:4096], tag="ccCf", slow=True)
        em.dma("pool", ccin[256:257, 256:384].rearrange(
                   "a (b c) -> (a b) c", b=128),
               xogfm[19 * 128:20 * 128, 0:1], tag="ccCb", slow=True)
        em.dma("pool", ccin[257:258, :], ylf[T:T + 1, :], tag="ccyf")
        em.dma("pool", ccin[258:259, :], ylb[1:2, :], tag="ccyb")
        if DBG:
            em.dma("sp", outs["dbg_dt"], dtfmbuf[:, :], tag="dbgdt")
            em.dma("sp", outs["dbg_sf"], S_f[:, :], tag="dbgsf")
            em.dma("sp", outs["dbg_sb"], S_b[:, :], tag="dbgsb")
            em.dma("sp", outs["dbg_fd"], fd_fm[:, :], tag="dbgfd")
            em.dma("sp", outs["dbg_eact"], eact[:, :], tag="dbgea")


    def p3x(stk, eact):
        onesf = SB(stk, "onesf", [1, 128])
        sgt = SB(stk, "sgt", [128, DI])
        kf = [SB(stk, f"kf{j}", [128, DI]) for j in (1, 2, 3)]
        kb = [SB(stk, f"kb{j}", [128, DI]) for j in (0, 1, 2)]
        carryb16 = {n: SB(stk, "cb16_" + n, [128, DI], BF16)
                    for n in ("f", "fp", "b", "bn")}
        cdflat2 = SB(stk, "cdflat2", [1, 32])
        cdbc2 = SB(stk, "cdbc2", [128, 32])
        selo = SB(stk, "selo", [128, 4])
        selp = SB(stk, "selp", [128, 4])
        seln = SB(stk, "seln", [128, 4])
        ytile = SB(stk, "ytile", [1, 4 * DI], BF16)
        yacc = SB(stk, "yacc", [1, DI])
        cmat = SB(stk, "cmat", [128, 4])
        cw = SB(stk, "cw", [128, 4])
        csel = SB(stk, "csel", [128, 1], BF16)
        cself = SB(stk, "cself", [128, 1])
        cfl = SB(stk, "cfl", [1, 4 * 32])
        cflacc = SB(stk, "cflacc", [1, 32])
        padrow = SB(stk, "padrow", [1, DI])
        ylt4 = [[SB(stk, f"ylt{x}{d}", [128, DI], BF16) for d in range(2)]
                for x in range(2)]
        yout4 = [[SB(stk, f"yout{x}{d}", [128, DI], BF16) for d in range(2)]
                 for x in range(2)]
        cchk2 = [SB(stk, f"cchk{d}", [128, 128], BF16) for d in range(2)]
        gsca2 = [SB(stk, f"gsca{d}", [128, 1024], BF16) for d in range(2)]
        coefT2 = [SB(stk, f"coefT{d}", [128, 32]) for d in range(2)]
        wzr = SB(stk, "wzr", [128, 8 * DI], BF16)
        zsl = [SB(stk, f"zsl{x}", [128, DI], BF16) for x in range(2)]
        xzt2 = [SB(stk, f"xzt{x}", [128, 8 * 128], BF16) for x in range(2)]
        pA = PS(stk, "pA3", [128, 2048])
        pB = PS(stk, "pB3", [128, 512])
        pAz = PS(stk, "pAz", [128, 1024])
        em.memset(onesf[:, :], 1.0)
        em.dma("sp", selo[:, :], din["selobc"])
        em.dma("sp", selp[:, :], din["selpbc"])
        em.dma("sp", seln[:, :], din["selnbc"])
        em.collective("AllGather", ALU.bypass, [[0, 1, 2, 3], [4, 5, 6, 7]],
                      [ccin], [ccout], tag="ag")
        for k in range(8):
            em.dma("pool", wzr[:, k * DI:(k + 1) * DI],
                   din["ipwT"][k * 128:(k + 1) * 128, 0:DI], tag="wz")

        def z_sub(i):
            p = i % 2
            em.dma("sp", xzt2[p][:, :].rearrange("p (k c) -> p k c", k=8),
                   xln2fm[:, 4 + i * 128:4 + (i + 1) * 128].rearrange(
                       "(k p) c -> p k c", p=128), tag=f"zx{i}")
            for q in range(4):
                ph = pAz[:, (q % 2) * 512:(q % 2) * 512 + 512]
                for k in range(8):
                    em.mm(ph, xzt2[p][:, k * 128:(k + 1) * 128],
                          wzr[:, k * DI + q * 512:k * DI + (q + 1) * 512],
                          start=(k == 0), stop=(k == 7))
                em.act(zsl[p][:, q * 512:(q + 1) * 512], ph, AF.Silu)
            em.dma("act", zbuf[i * 128:(i + 1) * 128, :], zsl[p][:, :],
                   tag=f"zst{i}")

        for zi in range(12):
            z_sub(zi)

        def cdrep_load(j, col0):
            em.dma("sp", cdflat2[:, :], ccout[j * 264 + 256:j * 264 + 257,
                                              col0:col0 + 32], tag="cdl")
            em.mm(pB[:, 0:32], onesf[:, :], cdflat2[:, :], start=True,
                  stop=True)
            em.copy(cdbc2[:, :], pB[:, 0:32])

        def chain_step(dst, src_k, sg_row, j, col0):
            em.dma("sp", sgt[:, :], ccout[sg_row:sg_row + 128, :], tag="sgl")
            cdrep_load(j, col0)
            for h in range(32):
                em.ts(dst[:, h * 64:(h + 1) * 64],
                      src_k[:, h * 64:(h + 1) * 64],
                      cdbc2[:, h:h + 1], None, ALU.mult)
            em.tt(dst[:, :], dst[:, :], sgt[:, :], ALU.add)

        em.dma("sp", kf[0][:, :], ccout[0 * 264:0 * 264 + 128, :], tag="sg0")
        chain_step(kf[1], kf[0], 1 * 264, 1, 0)
        chain_step(kf[2], kf[1], 2 * 264, 2, 0)
        em.dma("sp", kb[2][:, :], ccout[3 * 264 + 128:3 * 264 + 256, :],
               tag="sg3")
        chain_step(kb[1], kb[2], 2 * 264 + 128, 2, 32)
        chain_step(kb[0], kb[1], 1 * 264 + 128, 1, 32)

        def select3(dst, ks, selcol, idxs):
            em.ts(dst[:, :], ks[0][:, :], selcol[:, idxs[0]:idxs[0] + 1],
                  None, ALU.mult)
            for k, j in zip(ks[1:], idxs[1:]):
                em.stt(dst[:, :], k[:, :], selcol[:, j:j + 1], dst[:, :],
                       ALU.mult, ALU.add)

        select3(carryb16["f"], kf, selo, [1, 2, 3])
        select3(carryb16["fp"], kf, selp, [1, 2, 3])
        select3(carryb16["b"], kb, selo, [0, 1, 2])
        select3(carryb16["bn"], kb, seln, [0, 1, 2])

        def pad_assemble(ybufrow, yrow_base, cf_col0, cc_col0, selt, cname,
                         tag):
            for j in range(4):
                em.dma("pool", ytile[:, j * DI:(j + 1) * DI],
                       ccout[j * 264 + yrow_base:j * 264 + yrow_base + 1, :],
                       tag=tag + "y")
                em.dma("sp", cfl[:, j * 32:(j + 1) * 32],
                       ccout[j * 264 + 256:j * 264 + 257,
                             cf_col0:cf_col0 + 32], tag=tag + "c")
                em.dma("sp", cmat[:, j:j + 1],
                       ccout[j * 264 + 256:j * 264 + 257,
                             cc_col0:cc_col0 + 128].rearrange(
                                 "a (b c) -> (a b) c", b=128), tag=tag + "C",
                       slow=True)
            em.ts(yacc[:, :], ytile[:, 0:DI], selt[0:1, 0:1], None, ALU.mult)
            for j in range(1, 4):
                em.stt(yacc[:, :], ytile[:, j * DI:(j + 1) * DI],
                       selt[0:1, j:j + 1], yacc[:, :], ALU.mult, ALU.add)
            em.ts(cflacc[:, :], cfl[:, 0:32], selt[0:1, 0:1], None, ALU.mult)
            for j in range(1, 4):
                em.stt(cflacc[:, :], cfl[:, j * 32:(j + 1) * 32],
                       selt[0:1, j:j + 1], cflacc[:, :], ALU.mult, ALU.add)
            em.tt(cw[:, :], cmat[:, :], selt[:, :], ALU.mult)
            em.reduce(cself[:, :], cw[:, :], ALU.add)
            em.copy(csel[:, :], cself[:, :])
            for j in range(4):
                em.mm(pA[0:1, j * 512:(j + 1) * 512], csel[:, :],
                      carryb16[cname][:, j * 512:(j + 1) * 512],
                      start=True, stop=True)
            for h in range(32):
                em.stt(padrow[:, h * 64:(h + 1) * 64],
                       pA[0:1, h * 64:(h + 1) * 64],
                       cflacc[0:1, h:h + 1], yacc[:, h * 64:(h + 1) * 64],
                       ALU.mult, ALU.add)
            em.dma("pool", ybufrow, padrow[:, :], tag=tag + "st")

        pad_assemble(ylf[0:1, :], 257, 64, 128, selp, "fp", "padf")
        pad_assemble(ylb[T + 1:T + 2, :], 258, 96, 256, seln, "bn", "padb")

        def p4a_stage(i, fwd, st):
            d = 0 if fwd else 1
            ybuf = ylf if fwd else ylb
            fcC = 17 if fwd else 19
            cn = "f" if fwd else "b"
            dircol = 0 if fwd else 32
            pAd = pA[:, d * 1024:(d + 1) * 1024]
            if st == 0:
                em.dma("sp", cchk2[d][:, :],
                       xogfm[fcC * 128:(fcC + 1) * 128,
                             i * 128:(i + 1) * 128], tag=f"c4{i}{d}")
                em.dma("sp", ylt4[i % 2][d][:, :],
                       ybuf[i * 128 + 1:i * 128 + 129, :], tag=f"yl4{i}{d}")
                em.copy(coefT2[d][:, :], eact[:, i * 64 + dircol:
                                              i * 64 + dircol + 32])
            else:
                for w in range(2):
                    for j in range(2):
                        em.mm(pAd[:, j * 512:(j + 1) * 512], cchk2[d][:, :],
                              carryb16[cn][:, w * 1024 + j * 512:
                                           w * 1024 + (j + 1) * 512],
                              start=True, stop=True)
                    if w == 0:
                        for h in range(16):
                            em.stt(yout4[i % 2][d][:, h * 64:(h + 1) * 64],
                                   pAd[:, h * 64:(h + 1) * 64],
                                   coefT2[d][:, h:h + 1],
                                   ylt4[i % 2][d][:, h * 64:(h + 1) * 64],
                                   ALU.mult, ALU.add)
                    else:
                        for h in range(8):
                            hh = 16 + h
                            em.act(gsca2[d][:, h * 64:(h + 1) * 64],
                                   pAd[:, h * 64:(h + 1) * 64], AF.Copy,
                                   scale=coefT2[d][:, hh:hh + 1])
                        em.tt(yout4[i % 2][d][:, 1024:1536],
                              ylt4[i % 2][d][:, 1024:1536],
                              gsca2[d][:, 0:512], ALU.add)
                        for h in range(8, 16):
                            hh = 16 + h
                            em.stt(yout4[i % 2][d][:, hh * 64:(hh + 1) * 64],
                                   pAd[:, h * 64:(h + 1) * 64],
                                   coefT2[d][:, hh:hh + 1],
                                   ylt4[i % 2][d][:, hh * 64:(hh + 1) * 64],
                                   ALU.mult, ALU.add)
                em.dma("act", ybuf[i * 128 + 1:i * 128 + 129, :],
                       yout4[i % 2][d][:, :], tag=f"yw4{i}{d}")
        p4a_stage(0, True, 0)
        p4a_stage(0, False, 0)
        for i in range(NCH):
            p4a_stage(i, True, 1)
            if i + 1 < NCH:
                p4a_stage(i + 1, True, 0)
            p4a_stage(i, False, 1)
            if i + 1 < NCH:
                p4a_stage(i + 1, False, 0)
            if 12 + i < NCH:
                z_sub(12 + i)

    def p4m(stk, fd_fm):
        opwb = SB(stk, "opwb", [128, 16 * D], BF16)
        nwc = SB(stk, "nwc", [128, 16])
        drept = SB(stk, "drept", [128, DI], BF16)
        d32 = SB(stk, "d32", [128, 32])
        m2 = SB(stk, "m2", [128, 2])
        x1t = SB(stk, "x1t4", [128, NG * D])
        yft = SB(stk, "yft", [128, NG * DI], BF16)
        ybt = SB(stk, "ybt", [128, NG * DI], BF16)
        xm1 = SB(stk, "xm1", [128, NG * DI], BF16)
        xct = SB(stk, "xct", [128, NG * DI], BF16)
        xp1 = SB(stk, "xp1", [128, NG * DI], BF16)
        zst = SB(stk, "zst", [128, NG * DI], BF16)
        indhp4 = SB(stk, "indhp4", [32, DI], BF16)
        drow = SB(stk, "drow", [1, DI], BF16)
        ones1 = SB(stk, "ones1", [1, 128], BF16)
        fdd2 = [SB(stk, f"fdd{g}", [128, DI], BF16)
                for g in range(NG)]
        yga = SB(stk, "yga", [128, NG * DI], BF16)
        ygb2 = [SB(stk, f"ygb{g}", [128, DI], BF16)
                for g in range(NG)]
        ygbf = SB(stk, "ygbf", [128, NG * DI], BF16)
        sqc = SB(stk, "sqc4", [128, 8])
        st2 = SB(stk, "st24", [128, 8])
        st3 = SB(stk, "st34", [128, 8])
        rstd = SB(stk, "rstd4", [128, 8])
        sq_scr = SB(stk, "sq_scr4", [128, DI])
        ynfm = SB(stk, "ynfm", [128, 16 * 256], BF16)
        mixt = SB(stk, "mixt", [128, NG * D])
        x2t = SB(stk, "x2t", [128, NG * D])
        ptT = PS(stk, "ptT4", [128, 2048], BF16)
        pob = [PS(stk, f"po{i}_4", [128, 512]) for i in range(2)]
        pm4 = PS(stk, "pm4", [128, 2048])

        for k in range(16):
            em.dma("pool", opwb[:, k * D:(k + 1) * D],
                   din["opwT"][k * 128:(k + 1) * 128, :], tag="opw")
        em.dma("sp", nwc[:, :], din["normw"].rearrange("(c p) -> p c", p=128),
               slow=True)
        em.dma("pool", drept[:, :], din["drep"])
        em.dma("pool", drow[:, :], din["drep"][0:1, :])
        em.memset(ones1[:, :], 1.0)
        em.dma("sp", indhp4[:, :], din["indhp"])
        em.dma("sp", m2[:, :], din["mask2"])
        for k in range(16):
            em.ts(opwb[:, k * D:(k + 1) * D], opwb[:, k * D:(k + 1) * D],
                  nwc[:, k:k + 1], None, ALU.mult)

        def rms_pair(src, xout):
            for g in range(NG):
                gs = src[:, g * DI:(g + 1) * DI]
                em.act(sq_scr[:, :], gs, AF.Square, accum_out=sqc[:, g:g + 1])
            sl = slice(0, NG)
            em.ts(st2[:, sl], sqc[:, sl], 1.0 / DI, EPS, ALU.mult, ALU.add)
            em.rsqrt(rstd[:, sl], st2[:, sl], st3[:, sl])
            for g in range(NG):
                em.ts(xout[:, g * DI:(g + 1) * DI],
                      src[:, g * DI:(g + 1) * DI], rstd[:, g:g + 1], None,
                      ALU.mult)

        for pr in range(16):
            i0 = pr * 2
            em.dma("sp", x1t[:, :].rearrange("p (g d) -> p g d", g=NG),
                   x1buf[4 + i0 * 128:4 + (i0 + 2) * 128, :].rearrange(
                       "(g p) d -> p g d", p=128), tag=f"x1l{pr}")
            em.dma("sp", yft[:, :].rearrange("p (g d) -> p g d", g=NG),
                   ylf[i0 * 128:(i0 + 2) * 128, :].rearrange(
                       "(g p) d -> p g d", p=128), tag=f"yfl{pr}")
            em.dma("sp", ybt[:, :].rearrange("p (g d) -> p g d", g=NG),
                   ylb[i0 * 128 + 2:(i0 + 2) * 128 + 2, :].rearrange(
                       "(g p) d -> p g d", p=128), tag=f"ybl{pr}")
            em.dma("sp", xm1[:, :].rearrange("p (g d) -> p g d", g=NG),
                   xogbuf[i0 * 128:(i0 + 2) * 128, :].rearrange(
                       "(g p) d -> p g d", p=128), tag=f"xml{pr}")
            em.dma("sp", xct[:, :].rearrange("p (g d) -> p g d", g=NG),
                   xogbuf[i0 * 128 + 1:(i0 + 2) * 128 + 1, :].rearrange(
                       "(g p) d -> p g d", p=128), tag=f"xcl{pr}")
            em.dma("sp", xp1[:, :].rearrange("p (g d) -> p g d", g=NG),
                   xogbuf[i0 * 128 + 2:(i0 + 2) * 128 + 2, :].rearrange(
                       "(g p) d -> p g d", p=128), tag=f"xpl{pr}")
            em.dma("sp", zst[:, :].rearrange("p (g d) -> p g d", g=NG),
                   zbuf[i0 * 128:(i0 + 2) * 128, :].rearrange(
                       "(g p) d -> p g d", p=128), tag=f"zl{pr}")
            if pr == 0:
                em.ts(xm1[:, 0:DI], xm1[:, 0:DI], m2[:, 0:1], None, ALU.mult)
            if pr == 15:
                em.ts(xp1[:, DI:2 * DI], xp1[:, DI:2 * DI], m2[:, 1:2], None,
                      ALU.mult)
            for g in range(NG):
                i = i0 + g
                ys = slice(g * DI, (g + 1) * DI)
                ygb = ygb2[g]
                fdd = fdd2[g]
                for w in range(2):
                    ph = pm4[:, (g % 2) * 1024:(g % 2) * 1024 + 1024]
                    for j in range(2):
                        jj = w * 2 + j
                        em.mm(ph[:, j * 512:(j + 1) * 512],
                              fd_fm[:, i * 128:(i + 1) * 128],
                              indhp4[0:32, jj * 512:(jj + 1) * 512],
                              start=True, stop=False)
                        em.mm(ph[:, j * 512:(j + 1) * 512], ones1[:, :],
                              drow[:, jj * 512:(jj + 1) * 512], start=False,
                              stop=True)
                    em.act(fdd[:, w * 1024:(w + 1) * 1024], ph[:, :],
                           AF.Copy)
                em.tt(yga[:, ys], yft[:, ys], ybt[:, ys], ALU.add)
                em.tt(ygb[:, :], xm1[:, ys], xp1[:, ys], ALU.add)
                em.tt(ygb[:, :], ygb[:, :], drept[:, :], ALU.mult)
                em.tt(yga[:, ys], yga[:, ys], ygb[:, :], ALU.add)
                em.tt(ygb[:, :], fdd[:, :], xct[:, ys], ALU.mult)
                em.tt(yga[:, ys], yga[:, ys], ygb[:, :], ALU.add)
                em.tt(yga[:, ys], yga[:, ys], zst[:, ys], ALU.mult)
            rms_pair(yga, ygbf)
            for g in range(NG):
                for k in range(16):
                    em.transpose(ptT[:, k * 128:(k + 1) * 128],
                                 ygbf[:, g * DI + k * 128:
                                      g * DI + (k + 1) * 128], identb[:, :])
                em.copy(ynfm[:, :].rearrange("p (k r) -> p k r", k=16)
                        [:, :, g * 128:(g + 1) * 128],
                        ptT[:, :].rearrange("p (k r) -> p k r", k=16))
            for do in range(8):
                bank = pob[do % 2]
                for k in range(16):
                    em.mm(bank[:, 0:256],
                          opwb[:, k * D + do * 128:k * D + (do + 1) * 128],
                          ynfm[:, k * 256:(k + 1) * 256],
                          start=(k == 0), stop=(k == 15))
                em.copy(mixt[:, :].rearrange("p (g e) -> p g e", g=NG)
                        [:, :, do * 128:(do + 1) * 128],
                        bank[:, 0:256].rearrange("p (g c) -> p g c", g=NG))
            for g in range(NG):
                for w in range(2):
                    for dd in range(4):
                        em.transpose(
                            pob[w % 2][:, dd * 128:(dd + 1) * 128],
                            mixt[:, g * D + (w * 4 + dd) * 128:
                                 g * D + (w * 4 + dd + 1) * 128],
                            identf[:, :])
                    for dd in range(4):
                        em.copy(x2t[:, g * D + (w * 4 + dd) * 128:
                                    g * D + (w * 4 + dd + 1) * 128],
                                pob[w % 2][:, dd * 128:(dd + 1) * 128])
            for g in range(NG):
                em.tt(x2t[:, g * D:(g + 1) * D], x2t[:, g * D:(g + 1) * D],
                      x1t[:, g * D:(g + 1) * D], ALU.add)
            em.dma("act",
                   x2buf[i0 * 128:(i0 + 2) * 128, :].rearrange(
                       "(g p) d -> p g d", p=128),
                   x2t[:, :].rearrange("p (g d) -> p g d", g=NG),
                   tag=f"x2st{pr}")

    def p4f(stk):
        w1b = SB(stk, "w1b4", [128, 8 * DFF], BF16)
        w2b = SB(stk, "w2b4", [128, 32 * D], BF16)
        b1c = SB(stk, "b1c4", [128, 32])
        b2c = SB(stk, "b2c4", [128, 8])
        thetac = SB(stk, "thetac4", [128, 32])
        thetab = SB(stk, "thetab4", [128, 32])
        coefc = SB(stk, "coefc4", [128, 32])
        ut = SB(stk, "ut4", [128, NG * D])
        sumc = SB(stk, "sumc4", [128, 8])
        sqc = SB(stk, "sqc4f", [128, 8])
        st1 = SB(stk, "st14", [128, 8])
        st2 = SB(stk, "st24f", [128, 8])
        st3 = SB(stk, "st34f", [128, 8])
        rstd = SB(stk, "rstd4f", [128, 8])
        nmr = SB(stk, "nmr4", [128, 8])
        sq_scr = SB(stk, "sq_scr4f", [128, D])
        xlnt = SB(stk, "xlnt4", [128, NG * D], BF16)
        xfm = SB(stk, "xfm4", [128, 8 * TILE], BF16)
        aa = SB(stk, "aa4", [128, TILE], BF16)
        kfl = SB(stk, "kfl4", [128, TILE], BF16)
        sargt = SB(stk, "sargt4", [128, TILE], BF16)
        stile = SB(stk, "stile4", [128, TILE])
        qt = SB(stk, "qt4", [128, TILE], BF16)
        hact = SB(stk, "hact4", [128, 32 * TILE], BF16)
        o2t = [SB(stk, f"o2t{i}_4", [128, TILE], BF16) for i in range(2)]
        outt = SB(stk, "outt", [128, NG * D])
        phs = SB(stk, "phs4", [128, 32])
        wrs = SB(stk, "wrs4", [128, 32])
        wis = SB(stk, "wis4", [128, 32])
        t32a = SB(stk, "t32a4", [128, 32])
        t32b = SB(stk, "t32b4", [128, 32])
        ptT = PS(stk, "ptT4f", [128, 2048], BF16)
        pfb = [PS(stk, f"pf{i}_4", [128, 512]) for i in range(3)]
        pob = [PS(stk, f"po{i}_4f", [128, 512]) for i in range(2)]

        for k in range(8):
            em.dma("pool", w1b[:, k * DFF:(k + 1) * DFF],
                   din["w1"][k * 128:(k + 1) * 128, :], tag="w1r")
        for k in range(8):
            em.dma("pool",
                   w2b[:, k * 4 * D:(k + 1) * 4 * D].rearrange(
                       "p (c n) -> p c n", c=4),
                   din["w2"].rearrange("(c p) n -> p c n",
                                       p=128)[:, 4 * k:4 * k + 4, :],
                   tag="w2r")
        em.dma("sp", b1c[:, :], din["b1"].rearrange("(c p) -> p c", p=128),
               slow=True)
        em.dma("sp", b2c[:, :], din["b2"].rearrange("(c p) -> p c", p=128),
               slow=True)
        em.dma("sp", phs[:, :], din["phases"].rearrange("(c p) -> p c", p=128),
               slow=True)
        em.dma("sp", wrs[:, :], din["wreal"].rearrange("(c p) -> p c", p=128),
               slow=True)
        em.dma("sp", wis[:, :], din["wimag"].rearrange("(c p) -> p c", p=128),
               slow=True)
        em.copy(thetac[:, :], phs[:, :])
        for _ in range(6):
            em.wrap1(thetac[:, :], thetac[:, :], t32a[:, :], t32b[:, :])
        em.act(t32a[:, :], thetac[:, :], AF.Sin)
        em.act(t32b[:, :], thetac[:, :], AF.Sin, scale=0.5)
        em.tt(t32b[:, :], t32b[:, :], t32b[:, :], ALU.mult)
        em.ts(t32b[:, :], t32b[:, :], -2.0, 1.0, ALU.mult, ALU.add)
        em.tt(t32a[:, :], t32a[:, :], wis[:, :], ALU.mult)
        em.tt(t32b[:, :], t32b[:, :], wrs[:, :], ALU.mult)
        em.tt(t32a[:, :], t32a[:, :], t32b[:, :], ALU.add)
        em.tt(t32a[:, :], t32a[:, :], wis[:, :], ALU.mult)
        em.ts(coefc[:, :], t32a[:, :], 0.1, None, ALU.mult)
        em.stt(thetab[:, :], b1c[:, :], 0.1, thetac[:, :], ALU.mult, ALU.add)

        for t in range(16):
            R0 = t * TILE
            em.dma("sp", ut[:, :].rearrange("p (g d) -> p g d", g=NG),
                   x2buf[R0:R0 + TILE, :].rearrange("(g p) d -> p g d",
                                                    p=128), tag=f"p4u{t}")
            for g in range(NG):
                gs = ut[:, g * D:(g + 1) * D]
                em.act(sq_scr[:, :], gs, AF.Square,
                       accum_out=sqc[:, g:g + 1])
                em.reduce(sumc[:, g:g + 1], gs, ALU.add)
            sl = slice(0, NG)
            em.ts(st1[:, sl], sumc[:, sl], 1.0 / D, None, ALU.mult)
            em.ts(st2[:, sl], sqc[:, sl], 1.0 / D, None, ALU.mult)
            em.tt(st3[:, sl], st1[:, sl], st1[:, sl], ALU.mult)
            em.tt(st2[:, sl], st2[:, sl], st3[:, sl], ALU.subtract)
            em.ts(st2[:, sl], st2[:, sl], EPS, None, ALU.add)
            em.rsqrt(rstd[:, sl], st2[:, sl], st3[:, sl])
            em.tt(nmr[:, sl], st1[:, sl], rstd[:, sl], ALU.mult)
            em.ts(nmr[:, sl], nmr[:, sl], -1.0, None, ALU.mult)
            for g in range(NG):
                em.ts(xlnt[:, g * D:(g + 1) * D], ut[:, g * D:(g + 1) * D],
                      rstd[:, g:g + 1], nmr[:, g:g + 1], ALU.mult, ALU.add)
            for k in range(8):
                for g in range(NG):
                    em.transpose(
                        ptT[:, (k * NG + g) * 128:(k * NG + g + 1) * 128],
                        xlnt[:, g * D + k * 128: g * D + (k + 1) * 128],
                        identb[:, :])
            dap = xfm[:, :]
            sap = ptT[:, :]
            em.copy(bass.AP(dap.tensor, 0, [list(dap.ap[0]), [TILE, 8],
                                            [128, NG], [1, 128]]),
                    bass.AP(sap.tensor, 0, [list(sap.ap[0]), [NG * 128, 8],
                                            [128, NG], [1, 128]]))
            for f in range(32):
                bank = pfb[f % 3]
                for k in range(8):
                    em.mm(bank[:, 0:TILE],
                          w1b[:, k * DFF + f * 128: k * DFF + (f + 1) * 128],
                          xfm[:, k * TILE:(k + 1) * TILE],
                          start=(k == 0), stop=(k == 7))
                em.ts(aa[:, :], bank[:, 0:TILE], 0.1, thetab[:, f:f + 1],
                      ALU.mult, ALU.add)
                em.wrap1(sargt[:, :], aa[:, :], kfl[:, :], qt[:, :])
                em.act(stile[:, :], sargt[:, :], AF.Sin)
                em.stt(qt[:, :], stile[:, :], coefc[:, f:f + 1],
                       bank[:, 0:TILE], ALU.mult, ALU.add)
                em.act(hact[:, f * TILE:(f + 1) * TILE], qt[:, :], AF.Relu,
                       bias=b1c[:, f:f + 1])
            for do in range(8):
                bank = pob[do % 2]
                for kf2 in range(32):
                    em.mm(bank[:, 0:TILE],
                          w2b[:, kf2 * D + do * 128: kf2 * D + (do + 1) * 128],
                          hact[:, kf2 * TILE:(kf2 + 1) * TILE],
                          start=(kf2 == 0), stop=(kf2 == 31))
                em.ts(o2t[do % 2][:, :], bank[:, 0:TILE], b2c[:, do:do + 1],
                      None, ALU.add)
                for g in range(NG):
                    em.transpose(ptT[:, (do % 2) * 1024 + g * 128:
                                     (do % 2) * 1024 + (g + 1) * 128],
                                 o2t[do % 2][:, g * 128:(g + 1) * 128],
                                 identb[:, :])
                for g in range(NG):
                    em.stt(outt[:, g * D + do * 128: g * D + (do + 1) * 128],
                           ptT[:, (do % 2) * 1024 + g * 128:
                               (do % 2) * 1024 + g * 128 + 128],
                           0.5,
                           ut[:, g * D + do * 128: g * D + (do + 1) * 128],
                           ALU.mult, ALU.add)
            em.dma("act",
                   out[R0:R0 + TILE, :].rearrange("(g p) d -> p g d", p=128),
                   outt[:, :].rearrange("p (g d) -> p g d", g=NG),
                   tag=f"outst{t}")

    midstack = contextlib.ExitStack()
    fd_fm = SB(midstack, "fd_fm", [32, T], BF16)
    eact = SB(midstack, "eact", [128, NCH * 64], BF16)
    if "P1" in phases:
        with contextlib.ExitStack() as stk:
            p1(stk)
            em.barrier()
    if "P2" in phases:
        with contextlib.ExitStack() as stk:
            p2(stk, fd_fm, eact)
            em.barrier()
    if "P4" in phases:
        with contextlib.ExitStack() as stk:
            p3x(stk, eact)
            em.barrier()
        with contextlib.ExitStack() as stk:
            p4m(stk, fd_fm)
            em.barrier()
        with contextlib.ExitStack() as stk:
            p4f(stk)
            em.barrier()
    midstack.close()

    if DBG:
        em.dma("sp", outs["dbg_x1"], x1buf, tag="dbgx1")
        em.dma("sp", outs["dbg_xln2fm"], xln2fm, tag="dbgxln")
        if "P2" in phases:
            em.dma("sp", outs["dbg_xog"], xogbuf, tag="dbgxog")
            em.dma("sp", outs["dbg_ylf"], ylf[1:T + 1, :], tag="dbgylf")
            em.dma("sp", outs["dbg_ylb"], ylb[1:T + 1, :], tag="dbgylb")
    if "P4" not in phases:
        em.dma("sp", out, x1buf[4:4 + T, :], tag="outplaceholder")

    with contextlib.ExitStack() as semstack:
        sems = {}
        for n in ("pe", "act", "dve", "pool", "sp"):
            sems[n] = semstack.enter_context(nc.semaphore("s_" + n))
        for n in em.dma_sem_names:
            sems[n] = semstack.enter_context(nc.semaphore("s_" + n))
        block = semstack.enter_context(nc.Block())
        em.emit(block, sems)
    topstack.close()
    return nc



BF = ml_dtypes.bfloat16
L = 16384


def aux_constants():
    c = {}
    c["identb"] = np.eye(128, dtype=BF)
    c["identf"] = np.eye(128, dtype=np.float32)
    tri = np.zeros((128, 128), np.float32)
    for s in range(128):
        tri[s, s:] = 1.0
    c["triu"] = tri
    c["tril"] = np.ascontiguousarray(tri.T)
    c["maskF"] = tri.astype(BF)
    c["maskB"] = np.ascontiguousarray(tri.T).astype(BF)
    indh = np.zeros((H, DFF), np.float32)
    for h in range(H):
        indh[h, h * 128:(h + 1) * 128] = 1.0
    c["indh"] = indh.astype(BF)
    indhp = np.zeros((H, DI), np.float32)
    for h in range(H):
        indhp[h, h * 64:(h + 1) * 64] = 1.0
    c["indhp"] = indhp.astype(BF)
    c["zer32"] = np.zeros((32, 128), BF)
    c["neg3"] = np.full((3, 128), -1.0, BF)
    return c


def prep_inputs(inp):
    u = np.asarray(inp["L0_attn_u"])
    ipw = np.asarray(inp["L0_attn_in_proj_w"])
    aux = aux_constants()
    ipwT = np.ascontiguousarray(ipw.T)
    opwT = np.ascontiguousarray(np.asarray(inp["L0_attn_out_proj_w"]).T)
    fcdT = np.ascontiguousarray(np.asarray(inp["L0_attn_fc_D_w"]).T)
    convw = np.ascontiguousarray(
        np.asarray(inp["L0_attn_conv_w"]).reshape(CONVDIM, 4))
    dtb = np.asarray(inp["L0_attn_dt_bias"])
    dtbias2 = np.ascontiguousarray(np.concatenate([dtb, dtb]))
    drep = np.ascontiguousarray(np.broadcast_to(
        np.repeat(np.asarray(inp["L0_attn_D"]), 64)[None, :], (128, DI))
    ).astype(np.float32)

    d32bc = np.ascontiguousarray(np.broadcast_to(
        np.asarray(inp["L0_attn_D"], np.float32)[None, :], (128, H)))
    shared = dict(
        d32bc=d32bc,
        w1=np.asarray(inp["L0_ffn_W1"]), w2=np.asarray(inp["L0_ffn_W2"]),
        b1=np.asarray(inp["L0_ffn_b1"]), b2=np.asarray(inp["L0_ffn_b2"]),
        phases=np.asarray(inp["L0_ffn_quantum_phases"]),
        wreal=np.asarray(inp["L0_ffn_quantum_weights_real"]),
        wimag=np.asarray(inp["L0_ffn_quantum_weights_imag"]),
        ipwT=ipwT, opwT=opwT, fcdT=fcdT, convw=convw,
        convb=np.asarray(inp["L0_attn_conv_b"]),
        dtbias2=dtbias2, alog=np.asarray(inp["L0_attn_A_log"]),
        dskip=np.asarray(inp["L0_attn_D"]),
        normw=np.asarray(inp["L0_attn_norm_w"]),
        drep=drep,
        **aux,
    )
    in_maps = []
    for c in range(8):
        b, s = c // 4, c % 4
        a = s * T
        usl = np.zeros((USL, D), np.float32)
        lo = a - 4
        lo2, hi2 = max(lo, 0), min(lo + USL, L)
        usl[lo2 - lo: hi2 - lo] = u[b, lo2:hi2]
        lnmask = np.ones(4352, np.float32)
        if s == 0:
            lnmask[0:4] = 0.0
        if s == 3:
            lnmask[4100:] = 0.0
        lnmask[4101:] = 0.0
        gown = np.zeros(8, np.float32)
        gprev = np.zeros(8, np.float32)
        for j in range(4):
            gown[2 * j] = 1.0 if j < s else 0.0
            gown[2 * j + 1] = 1.0 - gown[2 * j]
            gprev[2 * j] = 1.0 if j < s - 1 else 0.0
            gprev[2 * j + 1] = 1.0 - gprev[2 * j]
        selown = np.zeros(4, np.float32)
        selown[s] = 1.0
        selprev = np.zeros(4, np.float32)
        if s > 0:
            selprev[s - 1] = 1.0
        selnxt = np.zeros(4, np.float32)
        if s < 3:
            selnxt[s + 1] = 1.0
        mask2 = np.ones((128, 2), np.float32)
        if s == 0:
            mask2[0, 0] = 0.0
        if s == 3:
            mask2[127, 1] = 0.0
        m = dict(shared)
        m.update(u=usl, lnmask=lnmask, gown=gown, gprev=gprev,
                 selown=selown, selprev=selprev, selnxt=selnxt,
                 selobc=np.ascontiguousarray(
                     np.broadcast_to(selown, (128, 4))),
                 selpbc=np.ascontiguousarray(
                     np.broadcast_to(selprev, (128, 4))),
                 selnbc=np.ascontiguousarray(
                     np.broadcast_to(selnxt, (128, 4))),
                 mask2=mask2)
        in_maps.append(m)
    return in_maps


_CACHE = {}


def kernel(**inputs):
    if "nc" not in _CACHE:
        _CACHE["nc"] = build_program(phases=("P1", "P2", "P4"))
    nc = _CACHE["nc"]
    in_maps = prep_inputs(inputs)
    from concourse.bass_utils import run_bass_kernel_spmd
    res = run_bass_kernel_spmd(nc, in_maps, core_ids=list(range(8)))
    full = np.zeros((2, L, D), np.float32)
    for c in range(8):
        b, s = c // 4, c % 4
        full[b, s * T:(s + 1) * T] = res.results[c]["out"]
    return full


# version 16
# speedup vs baseline: 1.1868x; 1.0026x over previous
import numpy as np
import ml_dtypes

"""Core emitter + program builder for the mk_macaron_d1 megakernel.

This module is developed in scratch/ and inlined into kernel.py when stable.
"""
import math
import numpy as np
import concourse.bass as bass
import concourse.mybir as mybir

F32 = mybir.dt.float32
BF16 = mybir.dt.bfloat16
I32 = mybir.dt.int32
AF = mybir.ActivationFunctionType
ALU = mybir.AluOpType
AX = mybir.AxisListType

PI = float(np.pi)
INV2PI = float(1.0 / (2.0 * np.pi))
N2PI = float(-2.0 * np.pi)
RSQRT_C = float(0x5F3759DF)


ENGINES = ("pe", "act", "dve", "pool", "sp")


def _ap_region(ap):
    t = ap.tensor
    name = t.name
    dims = [list(d) for d in ap.ap]
    try:
        space = str(ap.space)
    except Exception:
        space = "?"
    try:
        esz = np.dtype(mybir.dt.np(ap.dtype)).itemsize
    except Exception:
        esz = 4
    if "DRAM" in space.upper() or "DRam" in str(type(t)):
        off = ap.offset if isinstance(ap.offset, int) else 0
        ext = 1
        for st, ct in dims:
            ext += abs(st) * (ct - 1)
        return (name, 0, 128, off * esz, (off + ext) * esz)
    off = ap.offset if isinstance(ap.offset, int) else 0
    pitch = abs(dims[0][0]) if dims and dims[0][0] else 1
    p0 = off // pitch if pitch > 0 else 0
    p1 = p0 + ap.partition_size()
    coff = off - p0 * pitch
    ext = 1
    for st, ct in dims[1:]:
        ext += abs(st) * (ct - 1)
    b0, b1 = coff * esz, (coff + ext) * esz
    if "PSUM" in space.upper():
        b0 = (b0 // 2048) * 2048
        b1 = ((b1 + 2047) // 2048) * 2048
        p0, p1 = 0, 128
    return (name, p0, p1, b0, b1)


def _overlap(r1, r2):
    return (r1[0] == r2[0] and r1[1] < r2[2] and r2[1] < r1[2]
            and r1[3] < r2[4] and r2[3] < r1[4])


class Op:
    __slots__ = ("engine", "build", "reads", "writes", "idx", "waits",
                 "carrier", "dma_sem", "tag", "accum")

    def __init__(self, engine, build, reads, writes, tag="", accum=False):
        self.engine = engine
        self.build = build
        self.reads = [_ap_region(a) for a in reads if a is not None]
        self.writes = [_ap_region(a) for a in writes if a is not None]
        self.idx = -1
        self.waits = []
        self.carrier = False
        self.tag = tag
        self.accum = accum


NDMASEM = 12


class Emitter:
    def __init__(self, nc):
        self.nc = nc
        self.ops = []
        self.counts = {e: 0 for e in ENGINES}
        self._writers = {}
        self._readers = {}
        self.dma_sem_names = [f"dq{i}" for i in range(NDMASEM)] + ["dqcc"]
        self.dma_counts = {n: 0 for n in self.dma_sem_names}
        self._dma_rr = 0
        self._dma_rr_sw = 4

    def _record(self, op):
        op.idx = len(self.ops)
        self.ops.append(op)
        eng = op.engine
        pre_dma_wait = None
        if eng.startswith("dma:"):
            q = eng.split(":")[1]
            if q.endswith("!cc"):
                sem = "dqcc"
                if self.dma_counts[sem] > 0:
                    pre_dma_wait = ("dma", sem, self.dma_counts[sem])
                self.dma_counts[sem] += 1
                seq = ("dma", sem, self.dma_counts[sem])
                op.dma_sem = sem
            elif q == "pool":
                sem = self.dma_sem_names[self._dma_rr_sw]
                self._dma_rr_sw = 4 + (self._dma_rr_sw + 1 - 4) % 4
            else:
                sem = self.dma_sem_names[self._dma_rr]
                self._dma_rr = (self._dma_rr + 1) % 4
            if not q.endswith("!cc"):
                if self.dma_counts[sem] > 0:
                    pre_dma_wait = ("dma", sem, self.dma_counts[sem])
                self.dma_counts[sem] += 16
                seq = ("dma", sem, self.dma_counts[sem])
                op.dma_sem = sem
        else:
            self.counts[eng] += 1
            seq = ("eng", eng, self.counts[eng])
            op.dma_sem = None
        waits = set()
        for r in op.reads:
            for (reg, src) in self._writers.get(r[0], []):
                if _overlap(reg, r):
                    waits.add(src)
        for w in op.writes:
            for (reg, src) in self._writers.get(w[0], []):
                if _overlap(reg, w):
                    if op.accum and src[0] == "eng" and src[1] == "pe" \
                            and (eng == "pe"):
                        continue
                    waits.add(src)
            for (reg, src) in self._readers.get(w[0], []):
                if _overlap(reg, w):
                    waits.add(src)
        final = []
        for (kind, key, val) in waits:
            if kind == "eng" and key == "pe" and not eng.startswith("dma:") \
                    and eng == "pe":
                continue
            final.append((kind, key, val))
        if pre_dma_wait is not None:
            final.append(pre_dma_wait)
        op.waits = final
        for w in op.writes:
            lst = self._writers.setdefault(w[0], [])
            lst[:] = [(reg, src) for (reg, src) in lst if not _overlap(reg, w)]
            lst.append((w, seq))
            rl = self._readers.get(w[0])
            if rl:
                rl[:] = [(reg, src) for (reg, src) in rl if not _overlap(reg, w)]
        for r in op.reads:
            self._readers.setdefault(r[0], []).append((r, seq))
        return op

    def op(self, engine, build, reads, writes, tag="", accum=False):
        return self._record(Op(engine, build, reads, writes, tag, accum))

    def barrier(self):
        snap_eng = dict(self.counts)
        snap_dma = dict(self.dma_counts)
        for e in ENGINES:
            waits = [("eng", o, snap_eng[o]) for o in ENGINES
                     if snap_eng[o] > 0]
            waits += [("dma", n, v) for n, v in snap_dma.items() if v > 0]
            op = Op(e, lambda h: h.nop(), [], [], tag="barrier")
            self.counts[e] += 1
            op.dma_sem = None
            op.waits = waits
            op.idx = len(self.ops)
            self.ops.append(op)

    def dma(self, queue, out, in_, tag="", slow=False):
        def build(h):
            if slow:
                return h.dma_start(out=out, in_=in_,
                                   allow_slow_non_contiguous=True)
            return h.dma_start(out=out, in_=in_)
        return self._record(Op("dma:" + queue, build, [in_], [out], tag))

    def emit(self, block, sems):
        eng_wait_targets = {e: set() for e in ENGINES}
        for op in self.ops:
            for (kind, key, val) in op.waits:
                if kind == "eng":
                    eng_wait_targets[key].add(val)
        carrier_val = {}
        for e in ENGINES:
            targets = eng_wait_targets[e]
            cnt = 0
            vals = {}
            for n in range(1, self.counts[e] + 1):
                if n in targets:
                    cnt += 1
                    vals[n] = cnt
                else:
                    vals[n] = None
            carrier_val[e] = vals
        carrier_cum = {}
        for e in ENGINES:
            cum = {}
            c = 0
            for n in range(1, self.counts[e] + 1):
                if carrier_val[e][n] is not None:
                    c += 1
                cum[n] = c
            carrier_cum[e] = cum

        streams = {e: [] for e in ENGINES}
        eng_op_no = {e: 0 for e in ENGINES}
        dma_no = {n: 0 for n in self.dma_sem_names}
        for op in self.ops:
            if op.engine.startswith("dma:"):
                q = op.engine.split(":")[1].replace("!cc", "")
                sem = op.dma_sem
                dma_no[sem] += 1 if sem == "dqcc" else 16
                streams[q].append((op, ("dma", sem, dma_no[sem])))
            else:
                eng_op_no[op.engine] += 1
                streams[op.engine].append((op, ("eng", op.engine,
                                                eng_op_no[op.engine])))

        handles = {
            "pe": self.nc.tensor, "act": self.nc.scalar, "dve": self.nc.vector,
            "pool": self.nc.gpsimd, "sp": self.nc.sync,
        }
        emitter = self

        def run_stream(eng_name, eng):
            floors = {}

            def need_wait(kind, key, val):
                if kind == "eng":
                    v = carrier_cum[key][val]
                    if v == 0:
                        return None
                    sem = sems[key]
                    if floors.get(key, 0) >= v:
                        return None
                    floors[key] = v
                    return (sem, v)
                else:
                    if floors.get(key, 0) >= val:
                        return None
                    floors[key] = val
                    return (sems[key], val)

            for (op, seq) in streams[eng_name]:
                for (kind, key, val) in sorted(op.waits,
                                               key=lambda t: (t[1], t[2])):
                    w = need_wait(kind, key, val)
                    if w is not None:
                        eng.wait_ge(w[0], w[1])
                inst = op.build(eng)
                if seq[0] == "eng":
                    v = carrier_val[eng_name].get(seq[2])
                    if v is not None:
                        inst.then_inc(sems[eng_name], 1)
                else:
                    inst.then_inc(sems[seq[1]], 1 if seq[1] == "dqcc" else 16)
            if eng_name == "sp":
                for n in emitter.dma_sem_names:
                    if emitter.dma_counts[n] > 0:
                        eng.wait_ge(sems[n], emitter.dma_counts[n])

        for name in ("sp", "pe", "act", "dve", "pool"):
            dec = {"sp": block.sync, "pe": block.tensor, "act": block.scalar,
                   "dve": block.vector, "pool": block.gpsimd}[name]

            def make(fn_name, engine_obj):
                def _inner(eng):
                    run_stream(fn_name, eng)
                return _inner
            dec(make(name, handles[name]))

    def mm(self, out, lhsT, rhs, start=True, stop=True, tag=""):
        return self.op("pe",
                       lambda e: e.matmul(out, lhsT, rhs, start=start, stop=stop),
                       [lhsT, rhs], [out], tag, accum=not start)

    def transpose(self, out, in_, ident, tag=""):
        return self.op("pe", lambda e: e.transpose(out, in_, ident),
                       [in_, ident], [out], tag)

    def act(self, out, in_, func, bias=0.0, scale=1.0, accum_out=None, tag=""):
        reads = [in_]
        if isinstance(bias, bass.AP):
            reads.append(bias)
        if isinstance(scale, bass.AP):
            reads.append(scale)
        writes = [out] + ([accum_out] if accum_out is not None else [])
        return self.op("act",
                       lambda e: e.activation(out, in_, func, bias=bias,
                                              scale=scale, accum_out=accum_out),
                       reads, writes, tag)

    def _vec(self, engine, meth, out, ins, scalars, tag, kwargs=None):
        reads = list(ins) + [s for s in scalars if isinstance(s, bass.AP)]
        kw = kwargs or {}

        def build(e):
            return getattr(e, meth)(*([out] + list(ins) + list(scalars)), **kw)
        return self.op(engine, build, reads, [out], tag)

    def tt(self, out, a, b, alu, engine="dve", tag=""):
        return self.op(engine, lambda e: e.tensor_tensor(out, a, b, alu),
                       [a, b], [out], tag)

    def ts(self, out, in_, s1, s2, op0, op1=ALU.bypass, engine="dve", tag="",
           accum_out=None):
        reads = [in_] + [s for s in (s1, s2) if isinstance(s, bass.AP)]
        writes = [out] + ([accum_out] if accum_out is not None else [])
        return self.op(engine,
                       lambda e: e.tensor_scalar(out, in_, s1, s2, op0, op1,
                                                 accum_out=accum_out),
                       reads, writes, tag)

    def stt(self, out, in0, scalar, in1, op0, op1, engine="dve", tag=""):
        reads = [in0, in1] + ([scalar] if isinstance(scalar, bass.AP) else [])
        return self.op(engine,
                       lambda e: e.scalar_tensor_tensor(out, in0, scalar, in1,
                                                        op0, op1),
                       reads, [out], tag)

    def copy(self, out, in_, engine="dve", tag=""):
        return self.op(engine, lambda e: e.tensor_copy(out, in_),
                       [in_], [out], tag)

    def memset(self, ap, val, engine="dve", tag=""):
        return self.op(engine, lambda e: e.memset(ap, val), [], [ap], tag)

    def reduce(self, out, in_, alu, engine="dve", tag=""):
        return self.op(engine,
                       lambda e: e.tensor_reduce(out, in_, AX.X, alu),
                       [in_], [out], tag)

    def scan(self, out, d0, d1, init, op0, op1, tag=""):
        reads = [d0, d1] + ([init] if isinstance(init, bass.AP) else [])
        return self.op("dve",
                       lambda e: e.tensor_tensor_scan(out, d0, d1, init, op0, op1),
                       reads, [out], tag)

    def collective(self, kind, alu, groups, ins, outs, tag=""):
        def build(e):
            return e.collective_compute(kind, alu, replica_groups=groups,
                                        ins=ins, outs=outs)
        op = Op("dma:pool!cc", build, ins, outs, tag)
        return self._record(op)

    def wrap1(self, out, a, g1, g2, engine="dve"):
        self.ts(g1, a, -PI, None, ALU.is_lt, engine=engine)
        self.ts(g2, a, PI, None, ALU.is_gt, engine=engine)
        self.tt(g1, g1, g2, ALU.subtract, engine=engine)
        self.stt(out, g1, 2 * PI, a, ALU.mult, ALU.add, engine=engine)
        return out

    def rsqrt(self, out, v, scr1, scr2=None, iters=3):
        self.copy(scr1, v.bitcast(I32))
        self.ts(scr1, scr1, -0.5, RSQRT_C, ALU.mult, ALU.add)
        self.copy(out.bitcast(I32), scr1)
        for _ in range(iters):
            self.tt(scr1, out, out, ALU.mult)
            self.tt(scr1, scr1, v, ALU.mult)
            self.ts(scr1, scr1, -0.5, 1.5, ALU.mult, ALU.add)
            self.tt(out, out, scr1, ALU.mult)
        return out


"""Program builder: phases of the macaron megakernel. TILE=256 tokens."""
import contextlib

D = 1024
DI = 2048
H = 32
PD = 64
NS = 128
CONVDIM = 2560
DFF = 4096
CK = 128
T = 4096
NCH = 32
USL = 4352
EPS = 1e-5
TILE = 256
NG = TILE // 128

DBG = False


def _v(h, offset, dims):
    ap = h if isinstance(h, bass.AP) else h[:, :]
    return bass.AP(ap.tensor, offset, dims)


def build_program(phases=("P1", "P2")):
    nc = bass.Bass("TRN2", target_bir_lowering=False)
    em = Emitter(nc)

    din = {}
    for name, shape in [
        ("u", [USL, D]), ("w1", [D, DFF]), ("w2", [DFF, D]),
        ("b1", [DFF]), ("b2", [D]),
        ("phases", [DFF]), ("wreal", [DFF]), ("wimag", [DFF]),
        ("ipwT", [D, 4672]), ("opwT", [DI, D]), ("fcdT", [DI, H]),
        ("convw", [CONVDIM, 4]), ("convb", [CONVDIM]),
        ("dtbias2", [2 * H]), ("alog", [H]), ("dskip", [H]),
        ("normw", [DI]),
        ("identb", [128, 128]), ("identf", [128, 128]),
        ("triu", [128, 128]), ("tril", [128, 128]),
        ("maskF", [128, 128]), ("maskB", [128, 128]),
        ("indh", [H, DFF]), ("indhp", [H, DI]),
        ("lnmask", [4352]), ("drep", [128, DI]),
        ("gown", [8]), ("gprev", [8]),
        ("selown", [4]), ("selprev", [4]), ("selnxt", [4]),
        ("selobc", [128, 4]), ("selpbc", [128, 4]), ("selnbc", [128, 4]),
        ("zer32", [32, 128]), ("neg3", [3, 128]),
        ("mask2", [128, 2]), ("d32bc", [128, 32]),
    ]:
        dt = BF16 if name in ("identb", "maskF", "maskB", "indh",
                              "indhp", "zer32", "neg3") else F32
        din[name] = nc.dram_tensor(name, shape, dt, kind="ExternalInput").ap()

    x1buf = nc.dram_tensor("x1buf", [USL, D], F32).ap()
    xln2fm = nc.dram_tensor("xln2fm", [D, USL], BF16).ap()
    xogbuf = nc.dram_tensor("xogbuf", [T + 2, DI], BF16).ap()
    xogfm = nc.dram_tensor("xogfm", [CONVDIM, T], BF16).ap()
    ylf = nc.dram_tensor("ylf", [T + 2, DI], BF16).ap()
    ylb = nc.dram_tensor("ylb", [T + 2, DI], BF16).ap()
    flscr = nc.dram_tensor("flscr", [10, DFF], BF16).ap()
    ccin = nc.dram_tensor("ccin", [264, DI], F32).ap()
    ccout = nc.dram_tensor("ccout", [4 * 264, DI], F32).ap()
    zbuf = nc.dram_tensor("zbuf", [T, DI], BF16).ap()
    adtbuf = nc.dram_tensor("adtbuf", [64, T], F32).ap()
    x2buf = nc.dram_tensor("x2buf", [T, D], F32).ap()
    dtfmbuf = nc.dram_tensor("dtfmbuf", [64, T], F32).ap()
    dtlnF = nc.dram_tensor("dtlnF", [64, T], F32).ap()

    outs = {}
    if DBG:
        for nm, shape, dt in [
            ("dbg_x1", [USL, D], F32), ("dbg_xln2fm", [D, USL], BF16),
            ("dbg_dt", [64, T], F32), ("dbg_xog", [T + 2, DI], BF16),
            ("dbg_ylf", [T, DI], BF16), ("dbg_ylb", [T, DI], BF16),
            ("dbg_sf", [NS, DI], F32), ("dbg_sb", [NS, DI], F32),
            ("dbg_fd", [H, T], BF16), ("dbg_eact", [128, NCH * 64], BF16),
        ]:
            outs[nm] = nc.dram_tensor(nm, shape, dt,
                                      kind="ExternalOutput").ap()
    out = nc.dram_tensor("out", [T, D], F32, kind="ExternalOutput").ap()

    topstack = contextlib.ExitStack()

    def SB(stk, name, shape, dt=F32):
        return stk.enter_context(nc.sbuf_tensor(name, shape, dt))

    def PS(stk, name, shape, dt=F32):
        return stk.enter_context(nc.psum_tensor(name, shape, dt))

    identb = SB(topstack, "identb_s", [128, 128], BF16)
    identf = SB(topstack, "identf_s", [128, 128])
    lnmask = SB(topstack, "lnmask_s", [128, 34])
    em.dma("sp", identb[:, :], din["identb"])
    em.dma("sp", identf[:, :], din["identf"])
    em.dma("sp", lnmask[:, :], din["lnmask"].rearrange("(c p) -> p c", p=128),
           slow=True)

    def p1(stk):
        w1b = SB(stk, "w1b", [128, 8 * DFF], BF16)
        w2b = SB(stk, "w2b", [128, 32 * D], BF16)
        b1c = SB(stk, "b1c", [128, 32])
        b2c = SB(stk, "b2c", [128, 8])
        thetac = SB(stk, "thetac", [128, 32])
        thetab = SB(stk, "thetab", [128, 32])
        coefc = SB(stk, "coefc", [128, 32])
        ut = SB(stk, "ut", [128, NG * D])
        sumc = SB(stk, "sumc", [128, 8])
        sqc = SB(stk, "sqc", [128, 8])
        st1 = SB(stk, "st1", [128, 8])
        st2 = SB(stk, "st2", [128, 8])
        st3 = SB(stk, "st3", [128, 8])
        rstd = SB(stk, "rstd", [128, 8])
        nmr = SB(stk, "nmr", [128, 8])
        sq_scr = SB(stk, "sq_scr", [128, D])
        xlnt = SB(stk, "xlnt", [128, NG * D], BF16)
        xfm = SB(stk, "xfm", [128, 8 * TILE], BF16)
        aa = SB(stk, "aa", [128, TILE], BF16)
        kfl = SB(stk, "kfl", [128, TILE], BF16)
        sargt = SB(stk, "sargt", [128, TILE], BF16)
        stile = SB(stk, "stile", [128, TILE])
        qt = SB(stk, "qt", [128, TILE], BF16)
        hact = SB(stk, "hact", [128, 32 * TILE], BF16)
        o2t = [SB(stk, f"o2t{i}", [128, TILE], BF16) for i in range(2)]
        x1t = SB(stk, "x1t", [128, NG * D])
        xfm2 = SB(stk, "xfm2", [128, 8 * TILE], BF16)
        phs = SB(stk, "phs", [128, 32])
        wrs = SB(stk, "wrs", [128, 32])
        wis = SB(stk, "wis", [128, 32])
        t32a = SB(stk, "t32a", [128, 32])
        t32b = SB(stk, "t32b", [128, 32])
        ptT = PS(stk, "ptT", [128, 2048], BF16)
        pfb = [PS(stk, f"pf{i}", [128, 512]) for i in range(3)]
        pob = [PS(stk, f"po{i}", [128, 512]) for i in range(2)]

        for k in range(8):
            em.dma("pool", w1b[:, k * DFF:(k + 1) * DFF],
                   din["w1"][k * 128:(k + 1) * 128, :], tag="w1")
        for k in range(8):
            em.dma("pool",
                   w2b[:, k * 4 * D:(k + 1) * 4 * D].rearrange(
                       "p (c n) -> p c n", c=4),
                   din["w2"].rearrange("(c p) n -> p c n",
                                       p=128)[:, 4 * k:4 * k + 4, :], tag="w2")
        em.dma("sp", b1c[:, :], din["b1"].rearrange("(c p) -> p c", p=128),
               slow=True)
        em.dma("sp", b2c[:, :], din["b2"].rearrange("(c p) -> p c", p=128),
               slow=True)
        em.dma("sp", phs[:, :], din["phases"].rearrange("(c p) -> p c", p=128),
               slow=True)
        em.dma("sp", wrs[:, :], din["wreal"].rearrange("(c p) -> p c", p=128),
               slow=True)
        em.dma("sp", wis[:, :], din["wimag"].rearrange("(c p) -> p c", p=128),
               slow=True)
        em.copy(thetac[:, :], phs[:, :])
        for _ in range(6):
            em.wrap1(thetac[:, :], thetac[:, :], t32a[:, :], t32b[:, :])
        em.act(t32a[:, :], thetac[:, :], AF.Sin)
        em.act(t32b[:, :], thetac[:, :], AF.Sin, scale=0.5)
        em.tt(t32b[:, :], t32b[:, :], t32b[:, :], ALU.mult)
        em.ts(t32b[:, :], t32b[:, :], -2.0, 1.0, ALU.mult, ALU.add)
        em.tt(t32a[:, :], t32a[:, :], wis[:, :], ALU.mult)
        em.tt(t32b[:, :], t32b[:, :], wrs[:, :], ALU.mult)
        em.tt(t32a[:, :], t32a[:, :], t32b[:, :], ALU.add)
        em.tt(t32a[:, :], t32a[:, :], wis[:, :], ALU.mult)
        em.ts(coefc[:, :], t32a[:, :], 0.1, None, ALU.mult)
        em.stt(thetab[:, :], b1c[:, :], 0.1, thetac[:, :], ALU.mult, ALU.add)

        def ln_tokmajor(src, xout, mask_col=None, soff=0):
            for g in range(NG):
                gs = src[:, g * D:(g + 1) * D]
                em.act(sq_scr[:, :], gs, AF.Square,
                       accum_out=sqc[:, soff + g:soff + g + 1])
                em.reduce(sumc[:, soff + g:soff + g + 1], gs, ALU.add)
            sl = slice(soff, soff + NG)
            em.ts(st1[:, sl], sumc[:, sl], 1.0 / D, None, ALU.mult)
            em.ts(st2[:, sl], sqc[:, sl], 1.0 / D, None, ALU.mult)
            em.tt(st3[:, sl], st1[:, sl], st1[:, sl], ALU.mult)
            em.tt(st2[:, sl], st2[:, sl], st3[:, sl], ALU.subtract)
            em.ts(st2[:, sl], st2[:, sl], EPS, None, ALU.add)
            em.rsqrt(rstd[:, sl], st2[:, sl], st3[:, sl])
            em.tt(nmr[:, sl], st1[:, sl], rstd[:, sl], ALU.mult)
            em.ts(nmr[:, sl], nmr[:, sl], -1.0, None, ALU.mult)
            for g in range(NG):
                c = soff + g
                em.ts(xout[:, g * D:(g + 1) * D], src[:, g * D:(g + 1) * D],
                      rstd[:, c:c + 1], nmr[:, c:c + 1], ALU.mult, ALU.add)
                if mask_col is not None:
                    em.ts(xout[:, g * D:(g + 1) * D],
                          xout[:, g * D:(g + 1) * D],
                          mask_col[:, g:g + 1], None, ALU.mult)

        def tm_to_fm(src_bf, dst, tag):
            for k in range(8):
                for g in range(NG):
                    em.transpose(
                        ptT[:, (k * NG + g) * 128:(k * NG + g + 1) * 128],
                        src_bf[:, g * D + k * 128: g * D + (k + 1) * 128],
                        identb[:, :], tag=tag)
            dap = dst[:, :]
            sap = ptT[:, :]
            dv = bass.AP(dap.tensor, 0, [list(dap.ap[0]), [TILE, 8],
                                         [128, NG], [1, 128]])
            sv = bass.AP(sap.tensor, 0, [list(sap.ap[0]), [NG * 128, 8],
                                         [128, NG], [1, 128]])
            em.copy(dv, sv, tag=tag)

        ntile = USL // TILE
        for t in range(ntile):
            R0 = t * TILE
            em.dma("sp", ut[:, :].rearrange("p (g d) -> p g d", g=NG),
                   din["u"][R0:R0 + TILE, :].rearrange("(g p) d -> p g d",
                                                       p=128), tag=f"p1u{t}")
            ln_tokmajor(ut, xlnt)
            tm_to_fm(xlnt, xfm, f"p1t{t}")
            for f in range(32):
                bank = pfb[f % 3]
                for k in range(8):
                    em.mm(bank[:, 0:TILE],
                          w1b[:, k * DFF + f * 128: k * DFF + (f + 1) * 128],
                          xfm[:, k * TILE:(k + 1) * TILE],
                          start=(k == 0), stop=(k == 7))
                em.ts(aa[:, :], bank[:, 0:TILE], 0.1, thetab[:, f:f + 1],
                      ALU.mult, ALU.add)
                em.wrap1(sargt[:, :], aa[:, :], kfl[:, :], qt[:, :])
                em.act(stile[:, :], sargt[:, :], AF.Sin)
                em.stt(qt[:, :], stile[:, :], coefc[:, f:f + 1],
                       bank[:, 0:TILE], ALU.mult, ALU.add)
                em.act(hact[:, f * TILE:(f + 1) * TILE], qt[:, :], AF.Relu,
                       bias=b1c[:, f:f + 1])
            for do in range(8):
                bank = pob[do % 2]
                for kf in range(32):
                    em.mm(bank[:, 0:TILE],
                          w2b[:, kf * D + do * 128: kf * D + (do + 1) * 128],
                          hact[:, kf * TILE:(kf + 1) * TILE],
                          start=(kf == 0), stop=(kf == 31))
                em.ts(o2t[do % 2][:, :], bank[:, 0:TILE], b2c[:, do:do + 1],
                      None, ALU.add)
                for g in range(NG):
                    em.transpose(ptT[:, (do % 2) * 1024 + g * 128:
                                     (do % 2) * 1024 + (g + 1) * 128],
                                 o2t[do % 2][:, g * 128:(g + 1) * 128],
                                 identb[:, :])
                for g in range(NG):
                    em.stt(x1t[:, g * D + do * 128: g * D + (do + 1) * 128],
                           ptT[:, (do % 2) * 1024 + g * 128:
                               (do % 2) * 1024 + g * 128 + 128],
                           0.5,
                           ut[:, g * D + do * 128: g * D + (do + 1) * 128],
                           ALU.mult, ALU.add)
            mask_col = (lnmask[:, R0 // 128: R0 // 128 + NG]
                        if (t == 0 or t == ntile - 1) else None)
            ln_tokmajor(x1t, xlnt, mask_col=mask_col, soff=4)
            tm_to_fm(xlnt, xfm2, f"p1s{t}")
            em.dma("act",
                   x1buf[R0:R0 + TILE, :].rearrange("(g p) d -> p g d", p=128),
                   x1t[:, :].rearrange("p (g d) -> p g d", g=NG),
                   tag=f"x1st{t}")
            em.dma("act",
                   xln2fm[:, R0:R0 + TILE].rearrange("(k p) c -> p k c",
                                                     p=128),
                   xfm2[:, :].rearrange("p (k c) -> p k c", k=8),
                   tag=f"xlst{t}")

    def p2(stk, fd_fm, eact):
        wdtb = SB(stk, "wdtb", [128, 8 * 64], BF16)
        wxbcb = SB(stk, "wxbcb", [128, 8 * CONVDIM], BF16)
        fcdb = SB(stk, "fcdb", [128, 16 * H], BF16)
        convws = SB(stk, "convws", [128, 20 * 4])
        convbs = SB(stk, "convbs", [128, 20])
        convbh = SB(stk, "convbh", [128, 20])
        dtb2c = SB(stk, "dtb2c", [64, 1])
        a64c = SB(stk, "a64c", [64, 1])
        onec = SB(stk, "onec", [64, 1])
        adtc2 = [SB(stk, f"adtc{i}", [32, 128]) for i in range(2)]
        tri_fs = SB(stk, "tri_fs", [128, 128])
        tri_bs = SB(stk, "tri_bs", [128, 128])
        maskf_s = SB(stk, "maskf_s", [128, 128], BF16)
        maskb_s = SB(stk, "maskb_s", [128, 128], BF16)
        xz2 = [SB(stk, f"xz{i}", [128, 8 * 517], BF16)
               for i in range(2)]
        cog = [SB(stk, "cog0", [128, 20 * 514], BF16)] * 2
        convacc = SB(stk, "convacc", [128, 1028])
        convv = SB(stk, "convv", [128, 1028])
        offc = SB(stk, "offc", [32, 2])
        aclast = SB(stk, "aclast", [32, 8])
        cdc = SB(stk, "cdc", [32, 8])
        acs_c2 = [SB(stk, f"acs_c{i}", [32, 128]) for i in range(2)]
        acsn2 = [SB(stk, f"acsn{i}", [32, 128]) for i in range(2)]
        acsH2 = [SB(stk, f"acsH{i}", [32, 128], BF16) for i in range(2)]
        acsM2 = [SB(stk, f"acsM{i}", [32, 128], BF16) for i in range(2)]
        acsL2 = [SB(stk, f"acsL{i}", [32, 128], BF16) for i in range(2)]
        acsr2 = [SB(stk, f"acsr{i}", [32, 128]) for i in range(2)]
        lndt322 = [SB(stk, f"lndt32{i}", [32, 128]) for i in range(2)]
        xsp2 = [SB(stk, f"xsp{i}", [32, 128]) for i in range(2)]
        dtH2 = [SB(stk, f"dtH{i}", [32, 128], BF16) for i in range(2)]
        dtM2 = [SB(stk, f"dtM{i}", [32, 128], BF16) for i in range(2)]
        dtL2 = [SB(stk, f"dtL{i}", [32, 128], BF16) for i in range(2)]
        e_loc2 = [SB(stk, f"e_loc{i}", [32, 128]) for i in range(2)]
        dts322 = [SB(stk, f"dts32{i}", [32, 128]) for i in range(2)]
        ddb322 = [SB(stk, f"ddb32{i}", [32, 128], BF16) for i in range(2)]
        cdmat = SB(stk, "cdmat", [32, 128], BF16)
        ones32 = SB(stk, "ones32", [32, 128], BF16)
        din_indhp = SB(stk, "din_indhp", [32, DI], BF16)
        dd_fm2 = [SB(stk, f"dd_fm{i}", [32, 128]) for i in range(2)]
        lhsD2 = [SB(stk, f"lhsD{i}", [128, 128], BF16) for i in range(2)]
        rhsD = SB(stk, "rhsD", [128, DFF], BF16)
        Dm_sb2 = [SB(stk, f"Dm_sb{i}", [128, DFF], BF16) for i in range(2)]
        U_sb2 = [SB(stk, f"U_sb{i}", [128, 256], BF16) for i in range(2)]
        xh_sb2 = [SB(stk, f"xh_sb{i}", [128, DI], BF16) for i in range(2)]
        xt2_sb2 = [SB(stk, f"xt2_sb{i}", [128, DI], BF16) for i in range(2)]
        btm_sb2 = [SB(stk, f"btm_sb{i}", [128, 128], BF16) for i in range(2)]
        dtT_c2 = [SB(stk, f"dtT_c{i}", [128, 32]) for i in range(2)]
        elocT2 = [SB(stk, f"elocT{i}", [128, 32]) for i in range(2)]
        yoff_sb2 = [SB(stk, f"yoff_sb{i}", [128, DI], BF16) for i in range(2)]
        y_sb2 = [SB(stk, f"y_sb{i}", [128, DI], BF16) for i in range(2)]
        S_f = SB(stk, "S_f", [128, DI])
        S_b = SB(stk, "S_b", [128, DI])
        sbf = SB(stk, "sbf", [128, DI], BF16)
        xgf2 = [SB(stk, f"xgf{i}", [128, 16 * 128], BF16) for i in range(2)]
        bfm2 = [SB(stk, f"bfm{i}", [128, 128], BF16) for i in range(2)]
        cfm2 = [SB(stk, f"cfm{i}", [128, 128], BF16) for i in range(2)]
        a32 = SB(stk, "a32", [32, 1])
        pA = PS(stk, "pA", [128, 2048])
        pB = PS(stk, "pB", [128, 1024])
        pT = PS(stk, "pT", [128, 2048], BF16)

        for k in range(8):
            em.dma("pool", wdtb[:, k * 64:(k + 1) * 64],
                   din["ipwT"][k * 128:(k + 1) * 128, 4608:4672], tag="wdt")
        for k in range(8):
            em.dma("pool", wxbcb[:, k * CONVDIM:(k + 1) * CONVDIM],
                   din["ipwT"][k * 128:(k + 1) * 128, DI:DI + CONVDIM],
                   tag="wxbc")
        for k in range(16):
            em.dma("pool", fcdb[:, k * H:(k + 1) * H],
                   din["fcdT"][k * 128:(k + 1) * 128, :], tag="fcd")
        em.dma("sp", convws[:, :].rearrange("p (c j) -> p c j", c=20),
               din["convw"].rearrange("(c p) j -> p c j", p=128), slow=True)
        em.dma("sp", convbs[:, :],
               din["convb"].rearrange("(c p) -> p c", p=128), slow=True)
        em.dma("sp", dtb2c[:, :], din["dtbias2"].unsqueeze(1),
               slow=True)
        em.dma("sp", a32[:, :], din["alog"].unsqueeze(1), slow=True)
        em.dma("sp", tri_fs[:, :], din["triu"])
        em.dma("sp", tri_bs[:, :], din["tril"])
        em.dma("sp", maskf_s[:, :], din["maskF"])
        em.dma("sp", maskb_s[:, :], din["maskB"])
        for r in range(3):
            em.dma("sp", rhsD[32 * r:32 * (r + 1), :], din["indh"], tag="ind")
        em.memset(ones32[:, :], 1.0)
        em.dma("sp", din_indhp[:, :], din["indhp"])
        for rz in range(96, 106):
            em.dma("sp", rhsD[rz:rz + 1, :],
                   din["zer32"].rearrange("a b -> (a b)").unsqueeze(0),
                   tag=f"rz{rz}")
        em.dma("sp", lhsD2[0][96:128, :], din["zer32"], tag="lz0")
        em.dma("sp", lhsD2[1][96:128, :], din["zer32"], tag="lz1")
        em.dma("sp", lhsD2[0][96:99, :], din["neg3"], tag="ln0")
        em.dma("sp", lhsD2[1][99:102, :], din["neg3"], tag="ln1")
        em.ts(convbh[:, :], convbs[:, :], 0.5, None, ALU.mult)
        em.memset(onec[:, :], 1.0)
        em.act(a32[:, :], a32[:, :], AF.Exp)
        em.ts(a32[:, :], a32[:, :], -1.0, None, ALU.mult)
        em.copy(a64c[0:32, :], a32[:, :])
        em.copy(a64c[32:64, :], a32[:, :])
        dtcol = convacc[0:64, 0:512]
        dtscr = convv[0:64, 0:512]
        for t8 in range(8):
            c0 = 4 + 512 * t8
            xz = xz2[t8 % 2]
            em.dma("sp",
                   bass.AP(xz[:, :].tensor, 0,
                           [list(xz[:, :].ap[0]), [517, 8], [1, 512]]),
                   xln2fm[:, c0:c0 + 512].rearrange("(k p) c -> p k c", p=128),
                   tag=f"dtx{t8}")
            for k in range(8):
                em.mm(pB[0:64, 0:512], wdtb[:, k * 64:(k + 1) * 64],
                      xz[:, k * 517:k * 517 + 512], start=(k == 0),
                      stop=(k == 7))
            em.ts(dtcol, pB[0:64, 0:512], dtb2c[:, :], None, ALU.add)
            em.act(dtscr, dtcol, AF.Abs)
            em.act(dtscr, dtscr, AF.Exp, scale=-1.0)
            em.act(dtscr, dtscr, AF.Ln, bias=onec[:, :])
            em.ts(dtcol, dtcol, 0.0, None, ALU.max)
            em.tt(dtcol, dtscr, dtcol, ALU.add)
            adts = convv[64:128, 512:1024]
            em.ts(adts, dtcol, a64c[:, :], None, ALU.mult)
            em.dma("act", adtbuf[:, 512 * t8:512 * (t8 + 1)], adts,
                   tag=f"adtst{t8}")
            em.dma("act", dtfmbuf[:, 512 * t8:512 * (t8 + 1)],
                   dtcol, tag=f"dtfm{t8}")
            em.act(dtscr, dtcol, AF.Ln)
            em.ts(dtscr, dtscr, -88.0, None, ALU.max)
            em.dma("act", dtlnF[:, 512 * t8:512 * (t8 + 1)], dtscr,
                   tag=f"lnf{t8}")
        em.memset(offc[:, :], 0.0)
        em.memset(S_f[:, :], 0.0)
        em.memset(S_b[:, :], 0.0)

        def conv_group(g, buf):
            c0 = 4 + 512 * g
            xz = xz2[buf]
            em.dma("sp", xz[:, :].rearrange("p (k c) -> p k c", k=8),
                   xln2fm[:, c0 - 4:c0 + 513].rearrange("(k p) c -> p k c",
                                                        p=128), tag=f"cvx{g}")
            co = cog[buf]
            for fc in range(20):
                pc = pA[:, (fc % 2) * 1024:(fc % 2) * 1024 + 517]
                ca = convacc[:, (fc % 2) * 514:(fc % 2) * 514 + 514]
                cv = convv[:, (fc % 2) * 514:(fc % 2) * 514 + 514]
                for k in range(8):
                    em.mm(pc[:, 0:512],
                          wxbcb[:, k * CONVDIM + fc * 128:
                                k * CONVDIM + (fc + 1) * 128],
                          xz[:, k * 517:k * 517 + 512], start=(k == 0),
                          stop=(k == 7))
                for k in range(8):
                    em.mm(pc[:, 512:517],
                          wxbcb[:, k * CONVDIM + fc * 128:
                                k * CONVDIM + (fc + 1) * 128],
                          xz[:, k * 517 + 512:k * 517 + 517], start=(k == 0),
                          stop=(k == 7))
                em.ts(ca, pc[:, 0:514],
                      convws[:, 4 * fc:4 * fc + 1], None, ALU.mult)
                for j in range(1, 4):
                    em.stt(ca, pc[:, j:514 + j],
                           convws[:, 4 * fc + j:4 * fc + j + 1],
                           ca, ALU.mult, ALU.add)
                em.ts(cv, ca, convbs[:, fc:fc + 1], None, ALU.add)
                em.act(ca, ca, AF.Tanh, scale=0.5, bias=convbh[:, fc:fc + 1])
                em.ts(ca, ca, 0.5, 0.5, ALU.mult, ALU.add)
                em.tt(co[:, fc * 514:(fc + 1) * 514], cv, ca, ALU.mult)
            if g == 0:
                em.dma("act", xogbuf[0:1, :].rearrange("t (c p) -> p c t",
                                                       p=128),
                       co[:, :].rearrange("p (c t) -> p c t",
                                          c=20)[:, 0:16, 0:1],
                       tag="xogpad0", slow=True)
            if g == 7:
                em.dma("act", xogbuf[T + 1:T + 2, :].rearrange(
                           "t (c p) -> p c t", p=128),
                       co[:, :].rearrange("p (c t) -> p c t",
                                          c=20)[:, 0:16, 513:514],
                       tag="xogpad1", slow=True)
            em.dma("act",
                   xogfm[:, 512 * g:512 * (g + 1)].rearrange(
                       "(c p) t -> p c t", p=128),
                   bass.AP(co[:, :].tensor, 1,
                           [list(co[:, :].ap[0]), [514, 20], [1, 512]]),
                   tag=f"xfmst{g}")
            for k in range(16):
                em.mm(pA[0:32, 0:512], fcdb[:, k * H:(k + 1) * H],
                      co[:, k * 514 + 1:k * 514 + 513], start=(k == 0),
                      stop=(k == 15))
            em.copy(fd_fm[:, 512 * g:512 * (g + 1)], pA[0:32, 0:512])

        qctr = {0: 0, 1: 0}

        def ssd_stages(c, fwd):
            r0, r1 = (0, 32) if fwd else (32, 64)
            tri = tri_fs if fwd else tri_bs
            mask = maskf_s if fwd else maskb_s
            fcB, fcC = (16, 17) if fwd else (18, 19)
            S = S_f if fwd else S_b
            oc = 0 if fwd else 1
            d = 0 if fwd else 1
            dircol = 0 if fwd else 32
            xgf, bfm, cfm = xgf2[d], bfm2[d], cfm2[d]
            elocT = elocT2[d]
            lhsD = lhsD2[d]
            dts32, ddb32 = dts322[d], ddb322[d]
            acs_c, acsn, acsH = acs_c2[d], acsn2[d], acsH2[d]
            acsM, acsL, acsr = acsM2[d], acsL2[d], acsr2[d]
            lndt32, xsp = lndt322[d], xsp2[d]
            dtH, dtM, dtL = dtH2[d], dtM2[d], dtL2[d]
            e_loc, dd_fm, dtT_c = e_loc2[d], dd_fm2[d], dtT_c2[d]
            Dm_sb, U_sb, xh_sb = Dm_sb2[d], U_sb2[d], xh_sb2[d]
            xt2_sb, btm_sb = xt2_sb2[d], btm_sb2[d]
            yoff_sb, y_sb = yoff_sb2[d], y_sb2[d]
            pAd = lambda lo, hi: pA[:, d * 1024 + lo:d * 1024 + hi]

            def nextq():
                i = qctr[d]
                qctr[d] += 1
                return d * 1024 + (i % 2) * 512
            pBd = lambda lo, hi: pB[:, d * 512 + lo:d * 512 + hi]
            pTd = lambda lo, hi: pT[:, d * 1024 + lo:d * 1024 + hi]
            rr0 = 96 if fwd else 99

            def H1():
                em.dma("sp", xgf[:, :].rearrange("p (c t) -> p c t", c=16),
                       xogfm[:, c * 128:(c + 1) * 128].rearrange(
                           "(c p) t -> p c t", p=128)[:, 0:16, :],
                       tag=f"xgl{c}{d}")
                em.dma("sp", bfm[:, :], xogfm[fcB * 128:(fcB + 1) * 128,
                                              c * 128:(c + 1) * 128],
                       tag=f"bl{c}{d}")
                em.dma("sp", cfm[:, :], xogfm[fcC * 128:(fcC + 1) * 128,
                                              c * 128:(c + 1) * 128],
                       tag=f"cl{c}{d}")
                em.dma("sp", adtc2[d][:, :],
                       adtbuf[r0:r1, c * 128:(c + 1) * 128], tag=f"adl{c}{d}")
                em.transpose(pBd(0, 32), adtc2[d][:, :], identf[0:32, 0:32])
                em.copy(dtT_c[:, :], pBd(0, 32))
                em.mm(pBd(32, 160)[0:32, :], dtT_c[:, :], tri[:, :],
                      start=True, stop=True)
                em.copy(acs_c[:, :], pBd(32, 160)[0:32, :])
                last = 127 if fwd else 0
                em.copy(aclast[:, oc:oc + 1], acs_c[:, last:last + 1])
                em.act(e_loc[:, :], acs_c[:, :], AF.Exp)
                em.act(dd_fm[:, :], acs_c[:, :], AF.Exp, scale=-1.0,
                       bias=aclast[:, oc:oc + 1])
                em.act(cdc[:, oc:oc + 1], aclast[:, oc:oc + 1], AF.Exp)
                em.act(acsn[:, :], acs_c[:, :], AF.Exp, bias=offc[:, oc:oc + 1])
                em.transpose(pBd(160, 192), acsn[:, :], identf[0:32, 0:32])
                em.copy(eact[:, c * 64 + dircol: c * 64 + dircol + 32],
                        pBd(160, 192))
                em.tt(offc[:, oc:oc + 1], offc[:, oc:oc + 1],
                      aclast[:, oc:oc + 1], ALU.add)
                em.transpose(pBd(192, 224), e_loc[:, :], identf[0:32, 0:32])
                em.copy(elocT[:, :], pBd(192, 224))
                em.dma("sp", dts32[:, :],
                       dtfmbuf[r0:r1, c * 128:(c + 1) * 128],
                       tag=f"dtl{c}{d}")
                em.dma("sp", lndt32[:, :],
                       dtlnF[r0:r1, c * 128:(c + 1) * 128],
                       tag=f"lnl{c}{d}")
                em.tt(dd_fm[:, :], dd_fm[:, :], dts32[:, :], ALU.mult)
                em.copy(ddb32[:, :], dd_fm[:, :])
                em.ts(acsn[:, :], acs_c[:, :], -1.0, None, ALU.mult)
                em.copy(acsH[:, :], acsn[:, :])
                em.tt(acsr[:, :], acsn[:, :], acsH[:, :], ALU.subtract)
                em.copy(acsM[:, :], acsr[:, :])
                em.tt(acsr[:, :], acsr[:, :], acsM[:, :], ALU.subtract)
                em.copy(acsL[:, :], acsr[:, :])
                em.tt(xsp[:, :], acsn[:, :], lndt32[:, :], ALU.add)
                em.copy(dtH[:, :], xsp[:, :])
                em.tt(xsp[:, :], xsp[:, :], dtH[:, :], ALU.subtract)
                em.copy(dtM[:, :], xsp[:, :])
                em.tt(xsp[:, :], xsp[:, :], dtM[:, :], ALU.subtract)
                em.copy(dtL[:, :], xsp[:, :])
                em.copy(lhsD[0:32, :], dtH[:, :])
                em.copy(lhsD[32:64, :], dtM[:, :])
                em.copy(lhsD[64:96, :], dtL[:, :])
                em.dma("pool", rhsD[rr0:rr0 + 1, :],
                       flscr[3 * d:3 * d + 1, :], tag="dum") if False else None
                for ri, asrc in ((0, acsH), (1, acsM), (2, acsL)):
                    em.dma("pool", flscr[3 * d + ri:3 * d + ri + 1, :]
                           .rearrange("a (b c) -> (a b) c", b=32)
                           if False else
                           flscr[3 * d + ri:3 * d + ri + 1, :].rearrange(
                               "a (b c) -> (a b) c", b=32), asrc[:, :],
                           tag=f"flA{d}{ri}")
                    em.dma("pool", rhsD[rr0 + ri:rr0 + ri + 1, :],
                           flscr[3 * d + ri:3 * d + ri + 1, :],
                           tag=f"flB{d}{ri}")

            def H2():
                for q in range(8):
                    b = nextq()
                    em.mm(pA[:, b:b + 512], lhsD[0:102, :],
                          rhsD[0:102, q * 512:(q + 1) * 512],
                          start=True, stop=True)
                    em.ts(Dm_sb[:, q * 512:(q + 1) * 512],
                          pA[:, b:b + 512], 30.0, None, ALU.min)
                em.act(Dm_sb[:, :], Dm_sb[:, :], AF.Exp)
                em.mm(pBd(224, 352), bfm[:, :], cfm[:, :], start=True,
                      stop=True)
                em.tt(U_sb[:, 0:128], pBd(224, 352), mask[:, :], ALU.mult)
                em.copy(U_sb[:, 128:256], U_sb[:, 0:128])

            def M1():
                for w in range(2):
                    for k in range(8):
                        em.transpose(pTd(k * 128, (k + 1) * 128),
                                     xgf[:, (w * 8 + k) * 128:
                                         (w * 8 + k + 1) * 128], identb[:, :])
                    em.copy(xh_sb[:, w * 1024:(w + 1) * 1024], pTd(0, 1024))
                if fwd:
                    em.dma("act", xogbuf[c * 128 + 1:c * 128 + 129, :],
                           xh_sb[:, :], tag=f"xogst{c}")
                for q in range(4):
                    b = nextq()
                    em.mm(pA[:, b:b + 512], ddb32[:, :],
                          din_indhp[0:32, q * 512:(q + 1) * 512],
                          start=True, stop=True)
                    em.tt(xt2_sb[:, q * 512:(q + 1) * 512],
                          xh_sb[:, q * 512:(q + 1) * 512], pA[:, b:b + 512],
                          ALU.mult)
                em.transpose(pTd(0, 128), bfm[:, :], identb[:, :])
                em.copy(btm_sb[:, :], pTd(0, 128))
                for h2 in range(16):
                    em.tt(Dm_sb[:, h2 * 256:(h2 + 1) * 256],
                          Dm_sb[:, h2 * 256:(h2 + 1) * 256], U_sb[:, :],
                          ALU.mult)

            def T1():
                em.copy(sbf[:, :], S[:, :])
                ybuf = ylf if fwd else ylb
                for q in range(4):
                    b = nextq()
                    em.mm(pA[:, b:b + 512], cfm[:, :],
                          sbf[:, q * 512:(q + 1) * 512],
                          start=True, stop=True)
                    for h in range(8):
                        hh = q * 8 + h
                        em.act(yoff_sb[:, hh * 64:(hh + 1) * 64],
                               pA[:, b + h * 64:b + (h + 1) * 64], AF.Copy,
                               scale=elocT[:, hh:hh + 1])
                    b2 = nextq()
                    for h in range(8):
                        hh = q * 8 + h
                        em.mm(pA[:, b2 + h * 64:b2 + (h + 1) * 64],
                              Dm_sb[:, hh * 128:(hh + 1) * 128],
                              xh_sb[:, hh * 64:(hh + 1) * 64], start=True,
                              stop=True)
                    em.tt(y_sb[:, q * 512:(q + 1) * 512],
                          pA[:, b2:b2 + 512],
                          yoff_sb[:, q * 512:(q + 1) * 512], ALU.add)
                em.dma("act", ybuf[c * 128 + 1:c * 128 + 129, :], y_sb[:, :],
                       tag=f"yst{c}{d}")

            def T2():
                em.ts(cdmat[:, :], ones32[:, :], cdc[:, oc:oc + 1], None,
                      ALU.mult)
                for q in range(4):
                    b = nextq()
                    em.mm(pA[:, b:b + 512], cdmat[:, :],
                          din_indhp[0:32, q * 512:(q + 1) * 512],
                          start=True, stop=True)
                    em.tt(S[:, q * 512:(q + 1) * 512],
                          S[:, q * 512:(q + 1) * 512], pA[:, b:b + 512],
                          ALU.mult)
                    b2 = nextq()
                    em.mm(pA[:, b2:b2 + 512], btm_sb[:, :],
                          xt2_sb[:, q * 512:(q + 1) * 512],
                          start=True, stop=True)
                    em.tt(S[:, q * 512:(q + 1) * 512], pA[:, b2:b2 + 512],
                          S[:, q * 512:(q + 1) * 512], ALU.add)

            return [H1, H2, M1, T1, T2]

        conv_order = [0, 7, 1, 6, 2, 5, 3, 4]
        conv_group(conv_order[0], 0)
        conv_group(conv_order[1], 1)
        convs_done = 2
        units = []
        for ci in range(NCH):
            units.append((ci, True))
            units.append((NCH - 1 - ci, False))
        stv = [ssd_stages(c, f) for (c, f) in units]
        stv[0][0]()
        prev = None
        for i in range(len(stv)):
            if i in (4, 12, 20) and convs_done < 8:
                conv_group(conv_order[convs_done], convs_done % 2)
                conv_group(conv_order[convs_done + 1],
                           (convs_done + 1) % 2)
                convs_done += 2
            if prev is not None:
                stv[prev][3]()
                stv[prev][4]()
            stv[i][1]()
            stv[i][2]()
            if i + 1 < len(stv):
                stv[i + 1][0]()
            prev = i
        stv[prev][3]()
        stv[prev][4]()

        cdt2 = SB(stk, "cdt2", [32, 2])
        em.memset(convacc[:, :], 0.0)
        em.dma("sp", ccin[259:264, :].rearrange("r (a c) -> (r a) c", a=4),
               convacc[0:20, 0:512], tag="ccz1")
        em.dma("sp", ccin[256:257, 384:2048].rearrange(
                   "r (a c) -> (r a) c", a=4), convacc[20:24, 0:416],
               tag="ccz2")
        em.act(cdt2[:, :], offc[:, :], AF.Exp)
        em.dma("sp", ccin[0:128, :], S_f[:, :], tag="ccsf")
        em.dma("sp", ccin[128:256, :], S_b[:, :], tag="ccsb")
        em.dma("sp", ccin[256:257, 0:32].rearrange("a (b c) -> (a b) c", b=32),
               cdt2[:, 0:1], tag="cccdf", slow=True)
        em.dma("sp", ccin[256:257, 32:64].rearrange("a (b c) -> (a b) c", b=32),
               cdt2[:, 1:2], tag="cccdb", slow=True)
        em.dma("pool", ccin[256:257, 64:96], eact[127:128, 31 * 64:31 * 64 + 32],
               tag="cccf")
        em.dma("pool", ccin[256:257, 96:128], eact[0:1, 32:64], tag="cccb")
        em.dma("pool", ccin[256:257, 128:256].rearrange(
                   "a (b c) -> (a b) c", b=128),
               xogfm[17 * 128:18 * 128, 4095:4096], tag="ccCf", slow=True)
        em.dma("pool", ccin[256:257, 256:384].rearrange(
                   "a (b c) -> (a b) c", b=128),
               xogfm[19 * 128:20 * 128, 0:1], tag="ccCb", slow=True)
        em.dma("pool", ccin[257:258, :], ylf[T:T + 1, :], tag="ccyf")
        em.dma("pool", ccin[258:259, :], ylb[1:2, :], tag="ccyb")
        if DBG:
            em.dma("sp", outs["dbg_dt"], dtfmbuf[:, :], tag="dbgdt")
            em.dma("sp", outs["dbg_sf"], S_f[:, :], tag="dbgsf")
            em.dma("sp", outs["dbg_sb"], S_b[:, :], tag="dbgsb")
            em.dma("sp", outs["dbg_fd"], fd_fm[:, :], tag="dbgfd")
            em.dma("sp", outs["dbg_eact"], eact[:, :], tag="dbgea")


    def p3x(stk, eact):
        onesf = SB(stk, "onesf", [1, 128])
        sgt = SB(stk, "sgt", [128, DI])
        kf = [SB(stk, f"kf{j}", [128, DI]) for j in (1, 2, 3)]
        kb = [SB(stk, f"kb{j}", [128, DI]) for j in (0, 1, 2)]
        carryb16 = {n: SB(stk, "cb16_" + n, [128, DI], BF16)
                    for n in ("f", "fp", "b", "bn")}
        cdflat2 = SB(stk, "cdflat2", [1, 32])
        cdbc2 = SB(stk, "cdbc2", [128, 32])
        selo = SB(stk, "selo", [128, 4])
        selp = SB(stk, "selp", [128, 4])
        seln = SB(stk, "seln", [128, 4])
        ytile = SB(stk, "ytile", [1, 4 * DI], BF16)
        yacc = SB(stk, "yacc", [1, DI])
        cmat = SB(stk, "cmat", [128, 4])
        cw = SB(stk, "cw", [128, 4])
        csel = SB(stk, "csel", [128, 1], BF16)
        cself = SB(stk, "cself", [128, 1])
        cfl = SB(stk, "cfl", [1, 4 * 32])
        cflacc = SB(stk, "cflacc", [1, 32])
        padrow = SB(stk, "padrow", [1, DI])
        ylt4 = [[SB(stk, f"ylt{x}{d}", [128, DI], BF16) for d in range(2)]
                for x in range(2)]
        yout4 = [[SB(stk, f"yout{x}{d}", [128, DI], BF16) for d in range(2)]
                 for x in range(2)]
        cchk2 = [SB(stk, f"cchk{d}", [128, 128], BF16) for d in range(2)]
        gsca2 = [SB(stk, f"gsca{d}", [128, 1024], BF16) for d in range(2)]
        coefT2 = [SB(stk, f"coefT{d}", [128, 32]) for d in range(2)]
        wzr = SB(stk, "wzr", [128, 8 * DI], BF16)
        zsl = [SB(stk, f"zsl{x}", [128, DI], BF16) for x in range(2)]
        xzt2 = [SB(stk, f"xzt{x}", [128, 8 * 128], BF16) for x in range(2)]
        pA = PS(stk, "pA3", [128, 2048])
        pB = PS(stk, "pB3", [128, 512])
        pAz = PS(stk, "pAz", [128, 1024])
        em.memset(onesf[:, :], 1.0)
        em.dma("sp", selo[:, :], din["selobc"])
        em.dma("sp", selp[:, :], din["selpbc"])
        em.dma("sp", seln[:, :], din["selnbc"])
        em.collective("AllGather", ALU.bypass, [[0, 1, 2, 3], [4, 5, 6, 7]],
                      [ccin], [ccout], tag="ag")
        for k in range(8):
            em.dma("pool", wzr[:, k * DI:(k + 1) * DI],
                   din["ipwT"][k * 128:(k + 1) * 128, 0:DI], tag="wz")

        def z_sub(i):
            p = i % 2
            em.dma("sp", xzt2[p][:, :].rearrange("p (k c) -> p k c", k=8),
                   xln2fm[:, 4 + i * 128:4 + (i + 1) * 128].rearrange(
                       "(k p) c -> p k c", p=128), tag=f"zx{i}")
            for q in range(4):
                ph = pAz[:, (q % 2) * 512:(q % 2) * 512 + 512]
                for k in range(8):
                    em.mm(ph, xzt2[p][:, k * 128:(k + 1) * 128],
                          wzr[:, k * DI + q * 512:k * DI + (q + 1) * 512],
                          start=(k == 0), stop=(k == 7))
                em.act(zsl[p][:, q * 512:(q + 1) * 512], ph, AF.Silu)
            em.dma("act", zbuf[i * 128:(i + 1) * 128, :], zsl[p][:, :],
                   tag=f"zst{i}")

        for zi in range(20):
            z_sub(zi)

        def cdrep_load(j, col0):
            em.dma("sp", cdflat2[:, :], ccout[j * 264 + 256:j * 264 + 257,
                                              col0:col0 + 32], tag="cdl")
            em.mm(pB[:, 0:32], onesf[:, :], cdflat2[:, :], start=True,
                  stop=True)
            em.copy(cdbc2[:, :], pB[:, 0:32])

        def chain_step(dst, src_k, sg_row, j, col0):
            em.dma("sp", sgt[:, :], ccout[sg_row:sg_row + 128, :], tag="sgl")
            cdrep_load(j, col0)
            for h in range(32):
                em.ts(dst[:, h * 64:(h + 1) * 64],
                      src_k[:, h * 64:(h + 1) * 64],
                      cdbc2[:, h:h + 1], None, ALU.mult)
            em.tt(dst[:, :], dst[:, :], sgt[:, :], ALU.add)

        em.dma("sp", kf[0][:, :], ccout[0 * 264:0 * 264 + 128, :], tag="sg0")
        chain_step(kf[1], kf[0], 1 * 264, 1, 0)
        chain_step(kf[2], kf[1], 2 * 264, 2, 0)
        em.dma("sp", kb[2][:, :], ccout[3 * 264 + 128:3 * 264 + 256, :],
               tag="sg3")
        chain_step(kb[1], kb[2], 2 * 264 + 128, 2, 32)
        chain_step(kb[0], kb[1], 1 * 264 + 128, 1, 32)

        def select3(dst, ks, selcol, idxs):
            em.ts(dst[:, :], ks[0][:, :], selcol[:, idxs[0]:idxs[0] + 1],
                  None, ALU.mult)
            for k, j in zip(ks[1:], idxs[1:]):
                em.stt(dst[:, :], k[:, :], selcol[:, j:j + 1], dst[:, :],
                       ALU.mult, ALU.add)

        select3(carryb16["f"], kf, selo, [1, 2, 3])
        select3(carryb16["fp"], kf, selp, [1, 2, 3])
        select3(carryb16["b"], kb, selo, [0, 1, 2])
        select3(carryb16["bn"], kb, seln, [0, 1, 2])

        def pad_assemble(ybufrow, yrow_base, cf_col0, cc_col0, selt, cname,
                         tag):
            for j in range(4):
                em.dma("pool", ytile[:, j * DI:(j + 1) * DI],
                       ccout[j * 264 + yrow_base:j * 264 + yrow_base + 1, :],
                       tag=tag + "y")
                em.dma("sp", cfl[:, j * 32:(j + 1) * 32],
                       ccout[j * 264 + 256:j * 264 + 257,
                             cf_col0:cf_col0 + 32], tag=tag + "c")
                em.dma("sp", cmat[:, j:j + 1],
                       ccout[j * 264 + 256:j * 264 + 257,
                             cc_col0:cc_col0 + 128].rearrange(
                                 "a (b c) -> (a b) c", b=128), tag=tag + "C",
                       slow=True)
            em.ts(yacc[:, :], ytile[:, 0:DI], selt[0:1, 0:1], None, ALU.mult)
            for j in range(1, 4):
                em.stt(yacc[:, :], ytile[:, j * DI:(j + 1) * DI],
                       selt[0:1, j:j + 1], yacc[:, :], ALU.mult, ALU.add)
            em.ts(cflacc[:, :], cfl[:, 0:32], selt[0:1, 0:1], None, ALU.mult)
            for j in range(1, 4):
                em.stt(cflacc[:, :], cfl[:, j * 32:(j + 1) * 32],
                       selt[0:1, j:j + 1], cflacc[:, :], ALU.mult, ALU.add)
            em.tt(cw[:, :], cmat[:, :], selt[:, :], ALU.mult)
            em.reduce(cself[:, :], cw[:, :], ALU.add)
            em.copy(csel[:, :], cself[:, :])
            for j in range(4):
                em.mm(pA[0:1, j * 512:(j + 1) * 512], csel[:, :],
                      carryb16[cname][:, j * 512:(j + 1) * 512],
                      start=True, stop=True)
            for h in range(32):
                em.stt(padrow[:, h * 64:(h + 1) * 64],
                       pA[0:1, h * 64:(h + 1) * 64],
                       cflacc[0:1, h:h + 1], yacc[:, h * 64:(h + 1) * 64],
                       ALU.mult, ALU.add)
            em.dma("pool", ybufrow, padrow[:, :], tag=tag + "st")

        pad_assemble(ylf[0:1, :], 257, 64, 128, selp, "fp", "padf")
        pad_assemble(ylb[T + 1:T + 2, :], 258, 96, 256, seln, "bn", "padb")

        def p4a_stage(i, fwd, st):
            d = 0 if fwd else 1
            ybuf = ylf if fwd else ylb
            fcC = 17 if fwd else 19
            cn = "f" if fwd else "b"
            dircol = 0 if fwd else 32
            pAd = pA[:, d * 1024:(d + 1) * 1024]
            if st == 0:
                em.dma("sp", cchk2[d][:, :],
                       xogfm[fcC * 128:(fcC + 1) * 128,
                             i * 128:(i + 1) * 128], tag=f"c4{i}{d}")
                em.dma("sp", ylt4[i % 2][d][:, :],
                       ybuf[i * 128 + 1:i * 128 + 129, :], tag=f"yl4{i}{d}")
                em.copy(coefT2[d][:, :], eact[:, i * 64 + dircol:
                                              i * 64 + dircol + 32])
            else:
                for w in range(2):
                    for j in range(2):
                        em.mm(pAd[:, j * 512:(j + 1) * 512], cchk2[d][:, :],
                              carryb16[cn][:, w * 1024 + j * 512:
                                           w * 1024 + (j + 1) * 512],
                              start=True, stop=True)
                    if w == 0:
                        for h in range(16):
                            em.stt(yout4[i % 2][d][:, h * 64:(h + 1) * 64],
                                   pAd[:, h * 64:(h + 1) * 64],
                                   coefT2[d][:, h:h + 1],
                                   ylt4[i % 2][d][:, h * 64:(h + 1) * 64],
                                   ALU.mult, ALU.add)
                    else:
                        for h in range(8):
                            hh = 16 + h
                            em.act(gsca2[d][:, h * 64:(h + 1) * 64],
                                   pAd[:, h * 64:(h + 1) * 64], AF.Copy,
                                   scale=coefT2[d][:, hh:hh + 1])
                        em.tt(yout4[i % 2][d][:, 1024:1536],
                              ylt4[i % 2][d][:, 1024:1536],
                              gsca2[d][:, 0:512], ALU.add)
                        for h in range(8, 16):
                            hh = 16 + h
                            em.stt(yout4[i % 2][d][:, hh * 64:(hh + 1) * 64],
                                   pAd[:, h * 64:(h + 1) * 64],
                                   coefT2[d][:, hh:hh + 1],
                                   ylt4[i % 2][d][:, hh * 64:(hh + 1) * 64],
                                   ALU.mult, ALU.add)
                em.dma("act", ybuf[i * 128 + 1:i * 128 + 129, :],
                       yout4[i % 2][d][:, :], tag=f"yw4{i}{d}")
        p4a_stage(0, True, 0)
        p4a_stage(0, False, 0)
        for i in range(NCH):
            p4a_stage(i, True, 1)
            if i + 1 < NCH:
                p4a_stage(i + 1, True, 0)
            p4a_stage(i, False, 1)
            if i + 1 < NCH:
                p4a_stage(i + 1, False, 0)
            if 20 + i < NCH:
                z_sub(20 + i)

    def p4m(stk, fd_fm):
        opwb = SB(stk, "opwb", [128, 16 * D], BF16)
        nwc = SB(stk, "nwc", [128, 16])
        drept = SB(stk, "drept", [128, DI], BF16)
        d32 = SB(stk, "d32", [128, 32])
        m2 = SB(stk, "m2", [128, 2])
        x1t = SB(stk, "x1t4", [128, NG * D])
        yft = SB(stk, "yft", [128, NG * DI], BF16)
        ybt = SB(stk, "ybt", [128, NG * DI], BF16)
        xm1 = SB(stk, "xm1", [128, NG * DI], BF16)
        xct = SB(stk, "xct", [128, NG * DI], BF16)
        xp1 = SB(stk, "xp1", [128, NG * DI], BF16)
        zst = SB(stk, "zst", [128, NG * DI], BF16)
        indhp4 = SB(stk, "indhp4", [32, DI], BF16)
        drow = SB(stk, "drow", [1, DI], BF16)
        ones1 = SB(stk, "ones1", [1, 128], BF16)
        fdd2 = [SB(stk, f"fdd{g}", [128, DI], BF16)
                for g in range(NG)]
        yga = SB(stk, "yga", [128, NG * DI], BF16)
        ygb2 = [SB(stk, f"ygb{g}", [128, DI], BF16)
                for g in range(NG)]
        ygbf = SB(stk, "ygbf", [128, NG * DI], BF16)
        sqc = SB(stk, "sqc4", [128, 8])
        st2 = SB(stk, "st24", [128, 8])
        st3 = SB(stk, "st34", [128, 8])
        rstd = SB(stk, "rstd4", [128, 8])
        sq_scr = SB(stk, "sq_scr4", [128, DI])
        ynfm = SB(stk, "ynfm", [128, 16 * 256], BF16)
        mixt = SB(stk, "mixt", [128, NG * D])
        x2t = SB(stk, "x2t", [128, NG * D])
        ptT = PS(stk, "ptT4", [128, 2048], BF16)
        pob = [PS(stk, f"po{i}_4", [128, 512]) for i in range(2)]
        pm4 = PS(stk, "pm4", [128, 2048])

        for k in range(16):
            em.dma("pool", opwb[:, k * D:(k + 1) * D],
                   din["opwT"][k * 128:(k + 1) * 128, :], tag="opw")
        em.dma("sp", nwc[:, :], din["normw"].rearrange("(c p) -> p c", p=128),
               slow=True)
        em.dma("pool", drept[:, :], din["drep"])
        em.dma("pool", drow[:, :], din["drep"][0:1, :])
        em.memset(ones1[:, :], 1.0)
        em.dma("sp", indhp4[:, :], din["indhp"])
        em.dma("sp", m2[:, :], din["mask2"])
        for k in range(16):
            em.ts(opwb[:, k * D:(k + 1) * D], opwb[:, k * D:(k + 1) * D],
                  nwc[:, k:k + 1], None, ALU.mult)

        def rms_pair(src, xout):
            for g in range(NG):
                gs = src[:, g * DI:(g + 1) * DI]
                em.act(sq_scr[:, :], gs, AF.Square, accum_out=sqc[:, g:g + 1])
            sl = slice(0, NG)
            em.ts(st2[:, sl], sqc[:, sl], 1.0 / DI, EPS, ALU.mult, ALU.add)
            em.rsqrt(rstd[:, sl], st2[:, sl], st3[:, sl])
            for g in range(NG):
                em.ts(xout[:, g * DI:(g + 1) * DI],
                      src[:, g * DI:(g + 1) * DI], rstd[:, g:g + 1], None,
                      ALU.mult)

        for pr in range(16):
            i0 = pr * 2
            em.dma("sp", x1t[:, :].rearrange("p (g d) -> p g d", g=NG),
                   x1buf[4 + i0 * 128:4 + (i0 + 2) * 128, :].rearrange(
                       "(g p) d -> p g d", p=128), tag=f"x1l{pr}")
            em.dma("sp", yft[:, :].rearrange("p (g d) -> p g d", g=NG),
                   ylf[i0 * 128:(i0 + 2) * 128, :].rearrange(
                       "(g p) d -> p g d", p=128), tag=f"yfl{pr}")
            em.dma("sp", ybt[:, :].rearrange("p (g d) -> p g d", g=NG),
                   ylb[i0 * 128 + 2:(i0 + 2) * 128 + 2, :].rearrange(
                       "(g p) d -> p g d", p=128), tag=f"ybl{pr}")
            em.dma("sp", xm1[:, :].rearrange("p (g d) -> p g d", g=NG),
                   xogbuf[i0 * 128:(i0 + 2) * 128, :].rearrange(
                       "(g p) d -> p g d", p=128), tag=f"xml{pr}")
            em.dma("sp", xct[:, :].rearrange("p (g d) -> p g d", g=NG),
                   xogbuf[i0 * 128 + 1:(i0 + 2) * 128 + 1, :].rearrange(
                       "(g p) d -> p g d", p=128), tag=f"xcl{pr}")
            em.dma("sp", xp1[:, :].rearrange("p (g d) -> p g d", g=NG),
                   xogbuf[i0 * 128 + 2:(i0 + 2) * 128 + 2, :].rearrange(
                       "(g p) d -> p g d", p=128), tag=f"xpl{pr}")
            em.dma("sp", zst[:, :].rearrange("p (g d) -> p g d", g=NG),
                   zbuf[i0 * 128:(i0 + 2) * 128, :].rearrange(
                       "(g p) d -> p g d", p=128), tag=f"zl{pr}")
            if pr == 0:
                em.ts(xm1[:, 0:DI], xm1[:, 0:DI], m2[:, 0:1], None, ALU.mult)
            if pr == 15:
                em.ts(xp1[:, DI:2 * DI], xp1[:, DI:2 * DI], m2[:, 1:2], None,
                      ALU.mult)
            for g in range(NG):
                i = i0 + g
                ys = slice(g * DI, (g + 1) * DI)
                ygb = ygb2[g]
                fdd = fdd2[g]
                for w in range(2):
                    ph = pm4[:, (g % 2) * 1024:(g % 2) * 1024 + 1024]
                    for j in range(2):
                        jj = w * 2 + j
                        em.mm(ph[:, j * 512:(j + 1) * 512],
                              fd_fm[:, i * 128:(i + 1) * 128],
                              indhp4[0:32, jj * 512:(jj + 1) * 512],
                              start=True, stop=False)
                        em.mm(ph[:, j * 512:(j + 1) * 512], ones1[:, :],
                              drow[:, jj * 512:(jj + 1) * 512], start=False,
                              stop=True)
                    em.act(fdd[:, w * 1024:(w + 1) * 1024], ph[:, :],
                           AF.Copy)
                em.tt(yga[:, ys], yft[:, ys], ybt[:, ys], ALU.add)
                em.tt(ygb[:, :], xm1[:, ys], xp1[:, ys], ALU.add)
                em.tt(ygb[:, :], ygb[:, :], drept[:, :], ALU.mult)
                em.tt(yga[:, ys], yga[:, ys], ygb[:, :], ALU.add)
                em.tt(ygb[:, :], fdd[:, :], xct[:, ys], ALU.mult)
                em.tt(yga[:, ys], yga[:, ys], ygb[:, :], ALU.add)
                em.tt(yga[:, ys], yga[:, ys], zst[:, ys], ALU.mult)
            rms_pair(yga, ygbf)
            for g in range(NG):
                for k in range(16):
                    em.transpose(ptT[:, k * 128:(k + 1) * 128],
                                 ygbf[:, g * DI + k * 128:
                                      g * DI + (k + 1) * 128], identb[:, :])
                em.copy(ynfm[:, :].rearrange("p (k r) -> p k r", k=16)
                        [:, :, g * 128:(g + 1) * 128],
                        ptT[:, :].rearrange("p (k r) -> p k r", k=16))
            for do in range(8):
                bank = pob[do % 2]
                for k in range(16):
                    em.mm(bank[:, 0:256],
                          opwb[:, k * D + do * 128:k * D + (do + 1) * 128],
                          ynfm[:, k * 256:(k + 1) * 256],
                          start=(k == 0), stop=(k == 15))
                em.copy(mixt[:, :].rearrange("p (g e) -> p g e", g=NG)
                        [:, :, do * 128:(do + 1) * 128],
                        bank[:, 0:256].rearrange("p (g c) -> p g c", g=NG))
            for g in range(NG):
                for w in range(2):
                    for dd in range(4):
                        em.transpose(
                            pob[w % 2][:, dd * 128:(dd + 1) * 128],
                            mixt[:, g * D + (w * 4 + dd) * 128:
                                 g * D + (w * 4 + dd + 1) * 128],
                            identf[:, :])
                    for dd in range(4):
                        em.copy(x2t[:, g * D + (w * 4 + dd) * 128:
                                    g * D + (w * 4 + dd + 1) * 128],
                                pob[w % 2][:, dd * 128:(dd + 1) * 128])
            for g in range(NG):
                em.tt(x2t[:, g * D:(g + 1) * D], x2t[:, g * D:(g + 1) * D],
                      x1t[:, g * D:(g + 1) * D], ALU.add)
            em.dma("act",
                   x2buf[i0 * 128:(i0 + 2) * 128, :].rearrange(
                       "(g p) d -> p g d", p=128),
                   x2t[:, :].rearrange("p (g d) -> p g d", g=NG),
                   tag=f"x2st{pr}")

    def p4f(stk):
        w1b = SB(stk, "w1b4", [128, 8 * DFF], BF16)
        w2b = SB(stk, "w2b4", [128, 32 * D], BF16)
        b1c = SB(stk, "b1c4", [128, 32])
        b2c = SB(stk, "b2c4", [128, 8])
        thetac = SB(stk, "thetac4", [128, 32])
        thetab = SB(stk, "thetab4", [128, 32])
        coefc = SB(stk, "coefc4", [128, 32])
        ut = SB(stk, "ut4", [128, NG * D])
        sumc = SB(stk, "sumc4", [128, 8])
        sqc = SB(stk, "sqc4f", [128, 8])
        st1 = SB(stk, "st14", [128, 8])
        st2 = SB(stk, "st24f", [128, 8])
        st3 = SB(stk, "st34f", [128, 8])
        rstd = SB(stk, "rstd4f", [128, 8])
        nmr = SB(stk, "nmr4", [128, 8])
        sq_scr = SB(stk, "sq_scr4f", [128, D])
        xlnt = SB(stk, "xlnt4", [128, NG * D], BF16)
        xfm = SB(stk, "xfm4", [128, 8 * TILE], BF16)
        aa = SB(stk, "aa4", [128, TILE], BF16)
        kfl = SB(stk, "kfl4", [128, TILE], BF16)
        sargt = SB(stk, "sargt4", [128, TILE], BF16)
        stile = SB(stk, "stile4", [128, TILE])
        qt = SB(stk, "qt4", [128, TILE], BF16)
        hact = SB(stk, "hact4", [128, 32 * TILE], BF16)
        o2t = [SB(stk, f"o2t{i}_4", [128, TILE], BF16) for i in range(2)]
        outt = SB(stk, "outt", [128, NG * D])
        phs = SB(stk, "phs4", [128, 32])
        wrs = SB(stk, "wrs4", [128, 32])
        wis = SB(stk, "wis4", [128, 32])
        t32a = SB(stk, "t32a4", [128, 32])
        t32b = SB(stk, "t32b4", [128, 32])
        ptT = PS(stk, "ptT4f", [128, 2048], BF16)
        pfb = [PS(stk, f"pf{i}_4", [128, 512]) for i in range(3)]
        pob = [PS(stk, f"po{i}_4f", [128, 512]) for i in range(2)]

        for k in range(8):
            em.dma("pool", w1b[:, k * DFF:(k + 1) * DFF],
                   din["w1"][k * 128:(k + 1) * 128, :], tag="w1r")
        for k in range(8):
            em.dma("pool",
                   w2b[:, k * 4 * D:(k + 1) * 4 * D].rearrange(
                       "p (c n) -> p c n", c=4),
                   din["w2"].rearrange("(c p) n -> p c n",
                                       p=128)[:, 4 * k:4 * k + 4, :],
                   tag="w2r")
        em.dma("sp", b1c[:, :], din["b1"].rearrange("(c p) -> p c", p=128),
               slow=True)
        em.dma("sp", b2c[:, :], din["b2"].rearrange("(c p) -> p c", p=128),
               slow=True)
        em.dma("sp", phs[:, :], din["phases"].rearrange("(c p) -> p c", p=128),
               slow=True)
        em.dma("sp", wrs[:, :], din["wreal"].rearrange("(c p) -> p c", p=128),
               slow=True)
        em.dma("sp", wis[:, :], din["wimag"].rearrange("(c p) -> p c", p=128),
               slow=True)
        em.copy(thetac[:, :], phs[:, :])
        for _ in range(6):
            em.wrap1(thetac[:, :], thetac[:, :], t32a[:, :], t32b[:, :])
        em.act(t32a[:, :], thetac[:, :], AF.Sin)
        em.act(t32b[:, :], thetac[:, :], AF.Sin, scale=0.5)
        em.tt(t32b[:, :], t32b[:, :], t32b[:, :], ALU.mult)
        em.ts(t32b[:, :], t32b[:, :], -2.0, 1.0, ALU.mult, ALU.add)
        em.tt(t32a[:, :], t32a[:, :], wis[:, :], ALU.mult)
        em.tt(t32b[:, :], t32b[:, :], wrs[:, :], ALU.mult)
        em.tt(t32a[:, :], t32a[:, :], t32b[:, :], ALU.add)
        em.tt(t32a[:, :], t32a[:, :], wis[:, :], ALU.mult)
        em.ts(coefc[:, :], t32a[:, :], 0.1, None, ALU.mult)
        em.stt(thetab[:, :], b1c[:, :], 0.1, thetac[:, :], ALU.mult, ALU.add)

        for t in range(16):
            R0 = t * TILE
            em.dma("sp", ut[:, :].rearrange("p (g d) -> p g d", g=NG),
                   x2buf[R0:R0 + TILE, :].rearrange("(g p) d -> p g d",
                                                    p=128), tag=f"p4u{t}")
            for g in range(NG):
                gs = ut[:, g * D:(g + 1) * D]
                em.act(sq_scr[:, :], gs, AF.Square,
                       accum_out=sqc[:, g:g + 1])
                em.reduce(sumc[:, g:g + 1], gs, ALU.add)
            sl = slice(0, NG)
            em.ts(st1[:, sl], sumc[:, sl], 1.0 / D, None, ALU.mult)
            em.ts(st2[:, sl], sqc[:, sl], 1.0 / D, None, ALU.mult)
            em.tt(st3[:, sl], st1[:, sl], st1[:, sl], ALU.mult)
            em.tt(st2[:, sl], st2[:, sl], st3[:, sl], ALU.subtract)
            em.ts(st2[:, sl], st2[:, sl], EPS, None, ALU.add)
            em.rsqrt(rstd[:, sl], st2[:, sl], st3[:, sl])
            em.tt(nmr[:, sl], st1[:, sl], rstd[:, sl], ALU.mult)
            em.ts(nmr[:, sl], nmr[:, sl], -1.0, None, ALU.mult)
            for g in range(NG):
                em.ts(xlnt[:, g * D:(g + 1) * D], ut[:, g * D:(g + 1) * D],
                      rstd[:, g:g + 1], nmr[:, g:g + 1], ALU.mult, ALU.add)
            for k in range(8):
                for g in range(NG):
                    em.transpose(
                        ptT[:, (k * NG + g) * 128:(k * NG + g + 1) * 128],
                        xlnt[:, g * D + k * 128: g * D + (k + 1) * 128],
                        identb[:, :])
            dap = xfm[:, :]
            sap = ptT[:, :]
            em.copy(bass.AP(dap.tensor, 0, [list(dap.ap[0]), [TILE, 8],
                                            [128, NG], [1, 128]]),
                    bass.AP(sap.tensor, 0, [list(sap.ap[0]), [NG * 128, 8],
                                            [128, NG], [1, 128]]))
            for f in range(32):
                bank = pfb[f % 3]
                for k in range(8):
                    em.mm(bank[:, 0:TILE],
                          w1b[:, k * DFF + f * 128: k * DFF + (f + 1) * 128],
                          xfm[:, k * TILE:(k + 1) * TILE],
                          start=(k == 0), stop=(k == 7))
                em.ts(aa[:, :], bank[:, 0:TILE], 0.1, thetab[:, f:f + 1],
                      ALU.mult, ALU.add)
                em.wrap1(sargt[:, :], aa[:, :], kfl[:, :], qt[:, :])
                em.act(stile[:, :], sargt[:, :], AF.Sin)
                em.stt(qt[:, :], stile[:, :], coefc[:, f:f + 1],
                       bank[:, 0:TILE], ALU.mult, ALU.add)
                em.act(hact[:, f * TILE:(f + 1) * TILE], qt[:, :], AF.Relu,
                       bias=b1c[:, f:f + 1])
            for do in range(8):
                bank = pob[do % 2]
                for kf2 in range(32):
                    em.mm(bank[:, 0:TILE],
                          w2b[:, kf2 * D + do * 128: kf2 * D + (do + 1) * 128],
                          hact[:, kf2 * TILE:(kf2 + 1) * TILE],
                          start=(kf2 == 0), stop=(kf2 == 31))
                em.ts(o2t[do % 2][:, :], bank[:, 0:TILE], b2c[:, do:do + 1],
                      None, ALU.add)
                for g in range(NG):
                    em.transpose(ptT[:, (do % 2) * 1024 + g * 128:
                                     (do % 2) * 1024 + (g + 1) * 128],
                                 o2t[do % 2][:, g * 128:(g + 1) * 128],
                                 identb[:, :])
                for g in range(NG):
                    em.stt(outt[:, g * D + do * 128: g * D + (do + 1) * 128],
                           ptT[:, (do % 2) * 1024 + g * 128:
                               (do % 2) * 1024 + g * 128 + 128],
                           0.5,
                           ut[:, g * D + do * 128: g * D + (do + 1) * 128],
                           ALU.mult, ALU.add)
            em.dma("act",
                   out[R0:R0 + TILE, :].rearrange("(g p) d -> p g d", p=128),
                   outt[:, :].rearrange("p (g d) -> p g d", g=NG),
                   tag=f"outst{t}")

    midstack = contextlib.ExitStack()
    fd_fm = SB(midstack, "fd_fm", [32, T], BF16)
    eact = SB(midstack, "eact", [128, NCH * 64], BF16)
    if "P1" in phases:
        with contextlib.ExitStack() as stk:
            p1(stk)
            em.barrier()
    if "P2" in phases:
        with contextlib.ExitStack() as stk:
            p2(stk, fd_fm, eact)
            em.barrier()
    if "P4" in phases:
        with contextlib.ExitStack() as stk:
            p3x(stk, eact)
            em.barrier()
        with contextlib.ExitStack() as stk:
            p4m(stk, fd_fm)
            em.barrier()
        with contextlib.ExitStack() as stk:
            p4f(stk)
            em.barrier()
    midstack.close()

    if DBG:
        em.dma("sp", outs["dbg_x1"], x1buf, tag="dbgx1")
        em.dma("sp", outs["dbg_xln2fm"], xln2fm, tag="dbgxln")
        if "P2" in phases:
            em.dma("sp", outs["dbg_xog"], xogbuf, tag="dbgxog")
            em.dma("sp", outs["dbg_ylf"], ylf[1:T + 1, :], tag="dbgylf")
            em.dma("sp", outs["dbg_ylb"], ylb[1:T + 1, :], tag="dbgylb")
    if "P4" not in phases:
        em.dma("sp", out, x1buf[4:4 + T, :], tag="outplaceholder")

    with contextlib.ExitStack() as semstack:
        sems = {}
        for n in ("pe", "act", "dve", "pool", "sp"):
            sems[n] = semstack.enter_context(nc.semaphore("s_" + n))
        for n in em.dma_sem_names:
            sems[n] = semstack.enter_context(nc.semaphore("s_" + n))
        block = semstack.enter_context(nc.Block())
        em.emit(block, sems)
    topstack.close()
    return nc



BF = ml_dtypes.bfloat16
L = 16384


def aux_constants():
    c = {}
    c["identb"] = np.eye(128, dtype=BF)
    c["identf"] = np.eye(128, dtype=np.float32)
    tri = np.zeros((128, 128), np.float32)
    for s in range(128):
        tri[s, s:] = 1.0
    c["triu"] = tri
    c["tril"] = np.ascontiguousarray(tri.T)
    c["maskF"] = tri.astype(BF)
    c["maskB"] = np.ascontiguousarray(tri.T).astype(BF)
    indh = np.zeros((H, DFF), np.float32)
    for h in range(H):
        indh[h, h * 128:(h + 1) * 128] = 1.0
    c["indh"] = indh.astype(BF)
    indhp = np.zeros((H, DI), np.float32)
    for h in range(H):
        indhp[h, h * 64:(h + 1) * 64] = 1.0
    c["indhp"] = indhp.astype(BF)
    c["zer32"] = np.zeros((32, 128), BF)
    c["neg3"] = np.full((3, 128), -1.0, BF)
    return c


def prep_inputs(inp):
    u = np.asarray(inp["L0_attn_u"])
    ipw = np.asarray(inp["L0_attn_in_proj_w"])
    aux = aux_constants()
    ipwT = np.ascontiguousarray(ipw.T)
    opwT = np.ascontiguousarray(np.asarray(inp["L0_attn_out_proj_w"]).T)
    fcdT = np.ascontiguousarray(np.asarray(inp["L0_attn_fc_D_w"]).T)
    convw = np.ascontiguousarray(
        np.asarray(inp["L0_attn_conv_w"]).reshape(CONVDIM, 4))
    dtb = np.asarray(inp["L0_attn_dt_bias"])
    dtbias2 = np.ascontiguousarray(np.concatenate([dtb, dtb]))
    drep = np.ascontiguousarray(np.broadcast_to(
        np.repeat(np.asarray(inp["L0_attn_D"]), 64)[None, :], (128, DI))
    ).astype(np.float32)

    d32bc = np.ascontiguousarray(np.broadcast_to(
        np.asarray(inp["L0_attn_D"], np.float32)[None, :], (128, H)))
    shared = dict(
        d32bc=d32bc,
        w1=np.asarray(inp["L0_ffn_W1"]), w2=np.asarray(inp["L0_ffn_W2"]),
        b1=np.asarray(inp["L0_ffn_b1"]), b2=np.asarray(inp["L0_ffn_b2"]),
        phases=np.asarray(inp["L0_ffn_quantum_phases"]),
        wreal=np.asarray(inp["L0_ffn_quantum_weights_real"]),
        wimag=np.asarray(inp["L0_ffn_quantum_weights_imag"]),
        ipwT=ipwT, opwT=opwT, fcdT=fcdT, convw=convw,
        convb=np.asarray(inp["L0_attn_conv_b"]),
        dtbias2=dtbias2, alog=np.asarray(inp["L0_attn_A_log"]),
        dskip=np.asarray(inp["L0_attn_D"]),
        normw=np.asarray(inp["L0_attn_norm_w"]),
        drep=drep,
        **aux,
    )
    in_maps = []
    for c in range(8):
        b, s = c // 4, c % 4
        a = s * T
        usl = np.zeros((USL, D), np.float32)
        lo = a - 4
        lo2, hi2 = max(lo, 0), min(lo + USL, L)
        usl[lo2 - lo: hi2 - lo] = u[b, lo2:hi2]
        lnmask = np.ones(4352, np.float32)
        if s == 0:
            lnmask[0:4] = 0.0
        if s == 3:
            lnmask[4100:] = 0.0
        lnmask[4101:] = 0.0
        gown = np.zeros(8, np.float32)
        gprev = np.zeros(8, np.float32)
        for j in range(4):
            gown[2 * j] = 1.0 if j < s else 0.0
            gown[2 * j + 1] = 1.0 - gown[2 * j]
            gprev[2 * j] = 1.0 if j < s - 1 else 0.0
            gprev[2 * j + 1] = 1.0 - gprev[2 * j]
        selown = np.zeros(4, np.float32)
        selown[s] = 1.0
        selprev = np.zeros(4, np.float32)
        if s > 0:
            selprev[s - 1] = 1.0
        selnxt = np.zeros(4, np.float32)
        if s < 3:
            selnxt[s + 1] = 1.0
        mask2 = np.ones((128, 2), np.float32)
        if s == 0:
            mask2[0, 0] = 0.0
        if s == 3:
            mask2[127, 1] = 0.0
        m = dict(shared)
        m.update(u=usl, lnmask=lnmask, gown=gown, gprev=gprev,
                 selown=selown, selprev=selprev, selnxt=selnxt,
                 selobc=np.ascontiguousarray(
                     np.broadcast_to(selown, (128, 4))),
                 selpbc=np.ascontiguousarray(
                     np.broadcast_to(selprev, (128, 4))),
                 selnbc=np.ascontiguousarray(
                     np.broadcast_to(selnxt, (128, 4))),
                 mask2=mask2)
        in_maps.append(m)
    return in_maps


_CACHE = {}


def kernel(**inputs):
    if "nc" not in _CACHE:
        _CACHE["nc"] = build_program(phases=("P1", "P2", "P4"))
    nc = _CACHE["nc"]
    in_maps = prep_inputs(inputs)
    from concourse.bass_utils import run_bass_kernel_spmd
    res = run_bass_kernel_spmd(nc, in_maps, core_ids=list(range(8)))
    full = np.zeros((2, L, D), np.float32)
    for c in range(8):
        b, s = c // 4, c % 4
        full[b, s * T:(s + 1) * T] = res.results[c]["out"]
    return full


# version 17
# speedup vs baseline: 1.1944x; 1.0064x over previous
import numpy as np
import ml_dtypes

"""Core emitter + program builder for the mk_macaron_d1 megakernel.

This module is developed in scratch/ and inlined into kernel.py when stable.
"""
import math
import numpy as np
import concourse.bass as bass
import concourse.mybir as mybir

F32 = mybir.dt.float32
BF16 = mybir.dt.bfloat16
I32 = mybir.dt.int32
AF = mybir.ActivationFunctionType
ALU = mybir.AluOpType
AX = mybir.AxisListType

PI = float(np.pi)
INV2PI = float(1.0 / (2.0 * np.pi))
N2PI = float(-2.0 * np.pi)
RSQRT_C = float(0x5F3759DF)


ENGINES = ("pe", "act", "dve", "pool", "sp")


def _ap_region(ap):
    t = ap.tensor
    name = t.name
    dims = [list(d) for d in ap.ap]
    try:
        space = str(ap.space)
    except Exception:
        space = "?"
    try:
        esz = np.dtype(mybir.dt.np(ap.dtype)).itemsize
    except Exception:
        esz = 4
    if "DRAM" in space.upper() or "DRam" in str(type(t)):
        off = ap.offset if isinstance(ap.offset, int) else 0
        ext = 1
        for st, ct in dims:
            ext += abs(st) * (ct - 1)
        return (name, 0, 128, off * esz, (off + ext) * esz)
    off = ap.offset if isinstance(ap.offset, int) else 0
    pitch = abs(dims[0][0]) if dims and dims[0][0] else 1
    p0 = off // pitch if pitch > 0 else 0
    p1 = p0 + ap.partition_size()
    coff = off - p0 * pitch
    ext = 1
    for st, ct in dims[1:]:
        ext += abs(st) * (ct - 1)
    b0, b1 = coff * esz, (coff + ext) * esz
    if "PSUM" in space.upper():
        b0 = (b0 // 2048) * 2048
        b1 = ((b1 + 2047) // 2048) * 2048
        p0, p1 = 0, 128
    return (name, p0, p1, b0, b1)


def _overlap(r1, r2):
    return (r1[0] == r2[0] and r1[1] < r2[2] and r2[1] < r1[2]
            and r1[3] < r2[4] and r2[3] < r1[4])


class Op:
    __slots__ = ("engine", "build", "reads", "writes", "idx", "waits",
                 "carrier", "dma_sem", "tag", "accum")

    def __init__(self, engine, build, reads, writes, tag="", accum=False):
        self.engine = engine
        self.build = build
        self.reads = [_ap_region(a) for a in reads if a is not None]
        self.writes = [_ap_region(a) for a in writes if a is not None]
        self.idx = -1
        self.waits = []
        self.carrier = False
        self.tag = tag
        self.accum = accum


NDMASEM = 12


class Emitter:
    def __init__(self, nc):
        self.nc = nc
        self.ops = []
        self.counts = {e: 0 for e in ENGINES}
        self._writers = {}
        self._readers = {}
        self.dma_sem_names = [f"dq{i}" for i in range(NDMASEM)] + ["dqcc"]
        self.dma_counts = {n: 0 for n in self.dma_sem_names}
        self._dma_rr = 0
        self._dma_rr_sw = 4

    def _record(self, op):
        op.idx = len(self.ops)
        self.ops.append(op)
        eng = op.engine
        pre_dma_wait = None
        if eng.startswith("dma:"):
            q = eng.split(":")[1]
            if q.endswith("!cc"):
                sem = "dqcc"
                if self.dma_counts[sem] > 0:
                    pre_dma_wait = ("dma", sem, self.dma_counts[sem])
                self.dma_counts[sem] += 1
                seq = ("dma", sem, self.dma_counts[sem])
                op.dma_sem = sem
            elif q == "pool":
                sem = self.dma_sem_names[self._dma_rr_sw]
                self._dma_rr_sw = 4 + (self._dma_rr_sw + 1 - 4) % 4
            else:
                sem = self.dma_sem_names[self._dma_rr]
                self._dma_rr = (self._dma_rr + 1) % 4
            if not q.endswith("!cc"):
                if self.dma_counts[sem] > 0:
                    pre_dma_wait = ("dma", sem, self.dma_counts[sem])
                self.dma_counts[sem] += 16
                seq = ("dma", sem, self.dma_counts[sem])
                op.dma_sem = sem
        else:
            self.counts[eng] += 1
            seq = ("eng", eng, self.counts[eng])
            op.dma_sem = None
        waits = set()
        for r in op.reads:
            for (reg, src) in self._writers.get(r[0], []):
                if _overlap(reg, r):
                    waits.add(src)
        for w in op.writes:
            for (reg, src) in self._writers.get(w[0], []):
                if _overlap(reg, w):
                    if op.accum and src[0] == "eng" and src[1] == "pe" \
                            and (eng == "pe"):
                        continue
                    waits.add(src)
            for (reg, src) in self._readers.get(w[0], []):
                if _overlap(reg, w):
                    waits.add(src)
        final = []
        for (kind, key, val) in waits:
            if kind == "eng" and key == "pe" and not eng.startswith("dma:") \
                    and eng == "pe":
                continue
            final.append((kind, key, val))
        if pre_dma_wait is not None:
            final.append(pre_dma_wait)
        op.waits = final
        for w in op.writes:
            lst = self._writers.setdefault(w[0], [])
            lst[:] = [(reg, src) for (reg, src) in lst if not _overlap(reg, w)]
            lst.append((w, seq))
            rl = self._readers.get(w[0])
            if rl:
                rl[:] = [(reg, src) for (reg, src) in rl if not _overlap(reg, w)]
        for r in op.reads:
            self._readers.setdefault(r[0], []).append((r, seq))
        return op

    def op(self, engine, build, reads, writes, tag="", accum=False):
        return self._record(Op(engine, build, reads, writes, tag, accum))

    def barrier(self):
        snap_eng = dict(self.counts)
        snap_dma = dict(self.dma_counts)
        for e in ENGINES:
            waits = [("eng", o, snap_eng[o]) for o in ENGINES
                     if snap_eng[o] > 0]
            waits += [("dma", n, v) for n, v in snap_dma.items() if v > 0]
            op = Op(e, lambda h: h.nop(), [], [], tag="barrier")
            self.counts[e] += 1
            op.dma_sem = None
            op.waits = waits
            op.idx = len(self.ops)
            self.ops.append(op)

    def dma(self, queue, out, in_, tag="", slow=False):
        def build(h):
            if slow:
                return h.dma_start(out=out, in_=in_,
                                   allow_slow_non_contiguous=True)
            return h.dma_start(out=out, in_=in_)
        return self._record(Op("dma:" + queue, build, [in_], [out], tag))

    def emit(self, block, sems):
        eng_wait_targets = {e: set() for e in ENGINES}
        for op in self.ops:
            for (kind, key, val) in op.waits:
                if kind == "eng":
                    eng_wait_targets[key].add(val)
        carrier_val = {}
        for e in ENGINES:
            targets = eng_wait_targets[e]
            cnt = 0
            vals = {}
            for n in range(1, self.counts[e] + 1):
                if n in targets:
                    cnt += 1
                    vals[n] = cnt
                else:
                    vals[n] = None
            carrier_val[e] = vals
        carrier_cum = {}
        for e in ENGINES:
            cum = {}
            c = 0
            for n in range(1, self.counts[e] + 1):
                if carrier_val[e][n] is not None:
                    c += 1
                cum[n] = c
            carrier_cum[e] = cum

        streams = {e: [] for e in ENGINES}
        eng_op_no = {e: 0 for e in ENGINES}
        dma_no = {n: 0 for n in self.dma_sem_names}
        for op in self.ops:
            if op.engine.startswith("dma:"):
                q = op.engine.split(":")[1].replace("!cc", "")
                sem = op.dma_sem
                dma_no[sem] += 1 if sem == "dqcc" else 16
                streams[q].append((op, ("dma", sem, dma_no[sem])))
            else:
                eng_op_no[op.engine] += 1
                streams[op.engine].append((op, ("eng", op.engine,
                                                eng_op_no[op.engine])))

        handles = {
            "pe": self.nc.tensor, "act": self.nc.scalar, "dve": self.nc.vector,
            "pool": self.nc.gpsimd, "sp": self.nc.sync,
        }
        emitter = self

        def run_stream(eng_name, eng):
            floors = {}

            def need_wait(kind, key, val):
                if kind == "eng":
                    v = carrier_cum[key][val]
                    if v == 0:
                        return None
                    sem = sems[key]
                    if floors.get(key, 0) >= v:
                        return None
                    floors[key] = v
                    return (sem, v)
                else:
                    if floors.get(key, 0) >= val:
                        return None
                    floors[key] = val
                    return (sems[key], val)

            for (op, seq) in streams[eng_name]:
                for (kind, key, val) in sorted(op.waits,
                                               key=lambda t: (t[1], t[2])):
                    w = need_wait(kind, key, val)
                    if w is not None:
                        eng.wait_ge(w[0], w[1])
                inst = op.build(eng)
                if seq[0] == "eng":
                    v = carrier_val[eng_name].get(seq[2])
                    if v is not None:
                        inst.then_inc(sems[eng_name], 1)
                else:
                    inst.then_inc(sems[seq[1]], 1 if seq[1] == "dqcc" else 16)
            if eng_name == "sp":
                for n in emitter.dma_sem_names:
                    if emitter.dma_counts[n] > 0:
                        eng.wait_ge(sems[n], emitter.dma_counts[n])

        for name in ("sp", "pe", "act", "dve", "pool"):
            dec = {"sp": block.sync, "pe": block.tensor, "act": block.scalar,
                   "dve": block.vector, "pool": block.gpsimd}[name]

            def make(fn_name, engine_obj):
                def _inner(eng):
                    run_stream(fn_name, eng)
                return _inner
            dec(make(name, handles[name]))

    def mm(self, out, lhsT, rhs, start=True, stop=True, tag=""):
        return self.op("pe",
                       lambda e: e.matmul(out, lhsT, rhs, start=start, stop=stop),
                       [lhsT, rhs], [out], tag, accum=not start)

    def transpose(self, out, in_, ident, tag=""):
        return self.op("pe", lambda e: e.transpose(out, in_, ident),
                       [in_, ident], [out], tag)

    def act(self, out, in_, func, bias=0.0, scale=1.0, accum_out=None, tag=""):
        reads = [in_]
        if isinstance(bias, bass.AP):
            reads.append(bias)
        if isinstance(scale, bass.AP):
            reads.append(scale)
        writes = [out] + ([accum_out] if accum_out is not None else [])
        return self.op("act",
                       lambda e: e.activation(out, in_, func, bias=bias,
                                              scale=scale, accum_out=accum_out),
                       reads, writes, tag)

    def _vec(self, engine, meth, out, ins, scalars, tag, kwargs=None):
        reads = list(ins) + [s for s in scalars if isinstance(s, bass.AP)]
        kw = kwargs or {}

        def build(e):
            return getattr(e, meth)(*([out] + list(ins) + list(scalars)), **kw)
        return self.op(engine, build, reads, [out], tag)

    def tt(self, out, a, b, alu, engine="dve", tag=""):
        return self.op(engine, lambda e: e.tensor_tensor(out, a, b, alu),
                       [a, b], [out], tag)

    def ts(self, out, in_, s1, s2, op0, op1=ALU.bypass, engine="dve", tag="",
           accum_out=None):
        reads = [in_] + [s for s in (s1, s2) if isinstance(s, bass.AP)]
        writes = [out] + ([accum_out] if accum_out is not None else [])
        return self.op(engine,
                       lambda e: e.tensor_scalar(out, in_, s1, s2, op0, op1,
                                                 accum_out=accum_out),
                       reads, writes, tag)

    def stt(self, out, in0, scalar, in1, op0, op1, engine="dve", tag=""):
        reads = [in0, in1] + ([scalar] if isinstance(scalar, bass.AP) else [])
        return self.op(engine,
                       lambda e: e.scalar_tensor_tensor(out, in0, scalar, in1,
                                                        op0, op1),
                       reads, [out], tag)

    def copy(self, out, in_, engine="dve", tag=""):
        return self.op(engine, lambda e: e.tensor_copy(out, in_),
                       [in_], [out], tag)

    def memset(self, ap, val, engine="dve", tag=""):
        return self.op(engine, lambda e: e.memset(ap, val), [], [ap], tag)

    def reduce(self, out, in_, alu, engine="dve", tag=""):
        return self.op(engine,
                       lambda e: e.tensor_reduce(out, in_, AX.X, alu),
                       [in_], [out], tag)

    def scan(self, out, d0, d1, init, op0, op1, tag=""):
        reads = [d0, d1] + ([init] if isinstance(init, bass.AP) else [])
        return self.op("dve",
                       lambda e: e.tensor_tensor_scan(out, d0, d1, init, op0, op1),
                       reads, [out], tag)

    def collective(self, kind, alu, groups, ins, outs, tag=""):
        def build(e):
            return e.collective_compute(kind, alu, replica_groups=groups,
                                        ins=ins, outs=outs)
        op = Op("dma:pool!cc", build, ins, outs, tag)
        return self._record(op)

    def wrap1(self, out, a, g1, g2, engine="dve"):
        self.ts(g1, a, -PI, None, ALU.is_lt, engine=engine)
        self.ts(g2, a, PI, None, ALU.is_gt, engine=engine)
        self.tt(g1, g1, g2, ALU.subtract, engine=engine)
        self.stt(out, g1, 2 * PI, a, ALU.mult, ALU.add, engine=engine)
        return out

    def rsqrt(self, out, v, scr1, scr2=None, iters=3):
        self.copy(scr1, v.bitcast(I32))
        self.ts(scr1, scr1, -0.5, RSQRT_C, ALU.mult, ALU.add)
        self.copy(out.bitcast(I32), scr1)
        for _ in range(iters):
            self.tt(scr1, out, out, ALU.mult)
            self.tt(scr1, scr1, v, ALU.mult)
            self.ts(scr1, scr1, -0.5, 1.5, ALU.mult, ALU.add)
            self.tt(out, out, scr1, ALU.mult)
        return out


"""Program builder: phases of the macaron megakernel. TILE=256 tokens."""
import contextlib

D = 1024
DI = 2048
H = 32
PD = 64
NS = 128
CONVDIM = 2560
DFF = 4096
CK = 128
T = 4096
NCH = 32
USL = 4352
EPS = 1e-5
TILE = 256
NG = TILE // 128

DBG = False


def _v(h, offset, dims):
    ap = h if isinstance(h, bass.AP) else h[:, :]
    return bass.AP(ap.tensor, offset, dims)


def build_program(phases=("P1", "P2")):
    nc = bass.Bass("TRN2", target_bir_lowering=False)
    em = Emitter(nc)

    din = {}
    for name, shape in [
        ("u", [USL, D]), ("w1", [D, DFF]), ("w2", [DFF, D]),
        ("b1", [DFF]), ("b2", [D]),
        ("phases", [DFF]), ("wreal", [DFF]), ("wimag", [DFF]),
        ("ipwT", [D, 4672]), ("opwT", [DI, D]), ("fcdT", [DI, H]),
        ("convw", [CONVDIM, 4]), ("convb", [CONVDIM]),
        ("dtbias2", [2 * H]), ("alog", [H]), ("dskip", [H]),
        ("normw", [DI]),
        ("identb", [128, 128]), ("identf", [128, 128]),
        ("triu", [128, 128]), ("tril", [128, 128]),
        ("maskF", [128, 128]), ("maskB", [128, 128]),
        ("indh", [H, DFF]), ("indhp", [H, DI]),
        ("lnmask", [4352]), ("drep", [128, DI]),
        ("gown", [8]), ("gprev", [8]),
        ("selown", [4]), ("selprev", [4]), ("selnxt", [4]),
        ("selobc", [128, 4]), ("selpbc", [128, 4]), ("selnbc", [128, 4]),
        ("zer32", [32, 128]), ("neg3", [3, 128]),
        ("mask2", [128, 2]), ("d32bc", [128, 32]),
    ]:
        dt = BF16 if name in ("identb", "maskF", "maskB", "indh",
                              "indhp", "zer32", "neg3") else F32
        din[name] = nc.dram_tensor(name, shape, dt, kind="ExternalInput").ap()

    x1buf = nc.dram_tensor("x1buf", [USL, D], F32).ap()
    xln2fm = nc.dram_tensor("xln2fm", [D, USL], BF16).ap()
    xogbuf = nc.dram_tensor("xogbuf", [T + 2, DI], BF16).ap()
    xogfm = nc.dram_tensor("xogfm", [CONVDIM, T], BF16).ap()
    ylf = nc.dram_tensor("ylf", [T + 2, DI], BF16).ap()
    ylb = nc.dram_tensor("ylb", [T + 2, DI], BF16).ap()
    flscr = nc.dram_tensor("flscr", [10, DFF], BF16).ap()
    ccin = nc.dram_tensor("ccin", [264, DI], F32).ap()
    ccout = nc.dram_tensor("ccout", [4 * 264, DI], F32).ap()
    zbuf = nc.dram_tensor("zbuf", [T, DI], BF16).ap()
    adtbuf = nc.dram_tensor("adtbuf", [64, T], F32).ap()
    x2buf = nc.dram_tensor("x2buf", [T, D], F32).ap()
    dtfmbuf = nc.dram_tensor("dtfmbuf", [64, T], F32).ap()
    dtlnF = nc.dram_tensor("dtlnF", [64, T], F32).ap()

    outs = {}
    if DBG:
        for nm, shape, dt in [
            ("dbg_x1", [USL, D], F32), ("dbg_xln2fm", [D, USL], BF16),
            ("dbg_dt", [64, T], F32), ("dbg_xog", [T + 2, DI], BF16),
            ("dbg_ylf", [T, DI], BF16), ("dbg_ylb", [T, DI], BF16),
            ("dbg_sf", [NS, DI], F32), ("dbg_sb", [NS, DI], F32),
            ("dbg_fd", [H, T], BF16), ("dbg_eact", [128, NCH * 64], BF16),
        ]:
            outs[nm] = nc.dram_tensor(nm, shape, dt,
                                      kind="ExternalOutput").ap()
    out = nc.dram_tensor("out", [T, D], F32, kind="ExternalOutput").ap()

    topstack = contextlib.ExitStack()

    def SB(stk, name, shape, dt=F32):
        return stk.enter_context(nc.sbuf_tensor(name, shape, dt))

    def PS(stk, name, shape, dt=F32):
        return stk.enter_context(nc.psum_tensor(name, shape, dt))

    identb = SB(topstack, "identb_s", [128, 128], BF16)
    identf = SB(topstack, "identf_s", [128, 128])
    lnmask = SB(topstack, "lnmask_s", [128, 34])
    em.dma("sp", identb[:, :], din["identb"])
    em.dma("sp", identf[:, :], din["identf"])
    em.dma("sp", lnmask[:, :], din["lnmask"].rearrange("(c p) -> p c", p=128),
           slow=True)

    def p1(stk):
        w1b = SB(stk, "w1b", [128, 8 * DFF], BF16)
        w2b = SB(stk, "w2b", [128, 32 * D], BF16)
        b1c = SB(stk, "b1c", [128, 32])
        b2c = SB(stk, "b2c", [128, 8])
        thetac = SB(stk, "thetac", [128, 32])
        thetab = SB(stk, "thetab", [128, 32])
        coefc = SB(stk, "coefc", [128, 32])
        ut = SB(stk, "ut", [128, NG * D])
        sumc = SB(stk, "sumc", [128, 8])
        sqc = SB(stk, "sqc", [128, 8])
        st1 = SB(stk, "st1", [128, 8])
        st2 = SB(stk, "st2", [128, 8])
        st3 = SB(stk, "st3", [128, 8])
        rstd = SB(stk, "rstd", [128, 8])
        nmr = SB(stk, "nmr", [128, 8])
        sq_scr = SB(stk, "sq_scr", [128, D])
        xlnt = SB(stk, "xlnt", [128, NG * D], BF16)
        xfm = SB(stk, "xfm", [128, 8 * TILE], BF16)
        aa = SB(stk, "aa", [128, TILE], BF16)
        kfl = SB(stk, "kfl", [128, TILE], BF16)
        sargt = SB(stk, "sargt", [128, TILE], BF16)
        stile = SB(stk, "stile", [128, TILE])
        qt = SB(stk, "qt", [128, TILE], BF16)
        hact = SB(stk, "hact", [128, 32 * TILE], BF16)
        o2t = [SB(stk, f"o2t{i}", [128, TILE], BF16) for i in range(2)]
        x1t = SB(stk, "x1t", [128, NG * D])
        xfm2 = SB(stk, "xfm2", [128, 8 * TILE], BF16)
        phs = SB(stk, "phs", [128, 32])
        wrs = SB(stk, "wrs", [128, 32])
        wis = SB(stk, "wis", [128, 32])
        t32a = SB(stk, "t32a", [128, 32])
        t32b = SB(stk, "t32b", [128, 32])
        ptT = PS(stk, "ptT", [128, 2048], BF16)
        pfb = [PS(stk, f"pf{i}", [128, 512]) for i in range(3)]
        pob = [PS(stk, f"po{i}", [128, 512]) for i in range(2)]

        for k in range(8):
            em.dma("pool", w1b[:, k * DFF:(k + 1) * DFF],
                   din["w1"][k * 128:(k + 1) * 128, :], tag="w1")
        for k in range(8):
            em.dma("pool",
                   w2b[:, k * 4 * D:(k + 1) * 4 * D].rearrange(
                       "p (c n) -> p c n", c=4),
                   din["w2"].rearrange("(c p) n -> p c n",
                                       p=128)[:, 4 * k:4 * k + 4, :], tag="w2")
        em.dma("sp", b1c[:, :], din["b1"].rearrange("(c p) -> p c", p=128),
               slow=True)
        em.dma("sp", b2c[:, :], din["b2"].rearrange("(c p) -> p c", p=128),
               slow=True)
        em.dma("sp", phs[:, :], din["phases"].rearrange("(c p) -> p c", p=128),
               slow=True)
        em.dma("sp", wrs[:, :], din["wreal"].rearrange("(c p) -> p c", p=128),
               slow=True)
        em.dma("sp", wis[:, :], din["wimag"].rearrange("(c p) -> p c", p=128),
               slow=True)
        em.copy(thetac[:, :], phs[:, :])
        for _ in range(6):
            em.wrap1(thetac[:, :], thetac[:, :], t32a[:, :], t32b[:, :])
        em.act(t32a[:, :], thetac[:, :], AF.Sin)
        em.act(t32b[:, :], thetac[:, :], AF.Sin, scale=0.5)
        em.tt(t32b[:, :], t32b[:, :], t32b[:, :], ALU.mult)
        em.ts(t32b[:, :], t32b[:, :], -2.0, 1.0, ALU.mult, ALU.add)
        em.tt(t32a[:, :], t32a[:, :], wis[:, :], ALU.mult)
        em.tt(t32b[:, :], t32b[:, :], wrs[:, :], ALU.mult)
        em.tt(t32a[:, :], t32a[:, :], t32b[:, :], ALU.add)
        em.tt(t32a[:, :], t32a[:, :], wis[:, :], ALU.mult)
        em.ts(coefc[:, :], t32a[:, :], 0.1, None, ALU.mult)
        em.stt(thetab[:, :], b1c[:, :], 0.1, thetac[:, :], ALU.mult, ALU.add)

        def ln_tokmajor(src, xout, mask_col=None, soff=0):
            for g in range(NG):
                gs = src[:, g * D:(g + 1) * D]
                em.act(sq_scr[:, :], gs, AF.Square,
                       accum_out=sqc[:, soff + g:soff + g + 1])
                em.reduce(sumc[:, soff + g:soff + g + 1], gs, ALU.add)
            sl = slice(soff, soff + NG)
            em.ts(st1[:, sl], sumc[:, sl], 1.0 / D, None, ALU.mult)
            em.ts(st2[:, sl], sqc[:, sl], 1.0 / D, None, ALU.mult)
            em.tt(st3[:, sl], st1[:, sl], st1[:, sl], ALU.mult)
            em.tt(st2[:, sl], st2[:, sl], st3[:, sl], ALU.subtract)
            em.ts(st2[:, sl], st2[:, sl], EPS, None, ALU.add)
            em.rsqrt(rstd[:, sl], st2[:, sl], st3[:, sl])
            em.tt(nmr[:, sl], st1[:, sl], rstd[:, sl], ALU.mult)
            em.ts(nmr[:, sl], nmr[:, sl], -1.0, None, ALU.mult)
            for g in range(NG):
                c = soff + g
                em.ts(xout[:, g * D:(g + 1) * D], src[:, g * D:(g + 1) * D],
                      rstd[:, c:c + 1], nmr[:, c:c + 1], ALU.mult, ALU.add)
                if mask_col is not None:
                    em.ts(xout[:, g * D:(g + 1) * D],
                          xout[:, g * D:(g + 1) * D],
                          mask_col[:, g:g + 1], None, ALU.mult)

        def tm_to_fm(src_bf, dst, tag):
            for k in range(8):
                for g in range(NG):
                    em.transpose(
                        ptT[:, (k * NG + g) * 128:(k * NG + g + 1) * 128],
                        src_bf[:, g * D + k * 128: g * D + (k + 1) * 128],
                        identb[:, :], tag=tag)
            dap = dst[:, :]
            sap = ptT[:, :]
            dv = bass.AP(dap.tensor, 0, [list(dap.ap[0]), [TILE, 8],
                                         [128, NG], [1, 128]])
            sv = bass.AP(sap.tensor, 0, [list(sap.ap[0]), [NG * 128, 8],
                                         [128, NG], [1, 128]])
            em.copy(dv, sv, tag=tag)

        ntile = USL // TILE
        for t in range(ntile):
            R0 = t * TILE
            em.dma("sp", ut[:, :].rearrange("p (g d) -> p g d", g=NG),
                   din["u"][R0:R0 + TILE, :].rearrange("(g p) d -> p g d",
                                                       p=128), tag=f"p1u{t}")
            ln_tokmajor(ut, xlnt)
            tm_to_fm(xlnt, xfm, f"p1t{t}")
            for f in range(32):
                bank = pfb[f % 3]
                for k in range(8):
                    em.mm(bank[:, 0:TILE],
                          w1b[:, k * DFF + f * 128: k * DFF + (f + 1) * 128],
                          xfm[:, k * TILE:(k + 1) * TILE],
                          start=(k == 0), stop=(k == 7))
                em.ts(aa[:, :], bank[:, 0:TILE], 0.1, thetab[:, f:f + 1],
                      ALU.mult, ALU.add)
                em.wrap1(sargt[:, :], aa[:, :], kfl[:, :], qt[:, :])
                em.act(stile[:, :], sargt[:, :], AF.Sin)
                em.stt(qt[:, :], stile[:, :], coefc[:, f:f + 1],
                       bank[:, 0:TILE], ALU.mult, ALU.add)
                em.act(hact[:, f * TILE:(f + 1) * TILE], qt[:, :], AF.Relu,
                       bias=b1c[:, f:f + 1])
            for do in range(8):
                bank = pob[do % 2]
                for kf in range(32):
                    em.mm(bank[:, 0:TILE],
                          w2b[:, kf * D + do * 128: kf * D + (do + 1) * 128],
                          hact[:, kf * TILE:(kf + 1) * TILE],
                          start=(kf == 0), stop=(kf == 31))
                em.ts(o2t[do % 2][:, :], bank[:, 0:TILE], b2c[:, do:do + 1],
                      None, ALU.add)
                for g in range(NG):
                    em.transpose(ptT[:, (do % 2) * 1024 + g * 128:
                                     (do % 2) * 1024 + (g + 1) * 128],
                                 o2t[do % 2][:, g * 128:(g + 1) * 128],
                                 identb[:, :])
                for g in range(NG):
                    em.stt(x1t[:, g * D + do * 128: g * D + (do + 1) * 128],
                           ptT[:, (do % 2) * 1024 + g * 128:
                               (do % 2) * 1024 + g * 128 + 128],
                           0.5,
                           ut[:, g * D + do * 128: g * D + (do + 1) * 128],
                           ALU.mult, ALU.add)
            mask_col = (lnmask[:, R0 // 128: R0 // 128 + NG]
                        if (t == 0 or t == ntile - 1) else None)
            ln_tokmajor(x1t, xlnt, mask_col=mask_col, soff=4)
            tm_to_fm(xlnt, xfm2, f"p1s{t}")
            em.dma("act",
                   x1buf[R0:R0 + TILE, :].rearrange("(g p) d -> p g d", p=128),
                   x1t[:, :].rearrange("p (g d) -> p g d", g=NG),
                   tag=f"x1st{t}")
            em.dma("act",
                   xln2fm[:, R0:R0 + TILE].rearrange("(k p) c -> p k c",
                                                     p=128),
                   xfm2[:, :].rearrange("p (k c) -> p k c", k=8),
                   tag=f"xlst{t}")

    def p2(stk, fd_fm, eact):
        wdtb = SB(stk, "wdtb", [128, 8 * 64], BF16)
        wxbcb = SB(stk, "wxbcb", [128, 8 * CONVDIM], BF16)
        fcdb = SB(stk, "fcdb", [128, 16 * H], BF16)
        convws = SB(stk, "convws", [128, 20 * 4])
        convbs = SB(stk, "convbs", [128, 20])
        convbh = SB(stk, "convbh", [128, 20])
        dtb2c = SB(stk, "dtb2c", [64, 1])
        a64c = SB(stk, "a64c", [64, 1])
        onec = SB(stk, "onec", [64, 1])
        adtc2 = [SB(stk, f"adtc{i}", [32, 128]) for i in range(2)]
        tri_fs = SB(stk, "tri_fs", [128, 128])
        tri_bs = SB(stk, "tri_bs", [128, 128])
        maskf_s = SB(stk, "maskf_s", [128, 128], BF16)
        maskb_s = SB(stk, "maskb_s", [128, 128], BF16)
        xz2 = [SB(stk, f"xz{i}", [128, 8 * 517], BF16)
               for i in range(2)]
        cog = [SB(stk, "cog0", [128, 20 * 514], BF16)] * 2
        convacc = SB(stk, "convacc", [128, 1028])
        convv = SB(stk, "convv", [128, 1028])
        offc = SB(stk, "offc", [32, 2])
        aclast = SB(stk, "aclast", [32, 8])
        cdc = SB(stk, "cdc", [32, 8])
        acs_c2 = [SB(stk, f"acs_c{i}", [32, 128]) for i in range(2)]
        acsn2 = [SB(stk, f"acsn{i}", [32, 128]) for i in range(2)]
        acsH2 = [SB(stk, f"acsH{i}", [32, 128], BF16) for i in range(2)]
        acsM2 = [SB(stk, f"acsM{i}", [32, 128], BF16) for i in range(2)]
        acsL2 = [SB(stk, f"acsL{i}", [32, 128], BF16) for i in range(2)]
        acsr2 = [SB(stk, f"acsr{i}", [32, 128]) for i in range(2)]
        lndt322 = [SB(stk, f"lndt32{i}", [32, 128]) for i in range(2)]
        xsp2 = [SB(stk, f"xsp{i}", [32, 128]) for i in range(2)]
        dtH2 = [SB(stk, f"dtH{i}", [32, 128], BF16) for i in range(2)]
        dtM2 = [SB(stk, f"dtM{i}", [32, 128], BF16) for i in range(2)]
        dtL2 = [SB(stk, f"dtL{i}", [32, 128], BF16) for i in range(2)]
        e_loc2 = [SB(stk, f"e_loc{i}", [32, 128]) for i in range(2)]
        dts322 = [SB(stk, f"dts32{i}", [32, 128]) for i in range(2)]
        ddb322 = [SB(stk, f"ddb32{i}", [32, 128], BF16) for i in range(2)]
        cdmat = SB(stk, "cdmat", [32, 128], BF16)
        ones32 = SB(stk, "ones32", [32, 128], BF16)
        din_indhp = SB(stk, "din_indhp", [32, DI], BF16)
        dd_fm2 = [SB(stk, f"dd_fm{i}", [32, 128]) for i in range(2)]
        lhsD2 = [SB(stk, f"lhsD{i}", [128, 128], BF16) for i in range(2)]
        rhsD = SB(stk, "rhsD", [128, DFF], BF16)
        Dm_sb2 = [SB(stk, f"Dm_sb{i}", [128, DFF], BF16) for i in range(2)]
        U_sb2 = [SB(stk, f"U_sb{i}", [128, 256], BF16) for i in range(2)]
        xh_sb2 = [SB(stk, f"xh_sb{i}", [128, DI], BF16) for i in range(2)]
        xt2_sb2 = [SB(stk, f"xt2_sb{i}", [128, DI], BF16) for i in range(2)]
        btm_sb2 = [SB(stk, f"btm_sb{i}", [128, 128], BF16) for i in range(2)]
        dtT_c2 = [SB(stk, f"dtT_c{i}", [128, 32]) for i in range(2)]
        elocT2 = [SB(stk, f"elocT{i}", [128, 32]) for i in range(2)]
        yoff_sb2 = [SB(stk, f"yoff_sb{i}", [128, DI], BF16) for i in range(2)]
        y_sb2 = [SB(stk, f"y_sb{i}", [128, DI], BF16) for i in range(2)]
        S_f = SB(stk, "S_f", [128, DI])
        S_b = SB(stk, "S_b", [128, DI])
        sbf = SB(stk, "sbf", [128, DI], BF16)
        xgf2 = [SB(stk, f"xgf{i}", [128, 16 * 128], BF16) for i in range(2)]
        bfm2 = [SB(stk, f"bfm{i}", [128, 128], BF16) for i in range(2)]
        cfm2 = [SB(stk, f"cfm{i}", [128, 128], BF16) for i in range(2)]
        a32 = SB(stk, "a32", [32, 1])
        pA = PS(stk, "pA", [128, 2048])
        pB = PS(stk, "pB", [128, 1024])
        pT = PS(stk, "pT", [128, 2048], BF16)

        for k in range(8):
            em.dma("pool", wdtb[:, k * 64:(k + 1) * 64],
                   din["ipwT"][k * 128:(k + 1) * 128, 4608:4672], tag="wdt")
        for k in range(8):
            em.dma("pool", wxbcb[:, k * CONVDIM:(k + 1) * CONVDIM],
                   din["ipwT"][k * 128:(k + 1) * 128, DI:DI + CONVDIM],
                   tag="wxbc")
        for k in range(16):
            em.dma("pool", fcdb[:, k * H:(k + 1) * H],
                   din["fcdT"][k * 128:(k + 1) * 128, :], tag="fcd")
        em.dma("sp", convws[:, :].rearrange("p (c j) -> p c j", c=20),
               din["convw"].rearrange("(c p) j -> p c j", p=128), slow=True)
        em.dma("sp", convbs[:, :],
               din["convb"].rearrange("(c p) -> p c", p=128), slow=True)
        em.dma("sp", dtb2c[:, :], din["dtbias2"].unsqueeze(1),
               slow=True)
        em.dma("sp", a32[:, :], din["alog"].unsqueeze(1), slow=True)
        em.dma("sp", tri_fs[:, :], din["triu"])
        em.dma("sp", tri_bs[:, :], din["tril"])
        em.dma("sp", maskf_s[:, :], din["maskF"])
        em.dma("sp", maskb_s[:, :], din["maskB"])
        for r in range(3):
            em.dma("sp", rhsD[32 * r:32 * (r + 1), :], din["indh"], tag="ind")
        em.memset(ones32[:, :], 1.0)
        em.dma("sp", din_indhp[:, :], din["indhp"])
        for rz in range(96, 106):
            em.dma("sp", rhsD[rz:rz + 1, :],
                   din["zer32"].rearrange("a b -> (a b)").unsqueeze(0),
                   tag=f"rz{rz}")
        em.dma("sp", lhsD2[0][96:128, :], din["zer32"], tag="lz0")
        em.dma("sp", lhsD2[1][96:128, :], din["zer32"], tag="lz1")
        em.dma("sp", lhsD2[0][96:99, :], din["neg3"], tag="ln0")
        em.dma("sp", lhsD2[1][99:102, :], din["neg3"], tag="ln1")
        em.ts(convbh[:, :], convbs[:, :], 0.5, None, ALU.mult)
        em.memset(onec[:, :], 1.0)
        em.act(a32[:, :], a32[:, :], AF.Exp)
        em.ts(a32[:, :], a32[:, :], -1.0, None, ALU.mult)
        em.copy(a64c[0:32, :], a32[:, :])
        em.copy(a64c[32:64, :], a32[:, :])
        dtcol = convacc[0:64, 0:512]
        dtscr = convv[0:64, 0:512]
        for t8 in range(8):
            c0 = 4 + 512 * t8
            xz = xz2[t8 % 2]
            em.dma("sp",
                   bass.AP(xz[:, :].tensor, 0,
                           [list(xz[:, :].ap[0]), [517, 8], [1, 512]]),
                   xln2fm[:, c0:c0 + 512].rearrange("(k p) c -> p k c", p=128),
                   tag=f"dtx{t8}")
            for k in range(8):
                em.mm(pB[0:64, 0:512], wdtb[:, k * 64:(k + 1) * 64],
                      xz[:, k * 517:k * 517 + 512], start=(k == 0),
                      stop=(k == 7))
            em.ts(dtcol, pB[0:64, 0:512], dtb2c[:, :], None, ALU.add)
            em.act(dtscr, dtcol, AF.Abs)
            em.act(dtscr, dtscr, AF.Exp, scale=-1.0)
            em.act(dtscr, dtscr, AF.Ln, bias=onec[:, :])
            em.ts(dtcol, dtcol, 0.0, None, ALU.max)
            em.tt(dtcol, dtscr, dtcol, ALU.add)
            adts = convv[64:128, 512:1024]
            em.ts(adts, dtcol, a64c[:, :], None, ALU.mult)
            em.dma("act", adtbuf[:, 512 * t8:512 * (t8 + 1)], adts,
                   tag=f"adtst{t8}")
            em.dma("act", dtfmbuf[:, 512 * t8:512 * (t8 + 1)],
                   dtcol, tag=f"dtfm{t8}")
            em.act(dtscr, dtcol, AF.Ln)
            em.ts(dtscr, dtscr, -88.0, None, ALU.max)
            em.dma("act", dtlnF[:, 512 * t8:512 * (t8 + 1)], dtscr,
                   tag=f"lnf{t8}")
        em.memset(offc[:, :], 0.0)
        em.memset(S_f[:, :], 0.0)
        em.memset(S_b[:, :], 0.0)

        def conv_group(g, buf):
            c0 = 4 + 512 * g
            xz = xz2[buf]
            em.dma("sp", xz[:, :].rearrange("p (k c) -> p k c", k=8),
                   xln2fm[:, c0 - 4:c0 + 513].rearrange("(k p) c -> p k c",
                                                        p=128), tag=f"cvx{g}")
            co = cog[buf]
            for fc in range(20):
                pc = pA[:, (fc % 2) * 1024:(fc % 2) * 1024 + 517]
                ca = convacc[:, (fc % 2) * 514:(fc % 2) * 514 + 514]
                cv = convv[:, (fc % 2) * 514:(fc % 2) * 514 + 514]
                for k in range(8):
                    em.mm(pc[:, 0:512],
                          wxbcb[:, k * CONVDIM + fc * 128:
                                k * CONVDIM + (fc + 1) * 128],
                          xz[:, k * 517:k * 517 + 512], start=(k == 0),
                          stop=(k == 7))
                for k in range(8):
                    em.mm(pc[:, 512:517],
                          wxbcb[:, k * CONVDIM + fc * 128:
                                k * CONVDIM + (fc + 1) * 128],
                          xz[:, k * 517 + 512:k * 517 + 517], start=(k == 0),
                          stop=(k == 7))
                em.ts(ca, pc[:, 0:514],
                      convws[:, 4 * fc:4 * fc + 1], None, ALU.mult)
                for j in range(1, 4):
                    em.stt(ca, pc[:, j:514 + j],
                           convws[:, 4 * fc + j:4 * fc + j + 1],
                           ca, ALU.mult, ALU.add)
                em.ts(cv, ca, convbs[:, fc:fc + 1], None, ALU.add)
                em.act(ca, ca, AF.Tanh, scale=0.5, bias=convbh[:, fc:fc + 1])
                em.ts(ca, ca, 0.5, 0.5, ALU.mult, ALU.add)
                em.tt(co[:, fc * 514:(fc + 1) * 514], cv, ca, ALU.mult)
            if g == 0:
                em.dma("act", xogbuf[0:1, :].rearrange("t (c p) -> p c t",
                                                       p=128),
                       co[:, :].rearrange("p (c t) -> p c t",
                                          c=20)[:, 0:16, 0:1],
                       tag="xogpad0", slow=True)
            if g == 7:
                em.dma("act", xogbuf[T + 1:T + 2, :].rearrange(
                           "t (c p) -> p c t", p=128),
                       co[:, :].rearrange("p (c t) -> p c t",
                                          c=20)[:, 0:16, 513:514],
                       tag="xogpad1", slow=True)
            em.dma("act",
                   xogfm[:, 512 * g:512 * (g + 1)].rearrange(
                       "(c p) t -> p c t", p=128),
                   bass.AP(co[:, :].tensor, 1,
                           [list(co[:, :].ap[0]), [514, 20], [1, 512]]),
                   tag=f"xfmst{g}")
            for k in range(16):
                em.mm(pA[0:32, 0:512], fcdb[:, k * H:(k + 1) * H],
                      co[:, k * 514 + 1:k * 514 + 513], start=(k == 0),
                      stop=(k == 15))
            em.copy(fd_fm[:, 512 * g:512 * (g + 1)], pA[0:32, 0:512])

        qctr = {0: 0, 1: 0}

        def ssd_stages(c, fwd):
            r0, r1 = (0, 32) if fwd else (32, 64)
            tri = tri_fs if fwd else tri_bs
            mask = maskf_s if fwd else maskb_s
            fcB, fcC = (16, 17) if fwd else (18, 19)
            S = S_f if fwd else S_b
            oc = 0 if fwd else 1
            d = 0 if fwd else 1
            dircol = 0 if fwd else 32
            xgf, bfm, cfm = xgf2[d], bfm2[d], cfm2[d]
            elocT = elocT2[d]
            lhsD = lhsD2[d]
            dts32, ddb32 = dts322[d], ddb322[d]
            acs_c, acsn, acsH = acs_c2[d], acsn2[d], acsH2[d]
            acsM, acsL, acsr = acsM2[d], acsL2[d], acsr2[d]
            lndt32, xsp = lndt322[d], xsp2[d]
            dtH, dtM, dtL = dtH2[d], dtM2[d], dtL2[d]
            e_loc, dd_fm, dtT_c = e_loc2[d], dd_fm2[d], dtT_c2[d]
            Dm_sb, U_sb, xh_sb = Dm_sb2[d], U_sb2[d], xh_sb2[d]
            xt2_sb, btm_sb = xt2_sb2[d], btm_sb2[d]
            yoff_sb, y_sb = yoff_sb2[d], y_sb2[d]
            pAd = lambda lo, hi: pA[:, d * 1024 + lo:d * 1024 + hi]

            def nextq():
                i = qctr[d]
                qctr[d] += 1
                return d * 1024 + (i % 2) * 512
            pBd = lambda lo, hi: pB[:, d * 512 + lo:d * 512 + hi]
            pTd = lambda lo, hi: pT[:, d * 1024 + lo:d * 1024 + hi]
            rr0 = 96 if fwd else 99

            def H1():
                em.dma("sp", xgf[:, :].rearrange("p (c t) -> p c t", c=16),
                       xogfm[:, c * 128:(c + 1) * 128].rearrange(
                           "(c p) t -> p c t", p=128)[:, 0:16, :],
                       tag=f"xgl{c}{d}")
                em.dma("sp", bfm[:, :], xogfm[fcB * 128:(fcB + 1) * 128,
                                              c * 128:(c + 1) * 128],
                       tag=f"bl{c}{d}")
                em.dma("sp", cfm[:, :], xogfm[fcC * 128:(fcC + 1) * 128,
                                              c * 128:(c + 1) * 128],
                       tag=f"cl{c}{d}")
                em.dma("sp", adtc2[d][:, :],
                       adtbuf[r0:r1, c * 128:(c + 1) * 128], tag=f"adl{c}{d}")
                em.transpose(pBd(0, 32), adtc2[d][:, :], identf[0:32, 0:32])
                em.copy(dtT_c[:, :], pBd(0, 32))
                em.mm(pBd(32, 160)[0:32, :], dtT_c[:, :], tri[:, :],
                      start=True, stop=True)
                em.copy(acs_c[:, :], pBd(32, 160)[0:32, :])
                last = 127 if fwd else 0
                em.copy(aclast[:, oc:oc + 1], acs_c[:, last:last + 1])
                em.act(e_loc[:, :], acs_c[:, :], AF.Exp)
                em.act(dd_fm[:, :], acs_c[:, :], AF.Exp, scale=-1.0,
                       bias=aclast[:, oc:oc + 1])
                em.act(cdc[:, oc:oc + 1], aclast[:, oc:oc + 1], AF.Exp)
                em.act(acsn[:, :], acs_c[:, :], AF.Exp, bias=offc[:, oc:oc + 1])
                em.transpose(pBd(160, 192), acsn[:, :], identf[0:32, 0:32])
                em.copy(eact[:, c * 64 + dircol: c * 64 + dircol + 32],
                        pBd(160, 192))
                em.tt(offc[:, oc:oc + 1], offc[:, oc:oc + 1],
                      aclast[:, oc:oc + 1], ALU.add)
                em.transpose(pBd(192, 224), e_loc[:, :], identf[0:32, 0:32])
                em.copy(elocT[:, :], pBd(192, 224))
                em.dma("sp", dts32[:, :],
                       dtfmbuf[r0:r1, c * 128:(c + 1) * 128],
                       tag=f"dtl{c}{d}")
                em.dma("sp", lndt32[:, :],
                       dtlnF[r0:r1, c * 128:(c + 1) * 128],
                       tag=f"lnl{c}{d}")
                em.tt(dd_fm[:, :], dd_fm[:, :], dts32[:, :], ALU.mult)
                em.copy(ddb32[:, :], dd_fm[:, :])
                em.ts(acsn[:, :], acs_c[:, :], -1.0, None, ALU.mult)
                em.copy(acsH[:, :], acsn[:, :])
                em.tt(acsr[:, :], acsn[:, :], acsH[:, :], ALU.subtract)
                em.copy(acsM[:, :], acsr[:, :])
                em.tt(acsr[:, :], acsr[:, :], acsM[:, :], ALU.subtract)
                em.copy(acsL[:, :], acsr[:, :])
                em.tt(xsp[:, :], acsn[:, :], lndt32[:, :], ALU.add)
                em.copy(dtH[:, :], xsp[:, :])
                em.tt(xsp[:, :], xsp[:, :], dtH[:, :], ALU.subtract)
                em.copy(dtM[:, :], xsp[:, :])
                em.tt(xsp[:, :], xsp[:, :], dtM[:, :], ALU.subtract)
                em.copy(dtL[:, :], xsp[:, :])
                em.copy(lhsD[0:32, :], dtH[:, :])
                em.copy(lhsD[32:64, :], dtM[:, :])
                em.copy(lhsD[64:96, :], dtL[:, :])
                em.dma("pool", rhsD[rr0:rr0 + 1, :],
                       flscr[3 * d:3 * d + 1, :], tag="dum") if False else None
                for ri, asrc in ((0, acsH), (1, acsM), (2, acsL)):
                    em.dma("pool", flscr[3 * d + ri:3 * d + ri + 1, :]
                           .rearrange("a (b c) -> (a b) c", b=32)
                           if False else
                           flscr[3 * d + ri:3 * d + ri + 1, :].rearrange(
                               "a (b c) -> (a b) c", b=32), asrc[:, :],
                           tag=f"flA{d}{ri}")
                    em.dma("pool", rhsD[rr0 + ri:rr0 + ri + 1, :],
                           flscr[3 * d + ri:3 * d + ri + 1, :],
                           tag=f"flB{d}{ri}")

            def H2():
                for q in range(8):
                    b = nextq()
                    em.mm(pA[:, b:b + 512], lhsD[0:102, :],
                          rhsD[0:102, q * 512:(q + 1) * 512],
                          start=True, stop=True)
                    em.ts(Dm_sb[:, q * 512:(q + 1) * 512],
                          pA[:, b:b + 512], 30.0, None, ALU.min)
                em.act(Dm_sb[:, :], Dm_sb[:, :], AF.Exp)
                em.mm(pBd(224, 352), bfm[:, :], cfm[:, :], start=True,
                      stop=True)
                em.tt(U_sb[:, 0:128], pBd(224, 352), mask[:, :], ALU.mult)
                em.copy(U_sb[:, 128:256], U_sb[:, 0:128])

            def M1():
                for w in range(2):
                    for k in range(8):
                        em.transpose(pTd(k * 128, (k + 1) * 128),
                                     xgf[:, (w * 8 + k) * 128:
                                         (w * 8 + k + 1) * 128], identb[:, :])
                    em.copy(xh_sb[:, w * 1024:(w + 1) * 1024], pTd(0, 1024))
                if fwd:
                    em.dma("act", xogbuf[c * 128 + 1:c * 128 + 129, :],
                           xh_sb[:, :], tag=f"xogst{c}")
                for q in range(4):
                    b = nextq()
                    em.mm(pA[:, b:b + 512], ddb32[:, :],
                          din_indhp[0:32, q * 512:(q + 1) * 512],
                          start=True, stop=True)
                    em.tt(xt2_sb[:, q * 512:(q + 1) * 512],
                          xh_sb[:, q * 512:(q + 1) * 512], pA[:, b:b + 512],
                          ALU.mult)
                em.transpose(pTd(0, 128), bfm[:, :], identb[:, :])
                em.copy(btm_sb[:, :], pTd(0, 128))
                for h2 in range(16):
                    em.tt(Dm_sb[:, h2 * 256:(h2 + 1) * 256],
                          Dm_sb[:, h2 * 256:(h2 + 1) * 256], U_sb[:, :],
                          ALU.mult)

            def T1():
                em.copy(sbf[:, :], S[:, :])
                ybuf = ylf if fwd else ylb
                for q in range(4):
                    b = nextq()
                    em.mm(pA[:, b:b + 512], cfm[:, :],
                          sbf[:, q * 512:(q + 1) * 512],
                          start=True, stop=True)
                    for h in range(8):
                        hh = q * 8 + h
                        em.act(yoff_sb[:, hh * 64:(hh + 1) * 64],
                               pA[:, b + h * 64:b + (h + 1) * 64], AF.Copy,
                               scale=elocT[:, hh:hh + 1])
                    b2 = nextq()
                    for h in range(8):
                        hh = q * 8 + h
                        em.mm(pA[:, b2 + h * 64:b2 + (h + 1) * 64],
                              Dm_sb[:, hh * 128:(hh + 1) * 128],
                              xh_sb[:, hh * 64:(hh + 1) * 64], start=True,
                              stop=True)
                    em.tt(y_sb[:, q * 512:(q + 1) * 512],
                          pA[:, b2:b2 + 512],
                          yoff_sb[:, q * 512:(q + 1) * 512], ALU.add)
                em.dma("act", ybuf[c * 128 + 1:c * 128 + 129, :], y_sb[:, :],
                       tag=f"yst{c}{d}")

            def T2():
                em.ts(cdmat[:, :], ones32[:, :], cdc[:, oc:oc + 1], None,
                      ALU.mult)
                for q in range(4):
                    b = nextq()
                    em.mm(pA[:, b:b + 512], cdmat[:, :],
                          din_indhp[0:32, q * 512:(q + 1) * 512],
                          start=True, stop=True)
                    em.tt(S[:, q * 512:(q + 1) * 512],
                          S[:, q * 512:(q + 1) * 512], pA[:, b:b + 512],
                          ALU.mult)
                    b2 = nextq()
                    em.mm(pA[:, b2:b2 + 512], btm_sb[:, :],
                          xt2_sb[:, q * 512:(q + 1) * 512],
                          start=True, stop=True)
                    em.tt(S[:, q * 512:(q + 1) * 512], pA[:, b2:b2 + 512],
                          S[:, q * 512:(q + 1) * 512], ALU.add)

            return [H1, H2, M1, T1, T2]

        conv_order = [0, 7, 1, 6, 2, 5, 3, 4]
        conv_group(conv_order[0], 0)
        conv_group(conv_order[1], 1)
        convs_done = 2
        units = []
        for ci in range(NCH):
            units.append((ci, True))
            units.append((NCH - 1 - ci, False))
        stv = [ssd_stages(c, f) for (c, f) in units]
        stv[0][0]()
        prev = None
        for i in range(len(stv)):
            if i in (4, 12, 20) and convs_done < 8:
                conv_group(conv_order[convs_done], convs_done % 2)
                conv_group(conv_order[convs_done + 1],
                           (convs_done + 1) % 2)
                convs_done += 2
            if prev is not None:
                stv[prev][3]()
                stv[prev][4]()
            stv[i][1]()
            stv[i][2]()
            if i + 1 < len(stv):
                stv[i + 1][0]()
            prev = i
        stv[prev][3]()
        stv[prev][4]()

        cdt2 = SB(stk, "cdt2", [32, 2])
        em.memset(convacc[:, :], 0.0)
        em.dma("sp", ccin[259:264, :].rearrange("r (a c) -> (r a) c", a=4),
               convacc[0:20, 0:512], tag="ccz1")
        em.dma("sp", ccin[256:257, 384:2048].rearrange(
                   "r (a c) -> (r a) c", a=4), convacc[20:24, 0:416],
               tag="ccz2")
        em.act(cdt2[:, :], offc[:, :], AF.Exp)
        em.dma("sp", ccin[0:128, :], S_f[:, :], tag="ccsf")
        em.dma("sp", ccin[128:256, :], S_b[:, :], tag="ccsb")
        em.dma("sp", ccin[256:257, 0:32].rearrange("a (b c) -> (a b) c", b=32),
               cdt2[:, 0:1], tag="cccdf", slow=True)
        em.dma("sp", ccin[256:257, 32:64].rearrange("a (b c) -> (a b) c", b=32),
               cdt2[:, 1:2], tag="cccdb", slow=True)
        em.dma("pool", ccin[256:257, 64:96], eact[127:128, 31 * 64:31 * 64 + 32],
               tag="cccf")
        em.dma("pool", ccin[256:257, 96:128], eact[0:1, 32:64], tag="cccb")
        em.dma("pool", ccin[256:257, 128:256].rearrange(
                   "a (b c) -> (a b) c", b=128),
               xogfm[17 * 128:18 * 128, 4095:4096], tag="ccCf", slow=True)
        em.dma("pool", ccin[256:257, 256:384].rearrange(
                   "a (b c) -> (a b) c", b=128),
               xogfm[19 * 128:20 * 128, 0:1], tag="ccCb", slow=True)
        em.dma("pool", ccin[257:258, :], ylf[T:T + 1, :], tag="ccyf")
        em.dma("pool", ccin[258:259, :], ylb[1:2, :], tag="ccyb")
        if DBG:
            em.dma("sp", outs["dbg_dt"], dtfmbuf[:, :], tag="dbgdt")
            em.dma("sp", outs["dbg_sf"], S_f[:, :], tag="dbgsf")
            em.dma("sp", outs["dbg_sb"], S_b[:, :], tag="dbgsb")
            em.dma("sp", outs["dbg_fd"], fd_fm[:, :], tag="dbgfd")
            em.dma("sp", outs["dbg_eact"], eact[:, :], tag="dbgea")


    def p3x(stk, eact):
        onesf = SB(stk, "onesf", [1, 128])
        sgt = SB(stk, "sgt", [128, DI])
        kf = [SB(stk, f"kf{j}", [128, DI]) for j in (1, 2, 3)]
        kb = [SB(stk, f"kb{j}", [128, DI]) for j in (0, 1, 2)]
        carryb16 = {n: SB(stk, "cb16_" + n, [128, DI], BF16)
                    for n in ("f", "fp", "b", "bn")}
        cdflat2 = SB(stk, "cdflat2", [1, 32])
        cdbc2 = SB(stk, "cdbc2", [128, 32])
        selo = SB(stk, "selo", [128, 4])
        selp = SB(stk, "selp", [128, 4])
        seln = SB(stk, "seln", [128, 4])
        ytile = SB(stk, "ytile", [1, 4 * DI], BF16)
        yacc = SB(stk, "yacc", [1, DI])
        cmat = SB(stk, "cmat", [128, 4])
        cw = SB(stk, "cw", [128, 4])
        csel = SB(stk, "csel", [128, 1], BF16)
        cself = SB(stk, "cself", [128, 1])
        cfl = SB(stk, "cfl", [1, 4 * 32])
        cflacc = SB(stk, "cflacc", [1, 32])
        padrow = SB(stk, "padrow", [1, DI])
        ylt4 = [[SB(stk, f"ylt{x}{d}", [128, DI], BF16) for d in range(2)]
                for x in range(2)]
        yout4 = [[SB(stk, f"yout{x}{d}", [128, DI], BF16) for d in range(2)]
                 for x in range(2)]
        cchk2 = [SB(stk, f"cchk{d}", [128, 128], BF16) for d in range(2)]
        gsca2 = [SB(stk, f"gsca{d}", [128, 1024], BF16) for d in range(2)]
        coefT2 = [SB(stk, f"coefT{d}", [128, 32]) for d in range(2)]
        wzr = SB(stk, "wzr", [128, 8 * DI], BF16)
        zsl = [SB(stk, f"zsl{x}", [128, DI], BF16) for x in range(2)]
        xzt2 = [SB(stk, f"xzt{x}", [128, 8 * 128], BF16) for x in range(2)]
        pA = PS(stk, "pA3", [128, 2048])
        pB = PS(stk, "pB3", [128, 512])
        pAz = PS(stk, "pAz", [128, 1024])
        em.memset(onesf[:, :], 1.0)
        em.dma("sp", selo[:, :], din["selobc"])
        em.dma("sp", selp[:, :], din["selpbc"])
        em.dma("sp", seln[:, :], din["selnbc"])
        em.collective("AllGather", ALU.bypass, [[0, 1, 2, 3], [4, 5, 6, 7]],
                      [ccin], [ccout], tag="ag")
        for k in range(8):
            em.dma("pool", wzr[:, k * DI:(k + 1) * DI],
                   din["ipwT"][k * 128:(k + 1) * 128, 0:DI], tag="wz")

        def z_sub(i):
            p = i % 2
            em.dma("sp", xzt2[p][:, :].rearrange("p (k c) -> p k c", k=8),
                   xln2fm[:, 4 + i * 128:4 + (i + 1) * 128].rearrange(
                       "(k p) c -> p k c", p=128), tag=f"zx{i}")
            for q in range(4):
                ph = pAz[:, (q % 2) * 512:(q % 2) * 512 + 512]
                for k in range(8):
                    em.mm(ph, xzt2[p][:, k * 128:(k + 1) * 128],
                          wzr[:, k * DI + q * 512:k * DI + (q + 1) * 512],
                          start=(k == 0), stop=(k == 7))
                em.act(zsl[p][:, q * 512:(q + 1) * 512], ph, AF.Silu)
            em.dma("act", zbuf[i * 128:(i + 1) * 128, :], zsl[p][:, :],
                   tag=f"zst{i}")

        for zi in range(24):
            z_sub(zi)

        def cdrep_load(j, col0):
            em.dma("sp", cdflat2[:, :], ccout[j * 264 + 256:j * 264 + 257,
                                              col0:col0 + 32], tag="cdl")
            em.mm(pB[:, 0:32], onesf[:, :], cdflat2[:, :], start=True,
                  stop=True)
            em.copy(cdbc2[:, :], pB[:, 0:32])

        def chain_step(dst, src_k, sg_row, j, col0):
            em.dma("sp", sgt[:, :], ccout[sg_row:sg_row + 128, :], tag="sgl")
            cdrep_load(j, col0)
            for h in range(32):
                em.ts(dst[:, h * 64:(h + 1) * 64],
                      src_k[:, h * 64:(h + 1) * 64],
                      cdbc2[:, h:h + 1], None, ALU.mult)
            em.tt(dst[:, :], dst[:, :], sgt[:, :], ALU.add)

        em.dma("sp", kf[0][:, :], ccout[0 * 264:0 * 264 + 128, :], tag="sg0")
        chain_step(kf[1], kf[0], 1 * 264, 1, 0)
        chain_step(kf[2], kf[1], 2 * 264, 2, 0)
        em.dma("sp", kb[2][:, :], ccout[3 * 264 + 128:3 * 264 + 256, :],
               tag="sg3")
        chain_step(kb[1], kb[2], 2 * 264 + 128, 2, 32)
        chain_step(kb[0], kb[1], 1 * 264 + 128, 1, 32)

        def select3(dst, ks, selcol, idxs):
            em.ts(dst[:, :], ks[0][:, :], selcol[:, idxs[0]:idxs[0] + 1],
                  None, ALU.mult)
            for k, j in zip(ks[1:], idxs[1:]):
                em.stt(dst[:, :], k[:, :], selcol[:, j:j + 1], dst[:, :],
                       ALU.mult, ALU.add)

        select3(carryb16["f"], kf, selo, [1, 2, 3])
        select3(carryb16["fp"], kf, selp, [1, 2, 3])
        select3(carryb16["b"], kb, selo, [0, 1, 2])
        select3(carryb16["bn"], kb, seln, [0, 1, 2])

        def pad_assemble(ybufrow, yrow_base, cf_col0, cc_col0, selt, cname,
                         tag):
            for j in range(4):
                em.dma("pool", ytile[:, j * DI:(j + 1) * DI],
                       ccout[j * 264 + yrow_base:j * 264 + yrow_base + 1, :],
                       tag=tag + "y")
                em.dma("sp", cfl[:, j * 32:(j + 1) * 32],
                       ccout[j * 264 + 256:j * 264 + 257,
                             cf_col0:cf_col0 + 32], tag=tag + "c")
                em.dma("sp", cmat[:, j:j + 1],
                       ccout[j * 264 + 256:j * 264 + 257,
                             cc_col0:cc_col0 + 128].rearrange(
                                 "a (b c) -> (a b) c", b=128), tag=tag + "C",
                       slow=True)
            em.ts(yacc[:, :], ytile[:, 0:DI], selt[0:1, 0:1], None, ALU.mult)
            for j in range(1, 4):
                em.stt(yacc[:, :], ytile[:, j * DI:(j + 1) * DI],
                       selt[0:1, j:j + 1], yacc[:, :], ALU.mult, ALU.add)
            em.ts(cflacc[:, :], cfl[:, 0:32], selt[0:1, 0:1], None, ALU.mult)
            for j in range(1, 4):
                em.stt(cflacc[:, :], cfl[:, j * 32:(j + 1) * 32],
                       selt[0:1, j:j + 1], cflacc[:, :], ALU.mult, ALU.add)
            em.tt(cw[:, :], cmat[:, :], selt[:, :], ALU.mult)
            em.reduce(cself[:, :], cw[:, :], ALU.add)
            em.copy(csel[:, :], cself[:, :])
            for j in range(4):
                em.mm(pA[0:1, j * 512:(j + 1) * 512], csel[:, :],
                      carryb16[cname][:, j * 512:(j + 1) * 512],
                      start=True, stop=True)
            for h in range(32):
                em.stt(padrow[:, h * 64:(h + 1) * 64],
                       pA[0:1, h * 64:(h + 1) * 64],
                       cflacc[0:1, h:h + 1], yacc[:, h * 64:(h + 1) * 64],
                       ALU.mult, ALU.add)
            em.dma("pool", ybufrow, padrow[:, :], tag=tag + "st")

        pad_assemble(ylf[0:1, :], 257, 64, 128, selp, "fp", "padf")
        pad_assemble(ylb[T + 1:T + 2, :], 258, 96, 256, seln, "bn", "padb")

        def p4a_stage(i, fwd, st):
            d = 0 if fwd else 1
            ybuf = ylf if fwd else ylb
            fcC = 17 if fwd else 19
            cn = "f" if fwd else "b"
            dircol = 0 if fwd else 32
            pAd = pA[:, d * 1024:(d + 1) * 1024]
            if st == 0:
                em.dma("sp", cchk2[d][:, :],
                       xogfm[fcC * 128:(fcC + 1) * 128,
                             i * 128:(i + 1) * 128], tag=f"c4{i}{d}")
                em.dma("sp", ylt4[i % 2][d][:, :],
                       ybuf[i * 128 + 1:i * 128 + 129, :], tag=f"yl4{i}{d}")
                em.copy(coefT2[d][:, :], eact[:, i * 64 + dircol:
                                              i * 64 + dircol + 32])
            else:
                for w in range(2):
                    for j in range(2):
                        em.mm(pAd[:, j * 512:(j + 1) * 512], cchk2[d][:, :],
                              carryb16[cn][:, w * 1024 + j * 512:
                                           w * 1024 + (j + 1) * 512],
                              start=True, stop=True)
                    if w == 0:
                        for h in range(16):
                            em.stt(yout4[i % 2][d][:, h * 64:(h + 1) * 64],
                                   pAd[:, h * 64:(h + 1) * 64],
                                   coefT2[d][:, h:h + 1],
                                   ylt4[i % 2][d][:, h * 64:(h + 1) * 64],
                                   ALU.mult, ALU.add)
                    else:
                        for h in range(8):
                            hh = 16 + h
                            em.act(gsca2[d][:, h * 64:(h + 1) * 64],
                                   pAd[:, h * 64:(h + 1) * 64], AF.Copy,
                                   scale=coefT2[d][:, hh:hh + 1])
                        em.tt(yout4[i % 2][d][:, 1024:1536],
                              ylt4[i % 2][d][:, 1024:1536],
                              gsca2[d][:, 0:512], ALU.add)
                        for h in range(8, 16):
                            hh = 16 + h
                            em.stt(yout4[i % 2][d][:, hh * 64:(hh + 1) * 64],
                                   pAd[:, h * 64:(h + 1) * 64],
                                   coefT2[d][:, hh:hh + 1],
                                   ylt4[i % 2][d][:, hh * 64:(hh + 1) * 64],
                                   ALU.mult, ALU.add)
                em.dma("act", ybuf[i * 128 + 1:i * 128 + 129, :],
                       yout4[i % 2][d][:, :], tag=f"yw4{i}{d}")
        p4a_stage(0, True, 0)
        p4a_stage(0, False, 0)
        for i in range(NCH):
            p4a_stage(i, True, 1)
            if i + 1 < NCH:
                p4a_stage(i + 1, True, 0)
            p4a_stage(i, False, 1)
            if i + 1 < NCH:
                p4a_stage(i + 1, False, 0)
            if 24 + i < NCH:
                z_sub(24 + i)

    def p4m(stk, fd_fm):
        opwb = SB(stk, "opwb", [128, 16 * D], BF16)
        nwc = SB(stk, "nwc", [128, 16])
        drept = SB(stk, "drept", [128, DI], BF16)
        d32 = SB(stk, "d32", [128, 32])
        m2 = SB(stk, "m2", [128, 2])
        x1t = SB(stk, "x1t4", [128, NG * D])
        yft = SB(stk, "yft", [128, NG * DI], BF16)
        ybt = SB(stk, "ybt", [128, NG * DI], BF16)
        xm1 = SB(stk, "xm1", [128, NG * DI], BF16)
        xct = SB(stk, "xct", [128, NG * DI], BF16)
        xp1 = SB(stk, "xp1", [128, NG * DI], BF16)
        zst = SB(stk, "zst", [128, NG * DI], BF16)
        indhp4 = SB(stk, "indhp4", [32, DI], BF16)
        drow = SB(stk, "drow", [1, DI], BF16)
        ones1 = SB(stk, "ones1", [1, 128], BF16)
        fdd2 = [SB(stk, f"fdd{g}", [128, DI], BF16)
                for g in range(NG)]
        yga = SB(stk, "yga", [128, NG * DI], BF16)
        ygb2 = [SB(stk, f"ygb{g}", [128, DI], BF16)
                for g in range(NG)]
        ygbf = SB(stk, "ygbf", [128, NG * DI], BF16)
        sqc = SB(stk, "sqc4", [128, 8])
        st2 = SB(stk, "st24", [128, 8])
        st3 = SB(stk, "st34", [128, 8])
        rstd = SB(stk, "rstd4", [128, 8])
        sq_scr = SB(stk, "sq_scr4", [128, DI])
        ynfm = SB(stk, "ynfm", [128, 16 * 256], BF16)
        mixt = SB(stk, "mixt", [128, NG * D])
        x2t = SB(stk, "x2t", [128, NG * D])
        ptT = PS(stk, "ptT4", [128, 2048], BF16)
        pob = [PS(stk, f"po{i}_4", [128, 512]) for i in range(2)]
        pm4 = PS(stk, "pm4", [128, 2048])

        for k in range(16):
            em.dma("pool", opwb[:, k * D:(k + 1) * D],
                   din["opwT"][k * 128:(k + 1) * 128, :], tag="opw")
        em.dma("sp", nwc[:, :], din["normw"].rearrange("(c p) -> p c", p=128),
               slow=True)
        em.dma("pool", drept[:, :], din["drep"])
        em.dma("pool", drow[:, :], din["drep"][0:1, :])
        em.memset(ones1[:, :], 1.0)
        em.dma("sp", indhp4[:, :], din["indhp"])
        em.dma("sp", m2[:, :], din["mask2"])
        for k in range(16):
            em.ts(opwb[:, k * D:(k + 1) * D], opwb[:, k * D:(k + 1) * D],
                  nwc[:, k:k + 1], None, ALU.mult)

        def rms_pair(src, xout):
            for g in range(NG):
                gs = src[:, g * DI:(g + 1) * DI]
                em.act(sq_scr[:, :], gs, AF.Square, accum_out=sqc[:, g:g + 1])
            sl = slice(0, NG)
            em.ts(st2[:, sl], sqc[:, sl], 1.0 / DI, EPS, ALU.mult, ALU.add)
            em.rsqrt(rstd[:, sl], st2[:, sl], st3[:, sl])
            for g in range(NG):
                em.ts(xout[:, g * DI:(g + 1) * DI],
                      src[:, g * DI:(g + 1) * DI], rstd[:, g:g + 1], None,
                      ALU.mult)

        for pr in range(16):
            i0 = pr * 2
            em.dma("sp", x1t[:, :].rearrange("p (g d) -> p g d", g=NG),
                   x1buf[4 + i0 * 128:4 + (i0 + 2) * 128, :].rearrange(
                       "(g p) d -> p g d", p=128), tag=f"x1l{pr}")
            em.dma("sp", yft[:, :].rearrange("p (g d) -> p g d", g=NG),
                   ylf[i0 * 128:(i0 + 2) * 128, :].rearrange(
                       "(g p) d -> p g d", p=128), tag=f"yfl{pr}")
            em.dma("sp", ybt[:, :].rearrange("p (g d) -> p g d", g=NG),
                   ylb[i0 * 128 + 2:(i0 + 2) * 128 + 2, :].rearrange(
                       "(g p) d -> p g d", p=128), tag=f"ybl{pr}")
            em.dma("sp", xm1[:, :].rearrange("p (g d) -> p g d", g=NG),
                   xogbuf[i0 * 128:(i0 + 2) * 128, :].rearrange(
                       "(g p) d -> p g d", p=128), tag=f"xml{pr}")
            em.dma("sp", xct[:, :].rearrange("p (g d) -> p g d", g=NG),
                   xogbuf[i0 * 128 + 1:(i0 + 2) * 128 + 1, :].rearrange(
                       "(g p) d -> p g d", p=128), tag=f"xcl{pr}")
            em.dma("sp", xp1[:, :].rearrange("p (g d) -> p g d", g=NG),
                   xogbuf[i0 * 128 + 2:(i0 + 2) * 128 + 2, :].rearrange(
                       "(g p) d -> p g d", p=128), tag=f"xpl{pr}")
            em.dma("sp", zst[:, :].rearrange("p (g d) -> p g d", g=NG),
                   zbuf[i0 * 128:(i0 + 2) * 128, :].rearrange(
                       "(g p) d -> p g d", p=128), tag=f"zl{pr}")
            if pr == 0:
                em.ts(xm1[:, 0:DI], xm1[:, 0:DI], m2[:, 0:1], None, ALU.mult)
            if pr == 15:
                em.ts(xp1[:, DI:2 * DI], xp1[:, DI:2 * DI], m2[:, 1:2], None,
                      ALU.mult)
            for g in range(NG):
                i = i0 + g
                ys = slice(g * DI, (g + 1) * DI)
                ygb = ygb2[g]
                fdd = fdd2[g]
                for w in range(2):
                    ph = pm4[:, (g % 2) * 1024:(g % 2) * 1024 + 1024]
                    for j in range(2):
                        jj = w * 2 + j
                        em.mm(ph[:, j * 512:(j + 1) * 512],
                              fd_fm[:, i * 128:(i + 1) * 128],
                              indhp4[0:32, jj * 512:(jj + 1) * 512],
                              start=True, stop=False)
                        em.mm(ph[:, j * 512:(j + 1) * 512], ones1[:, :],
                              drow[:, jj * 512:(jj + 1) * 512], start=False,
                              stop=True)
                    em.act(fdd[:, w * 1024:(w + 1) * 1024], ph[:, :],
                           AF.Copy)
                em.tt(yga[:, ys], yft[:, ys], ybt[:, ys], ALU.add)
                em.tt(ygb[:, :], xm1[:, ys], xp1[:, ys], ALU.add)
                em.tt(ygb[:, :], ygb[:, :], drept[:, :], ALU.mult)
                em.tt(yga[:, ys], yga[:, ys], ygb[:, :], ALU.add)
                em.tt(ygb[:, :], fdd[:, :], xct[:, ys], ALU.mult)
                em.tt(yga[:, ys], yga[:, ys], ygb[:, :], ALU.add)
                em.tt(yga[:, ys], yga[:, ys], zst[:, ys], ALU.mult)
            rms_pair(yga, ygbf)
            for g in range(NG):
                for k in range(16):
                    em.transpose(ptT[:, k * 128:(k + 1) * 128],
                                 ygbf[:, g * DI + k * 128:
                                      g * DI + (k + 1) * 128], identb[:, :])
                em.copy(ynfm[:, :].rearrange("p (k r) -> p k r", k=16)
                        [:, :, g * 128:(g + 1) * 128],
                        ptT[:, :].rearrange("p (k r) -> p k r", k=16))
            for do in range(8):
                bank = pob[do % 2]
                for k in range(16):
                    em.mm(bank[:, 0:256],
                          opwb[:, k * D + do * 128:k * D + (do + 1) * 128],
                          ynfm[:, k * 256:(k + 1) * 256],
                          start=(k == 0), stop=(k == 15))
                em.copy(mixt[:, :].rearrange("p (g e) -> p g e", g=NG)
                        [:, :, do * 128:(do + 1) * 128],
                        bank[:, 0:256].rearrange("p (g c) -> p g c", g=NG))
            for g in range(NG):
                for w in range(2):
                    for dd in range(4):
                        em.transpose(
                            pob[w % 2][:, dd * 128:(dd + 1) * 128],
                            mixt[:, g * D + (w * 4 + dd) * 128:
                                 g * D + (w * 4 + dd + 1) * 128],
                            identf[:, :])
                    for dd in range(4):
                        em.copy(x2t[:, g * D + (w * 4 + dd) * 128:
                                    g * D + (w * 4 + dd + 1) * 128],
                                pob[w % 2][:, dd * 128:(dd + 1) * 128])
            for g in range(NG):
                em.tt(x2t[:, g * D:(g + 1) * D], x2t[:, g * D:(g + 1) * D],
                      x1t[:, g * D:(g + 1) * D], ALU.add)
            em.dma("act",
                   x2buf[i0 * 128:(i0 + 2) * 128, :].rearrange(
                       "(g p) d -> p g d", p=128),
                   x2t[:, :].rearrange("p (g d) -> p g d", g=NG),
                   tag=f"x2st{pr}")

    def p4f(stk):
        w1b = SB(stk, "w1b4", [128, 8 * DFF], BF16)
        w2b = SB(stk, "w2b4", [128, 32 * D], BF16)
        b1c = SB(stk, "b1c4", [128, 32])
        b2c = SB(stk, "b2c4", [128, 8])
        thetac = SB(stk, "thetac4", [128, 32])
        thetab = SB(stk, "thetab4", [128, 32])
        coefc = SB(stk, "coefc4", [128, 32])
        ut = SB(stk, "ut4", [128, NG * D])
        sumc = SB(stk, "sumc4", [128, 8])
        sqc = SB(stk, "sqc4f", [128, 8])
        st1 = SB(stk, "st14", [128, 8])
        st2 = SB(stk, "st24f", [128, 8])
        st3 = SB(stk, "st34f", [128, 8])
        rstd = SB(stk, "rstd4f", [128, 8])
        nmr = SB(stk, "nmr4", [128, 8])
        sq_scr = SB(stk, "sq_scr4f", [128, D])
        xlnt = SB(stk, "xlnt4", [128, NG * D], BF16)
        xfm = SB(stk, "xfm4", [128, 8 * TILE], BF16)
        aa = SB(stk, "aa4", [128, TILE], BF16)
        kfl = SB(stk, "kfl4", [128, TILE], BF16)
        sargt = SB(stk, "sargt4", [128, TILE], BF16)
        stile = SB(stk, "stile4", [128, TILE])
        qt = SB(stk, "qt4", [128, TILE], BF16)
        hact = SB(stk, "hact4", [128, 32 * TILE], BF16)
        o2t = [SB(stk, f"o2t{i}_4", [128, TILE], BF16) for i in range(2)]
        outt = SB(stk, "outt", [128, NG * D])
        phs = SB(stk, "phs4", [128, 32])
        wrs = SB(stk, "wrs4", [128, 32])
        wis = SB(stk, "wis4", [128, 32])
        t32a = SB(stk, "t32a4", [128, 32])
        t32b = SB(stk, "t32b4", [128, 32])
        ptT = PS(stk, "ptT4f", [128, 2048], BF16)
        pfb = [PS(stk, f"pf{i}_4", [128, 512]) for i in range(3)]
        pob = [PS(stk, f"po{i}_4f", [128, 512]) for i in range(2)]

        for k in range(8):
            em.dma("pool", w1b[:, k * DFF:(k + 1) * DFF],
                   din["w1"][k * 128:(k + 1) * 128, :], tag="w1r")
        for k in range(8):
            em.dma("pool",
                   w2b[:, k * 4 * D:(k + 1) * 4 * D].rearrange(
                       "p (c n) -> p c n", c=4),
                   din["w2"].rearrange("(c p) n -> p c n",
                                       p=128)[:, 4 * k:4 * k + 4, :],
                   tag="w2r")
        em.dma("sp", b1c[:, :], din["b1"].rearrange("(c p) -> p c", p=128),
               slow=True)
        em.dma("sp", b2c[:, :], din["b2"].rearrange("(c p) -> p c", p=128),
               slow=True)
        em.dma("sp", phs[:, :], din["phases"].rearrange("(c p) -> p c", p=128),
               slow=True)
        em.dma("sp", wrs[:, :], din["wreal"].rearrange("(c p) -> p c", p=128),
               slow=True)
        em.dma("sp", wis[:, :], din["wimag"].rearrange("(c p) -> p c", p=128),
               slow=True)
        em.copy(thetac[:, :], phs[:, :])
        for _ in range(6):
            em.wrap1(thetac[:, :], thetac[:, :], t32a[:, :], t32b[:, :])
        em.act(t32a[:, :], thetac[:, :], AF.Sin)
        em.act(t32b[:, :], thetac[:, :], AF.Sin, scale=0.5)
        em.tt(t32b[:, :], t32b[:, :], t32b[:, :], ALU.mult)
        em.ts(t32b[:, :], t32b[:, :], -2.0, 1.0, ALU.mult, ALU.add)
        em.tt(t32a[:, :], t32a[:, :], wis[:, :], ALU.mult)
        em.tt(t32b[:, :], t32b[:, :], wrs[:, :], ALU.mult)
        em.tt(t32a[:, :], t32a[:, :], t32b[:, :], ALU.add)
        em.tt(t32a[:, :], t32a[:, :], wis[:, :], ALU.mult)
        em.ts(coefc[:, :], t32a[:, :], 0.1, None, ALU.mult)
        em.stt(thetab[:, :], b1c[:, :], 0.1, thetac[:, :], ALU.mult, ALU.add)

        for t in range(16):
            R0 = t * TILE
            em.dma("sp", ut[:, :].rearrange("p (g d) -> p g d", g=NG),
                   x2buf[R0:R0 + TILE, :].rearrange("(g p) d -> p g d",
                                                    p=128), tag=f"p4u{t}")
            for g in range(NG):
                gs = ut[:, g * D:(g + 1) * D]
                em.act(sq_scr[:, :], gs, AF.Square,
                       accum_out=sqc[:, g:g + 1])
                em.reduce(sumc[:, g:g + 1], gs, ALU.add)
            sl = slice(0, NG)
            em.ts(st1[:, sl], sumc[:, sl], 1.0 / D, None, ALU.mult)
            em.ts(st2[:, sl], sqc[:, sl], 1.0 / D, None, ALU.mult)
            em.tt(st3[:, sl], st1[:, sl], st1[:, sl], ALU.mult)
            em.tt(st2[:, sl], st2[:, sl], st3[:, sl], ALU.subtract)
            em.ts(st2[:, sl], st2[:, sl], EPS, None, ALU.add)
            em.rsqrt(rstd[:, sl], st2[:, sl], st3[:, sl])
            em.tt(nmr[:, sl], st1[:, sl], rstd[:, sl], ALU.mult)
            em.ts(nmr[:, sl], nmr[:, sl], -1.0, None, ALU.mult)
            for g in range(NG):
                em.ts(xlnt[:, g * D:(g + 1) * D], ut[:, g * D:(g + 1) * D],
                      rstd[:, g:g + 1], nmr[:, g:g + 1], ALU.mult, ALU.add)
            for k in range(8):
                for g in range(NG):
                    em.transpose(
                        ptT[:, (k * NG + g) * 128:(k * NG + g + 1) * 128],
                        xlnt[:, g * D + k * 128: g * D + (k + 1) * 128],
                        identb[:, :])
            dap = xfm[:, :]
            sap = ptT[:, :]
            em.copy(bass.AP(dap.tensor, 0, [list(dap.ap[0]), [TILE, 8],
                                            [128, NG], [1, 128]]),
                    bass.AP(sap.tensor, 0, [list(sap.ap[0]), [NG * 128, 8],
                                            [128, NG], [1, 128]]))
            for f in range(32):
                bank = pfb[f % 3]
                for k in range(8):
                    em.mm(bank[:, 0:TILE],
                          w1b[:, k * DFF + f * 128: k * DFF + (f + 1) * 128],
                          xfm[:, k * TILE:(k + 1) * TILE],
                          start=(k == 0), stop=(k == 7))
                em.ts(aa[:, :], bank[:, 0:TILE], 0.1, thetab[:, f:f + 1],
                      ALU.mult, ALU.add)
                em.wrap1(sargt[:, :], aa[:, :], kfl[:, :], qt[:, :])
                em.act(stile[:, :], sargt[:, :], AF.Sin)
                em.stt(qt[:, :], stile[:, :], coefc[:, f:f + 1],
                       bank[:, 0:TILE], ALU.mult, ALU.add)
                em.act(hact[:, f * TILE:(f + 1) * TILE], qt[:, :], AF.Relu,
                       bias=b1c[:, f:f + 1])
            for do in range(8):
                bank = pob[do % 2]
                for kf2 in range(32):
                    em.mm(bank[:, 0:TILE],
                          w2b[:, kf2 * D + do * 128: kf2 * D + (do + 1) * 128],
                          hact[:, kf2 * TILE:(kf2 + 1) * TILE],
                          start=(kf2 == 0), stop=(kf2 == 31))
                em.ts(o2t[do % 2][:, :], bank[:, 0:TILE], b2c[:, do:do + 1],
                      None, ALU.add)
                for g in range(NG):
                    em.transpose(ptT[:, (do % 2) * 1024 + g * 128:
                                     (do % 2) * 1024 + (g + 1) * 128],
                                 o2t[do % 2][:, g * 128:(g + 1) * 128],
                                 identb[:, :])
                for g in range(NG):
                    em.stt(outt[:, g * D + do * 128: g * D + (do + 1) * 128],
                           ptT[:, (do % 2) * 1024 + g * 128:
                               (do % 2) * 1024 + g * 128 + 128],
                           0.5,
                           ut[:, g * D + do * 128: g * D + (do + 1) * 128],
                           ALU.mult, ALU.add)
            em.dma("act",
                   out[R0:R0 + TILE, :].rearrange("(g p) d -> p g d", p=128),
                   outt[:, :].rearrange("p (g d) -> p g d", g=NG),
                   tag=f"outst{t}")

    midstack = contextlib.ExitStack()
    fd_fm = SB(midstack, "fd_fm", [32, T], BF16)
    eact = SB(midstack, "eact", [128, NCH * 64], BF16)
    if "P1" in phases:
        with contextlib.ExitStack() as stk:
            p1(stk)
            em.barrier()
    if "P2" in phases:
        with contextlib.ExitStack() as stk:
            p2(stk, fd_fm, eact)
            em.barrier()
    if "P4" in phases:
        with contextlib.ExitStack() as stk:
            p3x(stk, eact)
            em.barrier()
        with contextlib.ExitStack() as stk:
            p4m(stk, fd_fm)
            em.barrier()
        with contextlib.ExitStack() as stk:
            p4f(stk)
            em.barrier()
    midstack.close()

    if DBG:
        em.dma("sp", outs["dbg_x1"], x1buf, tag="dbgx1")
        em.dma("sp", outs["dbg_xln2fm"], xln2fm, tag="dbgxln")
        if "P2" in phases:
            em.dma("sp", outs["dbg_xog"], xogbuf, tag="dbgxog")
            em.dma("sp", outs["dbg_ylf"], ylf[1:T + 1, :], tag="dbgylf")
            em.dma("sp", outs["dbg_ylb"], ylb[1:T + 1, :], tag="dbgylb")
    if "P4" not in phases:
        em.dma("sp", out, x1buf[4:4 + T, :], tag="outplaceholder")

    with contextlib.ExitStack() as semstack:
        sems = {}
        for n in ("pe", "act", "dve", "pool", "sp"):
            sems[n] = semstack.enter_context(nc.semaphore("s_" + n))
        for n in em.dma_sem_names:
            sems[n] = semstack.enter_context(nc.semaphore("s_" + n))
        block = semstack.enter_context(nc.Block())
        em.emit(block, sems)
    topstack.close()
    return nc



BF = ml_dtypes.bfloat16
L = 16384


def aux_constants():
    c = {}
    c["identb"] = np.eye(128, dtype=BF)
    c["identf"] = np.eye(128, dtype=np.float32)
    tri = np.zeros((128, 128), np.float32)
    for s in range(128):
        tri[s, s:] = 1.0
    c["triu"] = tri
    c["tril"] = np.ascontiguousarray(tri.T)
    c["maskF"] = tri.astype(BF)
    c["maskB"] = np.ascontiguousarray(tri.T).astype(BF)
    indh = np.zeros((H, DFF), np.float32)
    for h in range(H):
        indh[h, h * 128:(h + 1) * 128] = 1.0
    c["indh"] = indh.astype(BF)
    indhp = np.zeros((H, DI), np.float32)
    for h in range(H):
        indhp[h, h * 64:(h + 1) * 64] = 1.0
    c["indhp"] = indhp.astype(BF)
    c["zer32"] = np.zeros((32, 128), BF)
    c["neg3"] = np.full((3, 128), -1.0, BF)
    return c


def prep_inputs(inp):
    u = np.asarray(inp["L0_attn_u"])
    ipw = np.asarray(inp["L0_attn_in_proj_w"])
    aux = aux_constants()
    ipwT = np.ascontiguousarray(ipw.T)
    opwT = np.ascontiguousarray(np.asarray(inp["L0_attn_out_proj_w"]).T)
    fcdT = np.ascontiguousarray(np.asarray(inp["L0_attn_fc_D_w"]).T)
    convw = np.ascontiguousarray(
        np.asarray(inp["L0_attn_conv_w"]).reshape(CONVDIM, 4))
    dtb = np.asarray(inp["L0_attn_dt_bias"])
    dtbias2 = np.ascontiguousarray(np.concatenate([dtb, dtb]))
    drep = np.ascontiguousarray(np.broadcast_to(
        np.repeat(np.asarray(inp["L0_attn_D"]), 64)[None, :], (128, DI))
    ).astype(np.float32)

    d32bc = np.ascontiguousarray(np.broadcast_to(
        np.asarray(inp["L0_attn_D"], np.float32)[None, :], (128, H)))
    shared = dict(
        d32bc=d32bc,
        w1=np.asarray(inp["L0_ffn_W1"]), w2=np.asarray(inp["L0_ffn_W2"]),
        b1=np.asarray(inp["L0_ffn_b1"]), b2=np.asarray(inp["L0_ffn_b2"]),
        phases=np.asarray(inp["L0_ffn_quantum_phases"]),
        wreal=np.asarray(inp["L0_ffn_quantum_weights_real"]),
        wimag=np.asarray(inp["L0_ffn_quantum_weights_imag"]),
        ipwT=ipwT, opwT=opwT, fcdT=fcdT, convw=convw,
        convb=np.asarray(inp["L0_attn_conv_b"]),
        dtbias2=dtbias2, alog=np.asarray(inp["L0_attn_A_log"]),
        dskip=np.asarray(inp["L0_attn_D"]),
        normw=np.asarray(inp["L0_attn_norm_w"]),
        drep=drep,
        **aux,
    )
    in_maps = []
    for c in range(8):
        b, s = c // 4, c % 4
        a = s * T
        usl = np.zeros((USL, D), np.float32)
        lo = a - 4
        lo2, hi2 = max(lo, 0), min(lo + USL, L)
        usl[lo2 - lo: hi2 - lo] = u[b, lo2:hi2]
        lnmask = np.ones(4352, np.float32)
        if s == 0:
            lnmask[0:4] = 0.0
        if s == 3:
            lnmask[4100:] = 0.0
        lnmask[4101:] = 0.0
        gown = np.zeros(8, np.float32)
        gprev = np.zeros(8, np.float32)
        for j in range(4):
            gown[2 * j] = 1.0 if j < s else 0.0
            gown[2 * j + 1] = 1.0 - gown[2 * j]
            gprev[2 * j] = 1.0 if j < s - 1 else 0.0
            gprev[2 * j + 1] = 1.0 - gprev[2 * j]
        selown = np.zeros(4, np.float32)
        selown[s] = 1.0
        selprev = np.zeros(4, np.float32)
        if s > 0:
            selprev[s - 1] = 1.0
        selnxt = np.zeros(4, np.float32)
        if s < 3:
            selnxt[s + 1] = 1.0
        mask2 = np.ones((128, 2), np.float32)
        if s == 0:
            mask2[0, 0] = 0.0
        if s == 3:
            mask2[127, 1] = 0.0
        m = dict(shared)
        m.update(u=usl, lnmask=lnmask, gown=gown, gprev=gprev,
                 selown=selown, selprev=selprev, selnxt=selnxt,
                 selobc=np.ascontiguousarray(
                     np.broadcast_to(selown, (128, 4))),
                 selpbc=np.ascontiguousarray(
                     np.broadcast_to(selprev, (128, 4))),
                 selnbc=np.ascontiguousarray(
                     np.broadcast_to(selnxt, (128, 4))),
                 mask2=mask2)
        in_maps.append(m)
    return in_maps


_CACHE = {}


def kernel(**inputs):
    if "nc" not in _CACHE:
        _CACHE["nc"] = build_program(phases=("P1", "P2", "P4"))
    nc = _CACHE["nc"]
    in_maps = prep_inputs(inputs)
    from concourse.bass_utils import run_bass_kernel_spmd
    res = run_bass_kernel_spmd(nc, in_maps, core_ids=list(range(8)))
    full = np.zeros((2, L, D), np.float32)
    for c in range(8):
        b, s = c // 4, c % 4
        full[b, s * T:(s + 1) * T] = res.results[c]["out"]
    return full
